# version 18
# baseline (speedup 1.0000x reference)
import numpy as np
import ml_dtypes
from contextlib import ExitStack

import concourse.bass as bass
import concourse.mybir as mybir
from concourse.bass_utils import run_bass_kernel_spmd

F32 = mybir.dt.float32
BF16 = mybir.dt.bfloat16
AF = mybir.ActivationFunctionType
ALU = mybir.AluOpType
ts = bass.ts

P = 128
D = 1024
DH = 4096
NH, DK = 16, 64
S = 4096
T = 512
NT = S // T
KC = D // P
HC = DH // P
N_CORES = 8
EPS = 1e-5
DEBUG_BARRIER = False
SLOT = 8192


def build_program():
    nc = bass.Bass("TRN2", target_bir_lowering=False)

    xT = nc.dram_tensor("xT", [D, S], F32, kind="ExternalInput").ap()
    wqkv = nc.dram_tensor("wqkv", [D, 3 * D], F32, kind="ExternalInput").ap()
    wout = nc.dram_tensor("wout", [D, D], F32, kind="ExternalInput").ap()
    w1T = nc.dram_tensor("w1T", [D, DH], F32, kind="ExternalInput").ap()
    w2T = nc.dram_tensor("w2T", [D, DH], F32, kind="ExternalInput").ap()
    h1T = nc.dram_tensor("h1T", [DH, DH], F32, kind="ExternalInput").ap()
    h2T = nc.dram_tensor("h2T", [DH, DH], F32, kind="ExternalInput").ap()
    w3T = nc.dram_tensor("w3T", [DH, D], F32, kind="ExternalInput").ap()
    cb16 = nc.dram_tensor("cb16", [P, 2688], BF16, kind="ExternalInput").ap()
    cb32 = nc.dram_tensor("cb32", [P, 169], F32, kind="ExternalInput").ap()

    outT = nc.dram_tensor("outT", [D, S], F32, kind="ExternalOutput").ap()

    wbqkv_d = nc.dram_tensor("wbqkv_d", [D, 3 * D], BF16, kind="Internal").ap()
    wbout_d = nc.dram_tensor("wbout_d", [D, D], BF16, kind="Internal").ap()
    wb1_d = nc.dram_tensor("wb1_d", [D, DH], BF16, kind="Internal").ap()
    wb2_d = nc.dram_tensor("wb2_d", [D, DH], BF16, kind="Internal").ap()
    hb1_d = nc.dram_tensor("hb1_d", [DH, DH], BF16, kind="Internal").ap()
    hb2_d = nc.dram_tensor("hb2_d", [DH, DH], BF16, kind="Internal").ap()
    wb3_d = nc.dram_tensor("wb3_d", [DH, D], BF16, kind="Internal").ap()

    with ExitStack() as g:
        E = g.enter_context
        cb16_s = E(nc.sbuf_tensor("cb16_s", [P, 2688], BF16))
        cb32_s = E(nc.sbuf_tensor("cb32_s", [P, 169], F32))
        dsel_s = cb16_s[:, 0:P]
        bdsel_s = cb16_s[:, P:P + KC * P]
        gsel_s = cb16_s[:, P + KC * P:P + 2 * KC * P]
        densel_s = cb16_s[:, 2176:2304]
        rsel_s = cb16_s[:, 2304:2432]
        brow_s = cb16_s[:, 2432:2560]
        col0_s = cb16_s[:, 2560:2688]
        eps_s = cb32_s[:, 0:1]
        bqkv_s = cb32_s[:, 1:25]
        bout_s = cb32_s[:, 25:33]
        s1_s = cb32_s[:, 33:65]
        s2_s = cb32_s[:, 65:97]
        sh1_s = cb32_s[:, 97:129]
        sh2_s = cb32_s[:, 129:161]
        s3_s = cb32_s[:, 161:169]

        wsl = [E(nc.sbuf_tensor(f"wsl{j}", [P, SLOT], BF16)) for j in range(2)]
        xbuf = E(nc.sbuf_tensor("xbuf", [P, KC * T], F32))
        xln_s = E(nc.sbuf_tensor("xln_s", [P, KC * T], BF16))
        x1ln_s = E(nc.sbuf_tensor("x1ln_s", [P, KC * T], BF16))
        q_s = E(nc.sbuf_tensor("q_s", [P, KC * T], BF16))
        k_s = E(nc.sbuf_tensor("k_s", [P, KC * T], BF16))
        v_s = E(nc.sbuf_tensor("v_s", [P, KC * T], BF16))
        qrep = E(nc.sbuf_tensor("qrep", [P, T], BF16))
        tmpa = E(nc.sbuf_tensor("tmpa", [P, KC * T], BF16))
        esb = [E(nc.sbuf_tensor(f"esb{j}", [P, T], BF16)) for j in range(2)]
        tmp2 = [E(nc.sbuf_tensor(f"tmp2_{j}", [P, T], BF16)) for j in range(2)]
        rband = E(nc.sbuf_tensor("rband", [33, T], F32))
        rbandb = E(nc.sbuf_tensor("rbandb", [P, 2 * T], BF16))
        scr = E(nc.sbuf_tensor("scr", [1, 2 * T], F32))
        rowb = E(nc.sbuf_tensor("rowb", [P, T], BF16))

        sA = [E(nc.sbuf_tensor(f"sA{j}", [P, T], BF16)) for j in range(2)]
        xbb = [E(nc.sbuf_tensor("xbb0", [P, T], BF16)), qrep]
        ctxb = E(nc.sbuf_tensor("ctxb", [P, KC * T], BF16))
        x1_s = E(nc.sbuf_tensor("x1_s", [P, KC * T], F32))
        hB = E(nc.sbuf_tensor("hB", [P, HC * T], BF16))
        g1B = E(nc.sbuf_tensor("g1B", [P, HC * T], BF16))

        pb = [E(nc.psum_tensor(f"pb{i}", [P, T], F32)) for i in range(8)]

        SW = E(nc.semaphore("SW"))
        SI = E(nc.semaphore("SI"))
        SO = E(nc.semaphore("SO"))
        STn = E(nc.semaphore("STn"))
        SV = E(nc.semaphore("SV"))
        SS = E(nc.semaphore("SS"))
        WSEM = {name: E(nc.semaphore(f"W_{name}")) for name in
                ("qkv", "wout", "w1", "w2", "h1", "h2", "w3")}
        WCNT = {name: 0 for name in WSEM}
        SL = [E(nc.semaphore(f"SL{j}")) for j in range(2)]
        SX = E(nc.semaphore("SX"))
        blk = E(nc.Block())
        C = dict(w=0, i=0, o=0, t=0, v=0, s=0)
        SEMS = {"t": STn, "v": SV, "s": SS, "i": SI, "o": SO, "w": SW, "x": None}

        def t_inc(instr):
            C["t"] += 1
            return instr.then_inc(STn, 1)

        def v_inc(instr):
            C["v"] += 1
            return instr.then_inc(SV, 1)

        def s_inc(instr):
            C["s"] += 1
            return instr.then_inc(SS, 1)

        def wt(e, tok):
            if tok is None:
                return
            if tok[1] > 0:
                sem = SX if tok[0] == "x" else SEMS[tok[0]]
                e.wait_ge(sem, tok[1])

        XCNT = [0]
        @blk.sync
        def _(e):
            e.dma_start(out=xbuf[:].rearrange("p (k t) -> p k t", k=KC),
                        in_=xT[:, 0:T].rearrange("(k p) t -> p k t", p=P)).then_inc(SX, 16)
            for dst, src in [(cb16_s[:], cb16), (cb32_s[:], cb32)]:
                e.dma_start(out=dst, in_=src).then_inc(SI, 16)
                C["i"] += 16
        XCNT[0] += 16
        consts_loaded = C["i"]

        @blk.tensor
        def _(e):
            e.wait_ge(SI, consts_loaded)
        @blk.scalar
        def _(e):
            e.wait_ge(SI, consts_loaded)

        @blk.vector
        def _(e):
            e.wait_ge(SI, consts_loaded)
            e.memset(esb[0][:], 0.0)
            e.memset(esb[1][:], 0.0)
            e.memset(rbandb[:], 0.0)
            v_inc(e.memset(rowb[:], 0.0))
        memset_done = C["v"]

        @blk.gpsimd
        def _(e):
            for name, src_, dst, nrows in [("qkv", wqkv, wbqkv_d, D), ("wout", wout, wbout_d, D),
                                           ("w1", w1T, wb1_d, D), ("w2", w2T, wb2_d, D),
                                           ("h1", h1T, hb1_d, DH), ("h2", h2T, hb2_d, DH),
                                           ("w3", w3T, wb3_d, DH)]:
                step = 256
                for r0 in range(0, nrows, step):
                    e.dma_start(out=dst[r0:r0 + step, :], in_=src_[r0:r0 + step, :]).then_inc(WSEM[name], 16)
                    WCNT[name] += 16
                if name in ("qkv", "wout", "w1", "w2"):
                    e.wait_ge(WSEM[name], WCNT[name])

        x_loaded = 0
        x_free = None
        x1_free = None
        x1_store = None
        xln_free = None
        x1ln_free = None
        qkv_free = None
        ctxb_free = None
        rowb_free = None
        pbank_free = [None] * 8
        slot_free = [None, None]
        slot_loaded = [0, 0]
        slot_idx = [0]
        tmpa_free = [0, 0]
        esb_free = [0, 0]
        psS_free = [None, None]
        rbandb_free = [0, 0]
        tmp2_free = [0, 0]
        sA_free = [None, None]
        hB_free = None
        g1_freeB = None
        h_ready = None
        g1_ready = None
        g2_ready = None
        hB_mm_last = 0
        out_stored = None
        x1ln_ready_box = [None]
        x1ln_chunks_box = [None]

        SLCNT = [0, 0]

        def load_slice(kind, pos):
            j = slot_idx[0] % 2
            slot_idx[0] += 1
            @blk.sync
            def _(e, kind=kind, pos=pos, j=j):
                wt(e, slot_free[j])
                if kind == "w12":
                    e.wait_ge(WSEM["w1"], WCNT["w1"])
                    e.wait_ge(WSEM["w2"], WCNT["w2"])
                else:
                    wn = {"qkv": "qkv", "wout": "wout", "h1": "h1", "h2": "h2", "w3": "w3"}[kind]
                    e.wait_ge(WSEM[wn], WCNT[wn])
                if kind == "qkv":
                    e.dma_start(out=wsl[j][:].rearrange("p (k m) -> p k m", k=KC),
                                in_=wbqkv_d[:, ts(pos, D)].rearrange("(k p) m -> p k m", p=P)).then_inc(SL[j], 16)
                    SLCNT[j] += 16
                elif kind == "wout":
                    e.dma_start(out=wsl[j][:].rearrange("p (k m) -> p k m", k=KC),
                                in_=wbout_d.rearrange("(k p) m -> p k m", p=P)).then_inc(SL[j], 16)
                    SLCNT[j] += 16
                elif kind == "w12":
                    e.dma_start(out=wsl[j][:, 0:KC * T].rearrange("p (k m) -> p k m", k=KC),
                                in_=wb1_d[:, ts(pos, T)].rearrange("(k p) m -> p k m", p=P)).then_inc(SL[j], 16)
                    e.dma_start(out=wsl[j][:, KC * T:2 * KC * T].rearrange("p (k m) -> p k m", k=KC),
                                in_=wb2_d[:, ts(pos, T)].rearrange("(k p) m -> p k m", p=P)).then_inc(SL[j], 16)
                    SLCNT[j] += 32
                else:
                    src = {"h1": hb1_d, "h2": hb2_d, "w3": wb3_d}[kind]
                    e.dma_start(out=wsl[j][:].rearrange("p (k m) -> p k m", k=HC),
                                in_=src[:, ts(pos, 256)].rearrange("(k p) m -> p k m", p=P)).then_inc(SL[j], 16)
                    SLCNT[j] += 16
            slot_loaded[j] = SLCNT[j]
            return j

        def emit_ln(src_f32, dst_bf, pre_toks):
            nonlocal rowb_free
            mm_tok = [0, 0]
            first = [True]
            for c in range(KC):
                @blk.vector
                def _(e, c=c):
                    if first[0]:
                        for tok in pre_toks:
                            wt(e, tok)
                        first[0] = False
                    if mm_tok[c % 2] > 0:
                        e.wait_ge(STn, mm_tok[c % 2])
                    if tmp2_free[c % 2] > 0:
                        e.wait_ge(STn, tmp2_free[c % 2])
                    e.tensor_copy(out=xbb[c % 2][:], in_=src_f32[:, ts(c, T)])
                    v_inc(e.tensor_tensor(out=tmp2[c % 2][:], in0=xbb[c % 2][:], in1=xbb[c % 2][:], op=ALU.mult))
                sq_done = C["v"]

                @blk.tensor
                def _(e, c=c, sq_done=sq_done):
                    e.wait_ge(SV, sq_done)
                    if c == 0:
                        wt(e, pbank_free[4])
                        wt(e, pbank_free[5])
                    e.matmul(out=pb[4][:], lhsT=col0_s[:], rhs=xbb[c % 2][:],
                             start=(c == 0), stop=(c == KC - 1), skip_group_check=True)
                    t_inc(e.matmul(out=pb[5][:], lhsT=col0_s[:], rhs=tmp2[c % 2][:],
                                   start=(c == 0), stop=(c == KC - 1), skip_group_check=True))
                mm_tok[c % 2] = C["t"]
                tmp2_free[c % 2] = C["t"]
            stats_done = C["t"]

            m_row = scr[0:1, 0:T]
            v_row = scr[0:1, T:2 * T]
            w_row = rband[0:1, :]

            @blk.vector
            def _(e):
                e.wait_ge(STn, stats_done)
                e.tensor_scalar(out=m_row, in0=pb[4][0:1, :], scalar1=1.0 / D, scalar2=None, op0=ALU.mult)
                e.tensor_scalar(out=v_row, in0=pb[5][0:1, :], scalar1=1.0 / D, scalar2=None, op0=ALU.mult)
                e.tensor_tensor(out=w_row, in0=m_row, in1=m_row, op=ALU.mult)
                v_inc(e.tensor_tensor(out=v_row, in0=v_row, in1=w_row, op=ALU.subtract))
            var_done = C["v"]
            pbank_free[4] = ("v", var_done)
            pbank_free[5] = ("v", var_done)

            @blk.scalar
            def _(e):
                e.wait_ge(SV, var_done)
                e.activation(out=w_row, in_=v_row, func=AF.Ln, bias=eps_s[0:1, :])
                s_inc(e.activation(out=w_row, in_=w_row, func=AF.Exp, scale=-0.5))
            r_done = C["s"]

            @blk.vector
            def _(e):
                e.wait_ge(SS, r_done)
                wt(e, rowb_free)
                e.tensor_tensor(out=v_row, in0=m_row, in1=w_row, op=ALU.mult)
                e.tensor_copy(out=rowb[0:1, :], in_=w_row)
                v_inc(e.tensor_copy(out=rowb[32:33, :], in_=v_row))
            rowb_done = C["v"]

            @blk.tensor
            def _(e):
                e.wait_ge(SV, rowb_done)
                wt(e, pbank_free[2])
                wt(e, pbank_free[3])
                e.matmul(out=pb[2][:], lhsT=brow_s[:], rhs=rowb[:], start=True, stop=True)
                t_inc(e.matmul(out=pb[3][:], lhsT=rsel_s[:], rhs=rowb[:], start=True, stop=True))
            bcast_done = C["t"]
            rowb_free = ("t", bcast_done)

            apply_chunks = []
            for c in range(KC):
                @blk.vector
                def _(e, c=c):
                    if c == 0:
                        e.wait_ge(STn, bcast_done)
                    e.tensor_tensor(out=xbb[0][:], in0=src_f32[:, ts(c, T)], in1=pb[2][:], op=ALU.mult)
                    v_inc(e.tensor_tensor(out=dst_bf[:, ts(c, T)], in0=xbb[0][:], in1=pb[3][:], op=ALU.subtract))
                apply_chunks.append(C["v"])
            apply_done = C["v"]
            pbank_free[2] = ("v", apply_done)
            pbank_free[3] = ("v", apply_done)
            return apply_done, apply_chunks

        def emit_qkv_slice(jslot, sl, ln1_chunks):
            nonlocal qkv_free
            for mc in range(KC):
                m = sl * KC + mc
                bank = 4 + m % 2
                @blk.tensor
                def _(e, jslot=jslot, sl=sl, mc=mc, m=m, bank=bank):
                    if mc == 0:
                        e.wait_ge(SL[jslot], slot_loaded[jslot])
                    wt(e, pbank_free[bank])
                    ins = None
                    for k in range(KC):
                        if sl == 0 and mc == 0:
                            e.wait_ge(SV, ln1_chunks[k])
                        ins = e.matmul(out=pb[bank][:],
                                       lhsT=wsl[jslot][:, k * D + mc * P: k * D + (mc + 1) * P],
                                       rhs=xln_s[:, ts(k, T)], start=(k == 0), stop=(k == KC - 1))
                    t_inc(ins)
                mmd = C["t"]

                dst = (q_s, k_s, v_s)[m // KC]
                col = (m % KC) * T
                @blk.vector
                def _(e, m=m, bank=bank, dst=dst, col=col, mmd=mmd):
                    e.wait_ge(STn, mmd)
                    if m == 0:
                        wt(e, qkv_free)
                    v_inc(e.tensor_scalar(out=dst[:, col:col + T], in0=pb[bank][:],
                                          scalar1=bqkv_s[:, m:m + 1], scalar2=None, op0=ALU.add))
                pbank_free[bank] = ("v", C["v"])
            return C["v"]

        A = {}

        def attn_V1(h):
            c_q, r_q = h // 2, (h % 2) * DK
            prev_sc = A[h - 1]["sc"] if h >= 1 else (A[NH - 1]["sc"] if (NH - 1) in A and "sc" in A[NH - 1] else 0)
            @blk.vector
            def _(e, h=h, c_q=c_q, r_q=r_q, prev_sc=prev_sc):
                if prev_sc > 0:
                    e.wait_ge(STn, prev_sc)
                e.tensor_copy(out=qrep[0:DK, :], in_=q_s[r_q:r_q + DK, ts(c_q, T)])
                e.tensor_copy(out=qrep[DK:P, :], in_=q_s[r_q:r_q + DK, ts(c_q, T)])
                ins = None
                for c in range(KC):
                    ins = e.tensor_tensor(out=tmpa[:, ts(c, T)], in0=k_s[:, ts(c, T)],
                                          in1=qrep[:], op=ALU.mult)
                v_inc(ins)
            A[h] = {"erep": {}, "tmp2": {}}
            A[h]["tmp"] = C["v"]

        def attn_sc(h, qkv_done):
            sb = 6 + (h % 2)
            @blk.tensor
            def _(e, h=h, sb=sb, qkv_done=qkv_done):
                e.wait_ge(SV, A[h]["tmp"])
                if h >= 2:
                    e.wait_ge(SS, A[h - 2]["r"])
                ins = None
                for c in range(KC):
                    ins = e.matmul(out=pb[sb][:], lhsT=bdsel_s[:, ts(c, P)],
                                   rhs=tmpa[:, ts(c, T)],
                                   start=(c == 0), stop=(c == KC - 1), skip_group_check=True)
                t_inc(ins)
            A[h]["sc"] = C["t"]

        def attn_S1(h):
            sb = 6 + (h % 2)
            @blk.scalar
            def _(e, h=h, sb=sb):
                e.wait_ge(STn, A[h]["sc"])
                if h >= 2:
                    e.wait_ge(STn, A[h - 2]["erep"][KC - 1])
                e.activation(out=rband[0:1, 0:1], in_=eps_s[0:1, :], func=AF.Ln)
                s_inc(e.activation(out=esb[h % 2][0:NH, :], in_=pb[sb][0:NH, :],
                                   func=AF.Exp, scale=0.125))
            A[h]["exp"] = C["s"]

        def attn_erep_pair(h, j):
            sb = 6 + (h % 2)
            cpair = (2 * j, 2 * j + 1)
            if j == 0:
                tok2 = ("v", A[h - 2]["ctx"]) if h >= 2 else pbank_free[2]
                tok3 = ("v", A[h - 1]["tmp2"][7]) if h >= 1 else pbank_free[3]
            else:
                tok2 = ("v", A[h]["tmp2"][2 * j - 2])
                tok3 = ("v", A[h]["tmp2"][2 * j - 1])
            @blk.tensor
            def _(e, h=h, j=j, cpair=cpair, sb=sb, tok2=tok2, tok3=tok3):
                if j == 0:
                    e.wait_ge(SS, A[h]["exp"])
                    if h == 0:
                        e.wait_ge(SV, memset_done)
                wt(e, tok2)
                wt(e, tok3)
                for c in cpair:
                    gp = c % 2
                    t_inc(e.matmul(out=pb[2 + gp][:], lhsT=gsel_s[:, ts(c, P)],
                                   rhs=esb[h % 2][:], start=True, stop=True))
                    A[h]["erep"][c] = C["t"]
                if j == 0:
                    t_inc(e.matmul(out=pb[sb][:], lhsT=densel_s[:],
                                   rhs=esb[h % 2][:], start=True, stop=True))
                    A[h]["den"] = C["t"]

        def attn_tmp2_pair(h, j):
            cpair = (2 * j, 2 * j + 1)
            if j == 0:
                utok = A[h - 1]["u"] if h >= 1 else 0
            else:
                utok = A[h]["upair"][j - 1]
            @blk.vector
            def _(e, h=h, cpair=cpair, utok=utok):
                for c in cpair:
                    gp = c % 2
                    e.wait_ge(STn, A[h]["erep"][c])
                    if utok > 0:
                        e.wait_ge(STn, utok)
                    v_inc(e.tensor_tensor(out=tmp2[gp][:], in0=pb[2 + gp][:],
                                          in1=v_s[:, ts(c, T)], op=ALU.mult))
                    A[h]["tmp2"][c] = C["v"]
            if j == 3:
                A[h]["tmp2_done"] = C["v"]

        def attn_S2a(h):
            sb = 6 + (h % 2)
            @blk.scalar
            def _(e, h=h, sb=sb):
                e.wait_ge(STn, A[h]["den"])
                if h >= 2:
                    e.wait_ge(STn, A[h - 2]["rrep"])
                e.activation(out=rband[32:33, :], in_=pb[sb][32:33, :], func=AF.Ln)
                s_inc(e.activation(out=rbandb[32:33, ts(h % 2, T)], in_=rband[32:33, :],
                                   func=AF.Exp, scale=-1.0))
            A[h]["r"] = C["s"]

        def attn_u_pair(h, j):
            ub = 4 + (h % 2)
            cpair = (2 * j, 2 * j + 1)
            @blk.tensor
            def _(e, h=h, ub=ub, cpair=cpair):
                for c in cpair:
                    gp = c % 2
                    e.wait_ge(SV, A[h]["tmp2"][c])
                    if c == 0:
                        wt(e, pbank_free[4 + h % 2])
                    t_inc(e.matmul(out=pb[ub][:], lhsT=dsel_s[:], rhs=tmp2[gp][:],
                                   start=(c == 0), stop=(c == KC - 1), skip_group_check=True))
            A[h].setdefault("upair", {})[j] = C["t"]
            if j == 3:
                A[h]["u"] = C["t"]

        def attn_rrep(h):
            if (h + 1) in A and 6 in A[h + 1]["tmp2"]:
                tok = ("v", A[h + 1]["tmp2"][6])
            elif h >= 1 and "ctx" in A[h - 1]:
                tok = ("v", A[h - 1]["ctx"])
            else:
                tok = None
            @blk.tensor
            def _(e, h=h, tok=tok):
                e.wait_ge(SS, A[h]["r"])
                wt(e, tok)
                t_inc(e.matmul(out=pb[2][:], lhsT=rsel_s[:],
                               rhs=rbandb[:, ts(h % 2, T)], start=True, stop=True))
            A[h]["rrep"] = C["t"]

        def attn_S2b(h):
            c_q, r_q = h // 2, (h % 2) * DK
            ub = 4 + (h % 2)
            @blk.scalar
            def _(e, h=h, ub=ub, c_q=c_q, r_q=r_q):
                e.wait_ge(STn, A[h]["u"])
                if h < 2:
                    wt(e, ctxb_free)
                s_inc(e.activation(out=ctxb[r_q:r_q + DK, ts(c_q, T)], in_=pb[ub][0:DK, :],
                                   func=AF.Copy))
            A[h]["evac"] = C["s"]
            pbank_free[4 + h % 2] = ("s", A[h]["evac"])

        def attn_ctx(h):
            c_q, r_q = h // 2, (h % 2) * DK
            @blk.vector
            def _(e, h=h, c_q=c_q, r_q=r_q):
                e.wait_ge(STn, A[h]["rrep"])
                e.wait_ge(SS, A[h]["evac"])
                if h < 2:
                    wt(e, ctxb_free)
                v_inc(e.tensor_tensor(out=ctxb[r_q:r_q + DK, ts(c_q, T)],
                                      in0=ctxb[r_q:r_q + DK, ts(c_q, T)],
                                      in1=pb[2][r_q:r_q + DK, :], op=ALU.mult))
            A[h]["ctx"] = C["v"]
            pbank_free[2] = ("v", A[h]["ctx"])

        def emit_w12_slice(jslot, mr):
            nonlocal hB_free, h_ready, x1ln_free
            mm_toks = {}
            silu_base = C["s"]
            for mc in range(4):
                mcg = mr * 4 + mc
                @blk.tensor
                def _(e, jslot=jslot, mc=mc, mcg=mcg, mr=mr):
                    if mc == 0:
                        e.wait_ge(SL[jslot], slot_loaded[jslot])
                        if mr > 0:
                            wt(e, x1ln_ready_box[0])
                    wt(e, pbank_free[0] if mc % 2 == 0 else pbank_free[1])
                    bA = mc % 2
                    for k in range(KC):
                        if mr == 0 and mc == 0:
                            e.wait_ge(SV, x1ln_chunks_box[0][k])
                        ins = e.matmul(out=pb[bA][:],
                                       lhsT=wsl[jslot][:, k * T + mc * P: k * T + (mc + 1) * P],
                                       rhs=x1ln_s[:, ts(k, T)], start=(k == 0), stop=(k == KC - 1))
                    t_inc(ins)
                    wt(e, pbank_free[2 + bA])
                    for k in range(KC):
                        ins = e.matmul(out=pb[2 + bA][:],
                                       lhsT=wsl[jslot][:, KC * T + k * T + mc * P: KC * T + k * T + (mc + 1) * P],
                                       rhs=x1ln_s[:, ts(k, T)], start=(k == 0), stop=(k == KC - 1))
                    t_inc(ins)
                mm_toks[mc] = (C["t"] - 1, C["t"])
                pbank_free[mc % 2] = ("s", silu_base + mc + 1)

            stt_base = C["v"]
            prev_stt = (sA_free[0], sA_free[1])
            @blk.scalar
            def _(e, mr=mr, mm_toks=mm_toks, stt_base=stt_base, prev_stt=prev_stt):
                for mc in range(4):
                    mcg = mr * 4 + mc
                    e.wait_ge(STn, mm_toks[mc][0])
                    if mc >= 2:
                        e.wait_ge(SV, stt_base + mc - 1)
                    else:
                        wt(e, prev_stt[mc])
                    s_inc(e.activation(out=sA[mc % 2][:], in_=pb[mc % 2][:], func=AF.Silu,
                                       scale=s1_s[:, mcg:mcg + 1]))
            assert C["s"] == silu_base + 4

            for mc in range(4):
                mcg = mr * 4 + mc
                @blk.vector
                def _(e, mc=mc, mcg=mcg, mm_toks=mm_toks, silu_base=silu_base):
                    e.wait_ge(STn, mm_toks[mc][1])
                    e.wait_ge(SS, silu_base + mc + 1)
                    if mcg == 0:
                        wt(e, hB_free)
                    v_inc(e.scalar_tensor_tensor(out=hB[:, ts(mcg, T)], in0=pb[2 + mc % 2][:],
                                                 scalar=s2_s[:, mcg:mcg + 1], in1=sA[mc % 2][:],
                                                 op0=ALU.mult, op1=ALU.mult))
                pbank_free[2 + mc % 2] = ("v", C["v"])
                sA_free[mc % 2] = ("v", C["v"])
            if mr == 7:
                h_ready = ("v", C["v"])
                x1ln_free = ("t", C["t"])

        def hmul_mc(jslot, kind, sidx, mc):
            rhs_buf = hB if kind == "h1" else g1B
            rhs_ready = h_ready if kind == "h1" else g1_ready
            mcg = sidx * 2 + mc
            gelu_inc = C["s"] + 1
            @blk.tensor
            def _(e, jslot=jslot, mc=mc, rhs_buf=rhs_buf, rhs_ready=rhs_ready, sidx=sidx):
                if mc == 0:
                    e.wait_ge(SL[jslot], slot_loaded[jslot])
                    wt(e, rhs_ready)
                wt(e, pbank_free[mc % 2])
                ins = None
                for k in range(HC):
                    ins = e.matmul(out=pb[mc % 2][:],
                                   lhsT=wsl[jslot][:, k * 256 + mc * P: k * 256 + (mc + 1) * P],
                                   rhs=rhs_buf[:, ts(k, T)], start=(k == 0), stop=(k == HC - 1))
                t_inc(ins)
            mm_done = C["t"]

            dst_buf = g1B if kind == "h1" else hB
            scol = sh1_s if kind == "h1" else sh2_s
            @blk.scalar
            def _(e, kind=kind, mcg=mcg, mc=mc, dst_buf=dst_buf, scol=scol, mm_done=mm_done):
                e.wait_ge(STn, mm_done)
                if mcg == 0:
                    if kind == "h1":
                        wt(e, g1_freeB)
                    else:
                        e.wait_ge(STn, hB_mm_last)
                s_inc(e.activation(out=dst_buf[:, ts(mcg, T)], in_=pb[mc % 2][:], func=AF.Gelu,
                                   scale=scol[:, mcg:mcg + 1]))
            assert C["s"] == gelu_inc
            pbank_free[mc % 2] = ("s", gelu_inc)
            return mm_done

        def finish_hmul(kind, sidx, mm_done):
            nonlocal g1_ready, g2_ready, hB_mm_last, g1_freeB
            if kind == "h1" and sidx == 15:
                g1_ready = ("s", C["s"])
                hB_mm_last = C["t"]
            if kind == "h2" and sidx == 15:
                g2_ready = ("s", C["s"])
                g1_freeB = ("t", C["t"])

        def emit_hmul_slice(jslot, kind, sidx):
            for mc in range(2):
                hmul_mc(jslot, kind, sidx, mc)
            finish_hmul(kind, sidx, None)

        def emit_w3_slice(jslot, sidx, tile):
            nonlocal hB_free, x1_free
            for mc in range(2):
                mcg = sidx * 2 + mc
                @blk.tensor
                def _(e, jslot=jslot, mc=mc, sidx=sidx):
                    if mc == 0:
                        e.wait_ge(SL[jslot], slot_loaded[jslot])
                        wt(e, g2_ready)
                    wt(e, pbank_free[mc % 2])
                    for k in range(HC):
                        ins = e.matmul(out=pb[mc % 2][:],
                                       lhsT=wsl[jslot][:, k * 256 + mc * P: k * 256 + (mc + 1) * P],
                                       rhs=hB[:, ts(k, T)], start=(k == 0), stop=(k == HC - 1))
                    t_inc(ins)
                mmd = C["t"]

                @blk.vector
                def _(e, mc=mc, mcg=mcg, mmd=mmd):
                    e.wait_ge(STn, mmd)
                    v_inc(e.scalar_tensor_tensor(out=x1_s[:, ts(mcg, T)], in0=pb[mc % 2][:],
                                                 scalar=s3_s[:, mcg:mcg + 1], in1=x1_s[:, ts(mcg, T)],
                                                 op0=ALU.mult, op1=ALU.add))
                pbank_free[mc % 2] = ("v", C["v"])
            if sidx == 3:
                hB_free = ("t", C["t"])

        def emit_out_store(tile):
            nonlocal out_stored, x1_free
            @blk.sync
            def _(e, tile=tile):
                e.wait_ge(SV, C["v"])
                e.dma_start(out=outT[:, ts(tile, T)].rearrange("(k p) t -> p k t", p=P),
                            in_=x1_s[:].rearrange("p (k t) -> p k t", k=KC)).then_inc(SO, 16)
            C["o"] += 16
            out_stored = ("o", C["o"])
            x1_free = ("o", C["o"])

        def emit_wout(ctx_pair_toks, tile):
            nonlocal ctxb_free, x_free
            w_loaded = slot_loaded[wout_slot[0]]
            for m in range(KC):
                bank = m % 2
                @blk.tensor
                def _(e, m=m, bank=bank, w_loaded=w_loaded):
                    if m == 0:
                        e.wait_ge(SL[wout_slot[0]], w_loaded)
                    wt(e, pbank_free[bank])
                    ins = None
                    for k in range(KC):
                        if m == 0:
                            e.wait_ge(SV, ctx_pair_toks[k])
                        ins = e.matmul(out=pb[bank][:],
                                       lhsT=wsl[wout_slot[0]][:, k * D + m * P: k * D + (m + 1) * P],
                                       rhs=ctxb[:, ts(k, T)], start=(k == 0), stop=(k == KC - 1))
                    t_inc(ins)
                wmm = C["t"]
                if m == KC - 1:
                    ctxb_free = ("t", wmm)

                @blk.vector
                def _(e, m=m, bank=bank, wmm=wmm):
                    e.wait_ge(STn, wmm)
                    if m == 0:
                        wt(e, x1_free)
                    v_inc(e.scalar_tensor_tensor(out=x1_s[:, ts(m, T)], in0=pb[bank][:],
                                                 scalar=bout_s[:, m:m + 1], in1=xbuf[:, ts(m, T)],
                                                 op0=ALU.add, op1=ALU.add))
                pbank_free[bank] = ("v", C["v"])
            x_free = ("v", C["v"])
            return C["v"]

        x_loaded = XCNT[0]

        wout_slot = [0]
        w12_pre = [None, None]

        def dbg_barrier():
            @blk.tensor
            def _(e):
                e.wait_ge(SV, C["v"]); e.wait_ge(SS, C["s"]); e.wait_ge(SI, C["i"]); e.wait_ge(SO, C["o"])
                t_inc(e.engine_nop() if hasattr(e, "engine_nop") else e.nop())
            tcnt = C["t"]
            @blk.vector
            def _(e, tcnt=tcnt):
                e.wait_ge(STn, tcnt)
                v_inc(e.engine_nop())
            vcnt = C["v"]
            @blk.scalar
            def _(e, tcnt=tcnt, vcnt=vcnt):
                e.wait_ge(STn, tcnt); e.wait_ge(SV, vcnt)
                s_inc(e.nop())
            scnt = C["s"]
            @blk.sync
            def _(e, tcnt=tcnt, vcnt=vcnt, scnt=scnt):
                e.wait_ge(STn, tcnt); e.wait_ge(SV, vcnt); e.wait_ge(SS, scnt)

        def do_slice(kind, pos, tile):
            if kind == "w12" and pos in (0, 1) and w12_pre[pos] is not None:
                jj = w12_pre[pos]
                w12_pre[pos] = None
            else:
                jj = load_slice(kind, pos)
            if kind == "w12":
                emit_w12_slice(jj, pos)
            elif kind in ("h1", "h2"):
                emit_hmul_slice(jj, kind, pos)
            else:
                emit_w3_slice(jj, pos, tile)
            slot_free[jj] = ("t", C["t"])

        for i in range(NT):
            ffn_slices = []
            if i > 0:
                ffn_slices = [("w12", mr) for mr in range(8)] + \
                             [("h1", s_) for s_ in range(16)] + \
                             [("h2", s_) for s_ in range(16)]

            pre = [("x", x_loaded), xln_free]
            ln1_done, ln1_chunks = emit_ln(xbuf, xln_s, pre)

            qkv_done = 0
            fidx = 0
            for sl in range(3):
                if fidx < 8 and i > 0:
                    kind, pos = ffn_slices[fidx]; fidx += 1
                    do_slice(kind, pos, i - 1)
                jq = load_slice("qkv", sl)
                qkv_done = emit_qkv_slice(jq, sl, ln1_chunks)
                slot_free[jq] = ("t", C["t"])
            xln_free = ("t", C["t"])

            while fidx < 8 and i > 0:
                kind, pos = ffn_slices[fidx]; fidx += 1
                do_slice(kind, pos, i - 1)

            pend = []

            def queue_ffn():
                nonlocal fidx
                if i > 0 and fidx < len(ffn_slices):
                    kind, pos = ffn_slices[fidx]; fidx += 1
                    jj = load_slice(kind, pos)
                    pend.append([jj, kind, pos, 0])
                    pend.append([jj, kind, pos, 1])

            def run_ffn_mc():
                if pend:
                    jj, kind, pos, mc = pend.pop(0)
                    mm = hmul_mc(jj, kind, pos, mc)
                    if mc == 1:
                        finish_hmul(kind, pos, mm)
                        slot_free[jj] = ("t", C["t"])

            for h in range(NH + 2):
                if h < NH:
                    attn_V1(h)
                    queue_ffn()
                    run_ffn_mc()
                    attn_sc(h, qkv_done)
                    attn_S1(h)
                if 1 <= h <= NH:
                    hh = h - 1
                    attn_erep_pair(hh, 0)
                    attn_tmp2_pair(hh, 0)
                    run_ffn_mc()
                    attn_erep_pair(hh, 1)
                    attn_u_pair(hh, 0)
                    attn_tmp2_pair(hh, 1)
                    queue_ffn()
                    run_ffn_mc()
                    attn_erep_pair(hh, 2)
                    attn_u_pair(hh, 1)
                    attn_tmp2_pair(hh, 2)
                    attn_S2a(hh)
                    run_ffn_mc()
                    attn_erep_pair(hh, 3)
                    attn_u_pair(hh, 2)
                    attn_tmp2_pair(hh, 3)
                    run_ffn_mc()
                    attn_u_pair(hh, 3)
                if 2 <= h <= NH + 1:
                    attn_rrep(h - 2)
                    attn_S2b(h - 2)
                    attn_ctx(h - 2)
            ctx_done = A[NH - 1]["ctx"]
            qkv_free = ("v", ctx_done)
            pbank_free[2] = ("v", A[NH - 1]["ctx"])
            pbank_free[3] = ("v", A[NH - 1]["tmp2_done"])
            while pend:
                run_ffn_mc()
            while i > 0 and fidx < len(ffn_slices):
                kind, pos = ffn_slices[fidx]; fidx += 1
                do_slice(kind, pos, i - 1)

            if i > 0:
                for s_ in range(4):
                    do_slice("w3", s_, i - 1)
            jw = load_slice("wout", 0)
            wout_slot[0] = jw
            wout_loaded = slot_loaded[jw]
            w12_pre[0] = load_slice("w12", 0)
            if i > 0:
                emit_out_store(i - 1)

            x1_done = emit_wout([A[2 * k_ + 1]["ctx"] for k_ in range(KC)], i)
            slot_free[wout_slot[0]] = ("t", C["t"])

            if i + 1 < NT:
                @blk.sync
                def _(e, i=i):
                    wt(e, x_free)
                    e.dma_start(out=xbuf[:].rearrange("p (k t) -> p k t", k=KC),
                                in_=xT[:, ts(i + 1, T)].rearrange("(k p) t -> p k t", p=P)).then_inc(SX, 16)
                XCNT[0] += 16
                x_loaded = XCNT[0]
            w12_pre[1] = load_slice("w12", 1)

            ln2_done, ln2_chunks = emit_ln(x1_s, x1ln_s, [x1ln_free, ("v", x1_done)])
            x1ln_ready_box[0] = ("v", ln2_done)
            x1ln_chunks_box[0] = ln2_chunks
            if DEBUG_BARRIER:
                dbg_barrier()

        @blk.tensor
        def _(e, ln2_done=ln2_done):
            e.wait_ge(SV, ln2_done)
        ffn_slices = [("w12", mr) for mr in range(8)] + \
                     [("h1", s_) for s_ in range(16)] + \
                     [("h2", s_) for s_ in range(16)]
        for kind, pos in ffn_slices:
            do_slice(kind, pos, NT - 1)
        for s_ in range(4):
            do_slice("w3", s_, NT - 1)
        emit_out_store(NT - 1)

        @blk.sync
        def _(e):
            e.wait_ge(SO, C["o"])

    return nc


_PROG_CACHE = {}

def _get_prog():
    if "nc" not in _PROG_CACHE:
        _PROG_CACHE["nc"] = build_program()
    return _PROG_CACHE["nc"]


def _consts():
    bf = ml_dtypes.bfloat16
    dsel = np.zeros((P, P), bf)
    for p in range(P):
        dsel[p, p % DK] = 1
    bdsel = np.zeros((P, KC * P), bf)
    gsel = np.zeros((P, KC * P), bf)
    for c in range(KC):
        for p in range(P):
            gg = 2 * c + p // DK
            bdsel[p, c * P + gg] = 1
            gsel[gg, c * P + p] = 1
    densel = np.zeros((P, P), bf); densel[0:NH, 32] = 1
    rsel = np.zeros((P, P), bf); rsel[32, :] = 1
    brow = np.zeros((P, P), bf); brow[0, :] = 1
    col0 = np.zeros((P, P), bf); col0[:, 0] = 1
    eps = np.full((P, 1), EPS, np.float32)
    return dsel, bdsel, gsel, densel, rsel, brow, col0, eps


def kernel(L0_attn_x, L0_attn_W_qkv, L0_attn_b_qkv, L0_attn_W_out, L0_attn_b_out,
           L0_ffn_w1_w, L0_ffn_w1_s, L0_ffn_w2_w, L0_ffn_w2_s,
           L0_ffn_h1_w, L0_ffn_h1_s, L0_ffn_h2_w, L0_ffn_h2_s,
           L0_ffn_w3_w, L0_ffn_w3_s):
    x = np.asarray(L0_attn_x)
    B = x.shape[0]
    wq = np.asarray(L0_attn_W_qkv)
    wq3 = wq.reshape(D, NH, 3, DK)
    wqkv = np.ascontiguousarray(wq3.transpose(0, 2, 1, 3).reshape(D, 3 * D))
    bq3 = np.asarray(L0_attn_b_qkv).reshape(NH, 3, DK)
    bqkv = np.ascontiguousarray(bq3.transpose(1, 0, 2).reshape(3 * D // P, P).T)
    bout = np.ascontiguousarray(np.asarray(L0_attn_b_out).reshape(D // P, P).T)
    w1T = np.ascontiguousarray(np.asarray(L0_ffn_w1_w).T)
    w2T = np.ascontiguousarray(np.asarray(L0_ffn_w2_w).T)
    h1T = np.ascontiguousarray(np.asarray(L0_ffn_h1_w).T)
    h2T = np.ascontiguousarray(np.asarray(L0_ffn_h2_w).T)
    w3T = np.ascontiguousarray(np.asarray(L0_ffn_w3_w).T)
    s1 = np.ascontiguousarray(np.asarray(L0_ffn_w1_s).reshape(HC, P).T)
    s2 = np.ascontiguousarray(np.asarray(L0_ffn_w2_s).reshape(HC, P).T)
    sh1 = np.ascontiguousarray(np.asarray(L0_ffn_h1_s).reshape(HC, P).T)
    sh2 = np.ascontiguousarray(np.asarray(L0_ffn_h2_s).reshape(HC, P).T)
    s3 = np.ascontiguousarray(np.asarray(L0_ffn_w3_s).reshape(D // P, P).T)
    dsel, bdsel, gsel, densel, rsel, brow, col0, eps = _consts()
    cb16 = np.ascontiguousarray(np.concatenate(
        [dsel, bdsel, gsel, densel, rsel, brow, col0], axis=1))
    cb32 = np.ascontiguousarray(np.concatenate(
        [eps, bqkv, bout, s1, s2, sh1, sh2, s3], axis=1))

    shared = dict(wqkv=wqkv, wout=np.ascontiguousarray(np.asarray(L0_attn_W_out)),
                  w1T=w1T, w2T=w2T, h1T=h1T, h2T=h2T, w3T=w3T,
                  cb16=cb16, cb32=cb32)

    in_maps = []
    for c in range(B):
        m = dict(shared)
        m["xT"] = np.ascontiguousarray(x[c].T)
        in_maps.append(m)

    nc = _get_prog()
    res = run_bass_kernel_spmd(nc, in_maps, core_ids=list(range(N_CORES)))
    out = np.stack([np.asarray(r["outT"], dtype=np.float32).T for r in res.results], axis=0)
    return out


# version 19
# speedup vs baseline: 1.0385x; 1.0385x over previous
import numpy as np
import ml_dtypes
from contextlib import ExitStack

import concourse.bass as bass
import concourse.mybir as mybir
from concourse.bass_utils import run_bass_kernel_spmd

F32 = mybir.dt.float32
BF16 = mybir.dt.bfloat16
AF = mybir.ActivationFunctionType
ALU = mybir.AluOpType
ts = bass.ts

P = 128
D = 1024
DH = 4096
NH, DK = 16, 64
S = 4096
T = 512
NT = S // T
KC = D // P
HC = DH // P
N_CORES = 8
EPS = 1e-5
DEBUG_BARRIER = False
SLOT = 8192


def build_program():
    nc = bass.Bass("TRN2", target_bir_lowering=False)

    xT = nc.dram_tensor("xT", [D, S], F32, kind="ExternalInput").ap()
    wqkv = nc.dram_tensor("wqkv", [D, 3 * D], F32, kind="ExternalInput").ap()
    wout = nc.dram_tensor("wout", [D, D], F32, kind="ExternalInput").ap()
    w1T = nc.dram_tensor("w1T", [D, DH], F32, kind="ExternalInput").ap()
    w2T = nc.dram_tensor("w2T", [D, DH], F32, kind="ExternalInput").ap()
    h1T = nc.dram_tensor("h1T", [DH, DH], F32, kind="ExternalInput").ap()
    h2T = nc.dram_tensor("h2T", [DH, DH], F32, kind="ExternalInput").ap()
    w3T = nc.dram_tensor("w3T", [DH, D], F32, kind="ExternalInput").ap()
    cb16 = nc.dram_tensor("cb16", [P, 2688], BF16, kind="ExternalInput").ap()
    cb32 = nc.dram_tensor("cb32", [P, 169], F32, kind="ExternalInput").ap()

    outT = nc.dram_tensor("outT", [D, S], F32, kind="ExternalOutput").ap()

    wbqkv_d = nc.dram_tensor("wbqkv_d", [D, 3 * D], BF16, kind="Internal").ap()
    wbout_d = nc.dram_tensor("wbout_d", [D, D], BF16, kind="Internal").ap()
    wb1_d = nc.dram_tensor("wb1_d", [D, DH], BF16, kind="Internal").ap()
    wb2_d = nc.dram_tensor("wb2_d", [D, DH], BF16, kind="Internal").ap()
    hb1_d = nc.dram_tensor("hb1_d", [DH, DH], BF16, kind="Internal").ap()
    hb2_d = nc.dram_tensor("hb2_d", [DH, DH], BF16, kind="Internal").ap()
    wb3_d = nc.dram_tensor("wb3_d", [DH, D], BF16, kind="Internal").ap()

    with ExitStack() as g:
        E = g.enter_context
        cb16_s = E(nc.sbuf_tensor("cb16_s", [P, 2688], BF16))
        cb32_s = E(nc.sbuf_tensor("cb32_s", [P, 169], F32))
        dsel_s = cb16_s[:, 0:P]
        bdsel_s = cb16_s[:, P:P + KC * P]
        gsel_s = cb16_s[:, P + KC * P:P + 2 * KC * P]
        densel_s = cb16_s[:, 2176:2304]
        rsel_s = cb16_s[:, 2304:2432]
        brow_s = cb16_s[:, 2432:2560]
        col0_s = cb16_s[:, 2560:2688]
        eps_s = cb32_s[:, 0:1]
        bqkv_s = cb32_s[:, 1:25]
        bout_s = cb32_s[:, 25:33]
        s1_s = cb32_s[:, 33:65]
        s2_s = cb32_s[:, 65:97]
        sh1_s = cb32_s[:, 97:129]
        sh2_s = cb32_s[:, 129:161]
        s3_s = cb32_s[:, 161:169]

        wsl = [E(nc.sbuf_tensor(f"wsl{j}", [P, SLOT], BF16)) for j in range(2)]
        xbuf = E(nc.sbuf_tensor("xbuf", [P, KC * T], F32))
        xln_s = E(nc.sbuf_tensor("xln_s", [P, KC * T], BF16))
        x1ln_s = E(nc.sbuf_tensor("x1ln_s", [P, KC * T], BF16))
        q_s = E(nc.sbuf_tensor("q_s", [P, KC * T], BF16))
        k_s = E(nc.sbuf_tensor("k_s", [P, KC * T], BF16))
        v_s = E(nc.sbuf_tensor("v_s", [P, KC * T], BF16))
        qrep = E(nc.sbuf_tensor("qrep", [P, T], BF16))
        tmpa = E(nc.sbuf_tensor("tmpa", [P, KC * T], BF16))
        esb = [E(nc.sbuf_tensor(f"esb{j}", [P, T], BF16)) for j in range(2)]
        tmp2 = [E(nc.sbuf_tensor(f"tmp2_{j}", [P, T], BF16)) for j in range(2)]
        rband = E(nc.sbuf_tensor("rband", [33, T], F32))
        rbandb = E(nc.sbuf_tensor("rbandb", [P, 2 * T], BF16))
        scr = E(nc.sbuf_tensor("scr", [1, 2 * T], F32))
        rowb = E(nc.sbuf_tensor("rowb", [P, T], BF16))

        sA = [E(nc.sbuf_tensor(f"sA{j}", [P, T], BF16)) for j in range(2)]
        xbb = [E(nc.sbuf_tensor("xbb0", [P, T], BF16)), qrep]
        ctxb = E(nc.sbuf_tensor("ctxb", [P, KC * T], BF16))
        x1_s = E(nc.sbuf_tensor("x1_s", [P, KC * T], F32))
        hB = E(nc.sbuf_tensor("hB", [P, HC * T], BF16))
        g1B = E(nc.sbuf_tensor("g1B", [P, HC * T], BF16))

        pb = [E(nc.psum_tensor(f"pb{i}", [P, T], F32)) for i in range(8)]

        SW = E(nc.semaphore("SW"))
        SI = E(nc.semaphore("SI"))
        SO = E(nc.semaphore("SO"))
        STn = E(nc.semaphore("STn"))
        SV = E(nc.semaphore("SV"))
        SS = E(nc.semaphore("SS"))
        WSEM = {name: E(nc.semaphore(f"W_{name}")) for name in
                ("qkv", "wout", "w1", "w2", "h1", "h2", "w3")}
        WCNT = {name: 0 for name in WSEM}
        SL = [E(nc.semaphore(f"SL{j}")) for j in range(2)]
        SX = E(nc.semaphore("SX"))
        blk = E(nc.Block())
        C = dict(w=0, i=0, o=0, t=0, v=0, s=0)
        SEMS = {"t": STn, "v": SV, "s": SS, "i": SI, "o": SO, "w": SW, "x": None}

        def t_inc(instr):
            C["t"] += 1
            return instr.then_inc(STn, 1)

        def v_inc(instr):
            C["v"] += 1
            return instr.then_inc(SV, 1)

        def s_inc(instr):
            C["s"] += 1
            return instr.then_inc(SS, 1)

        def wt(e, tok):
            if tok is None:
                return
            if tok[1] > 0:
                sem = SX if tok[0] == "x" else SEMS[tok[0]]
                e.wait_ge(sem, tok[1])

        XCNT = [0]
        @blk.sync
        def _(e):
            e.dma_start(out=xbuf[:].rearrange("p (k t) -> p k t", k=KC),
                        in_=xT[:, 0:T].rearrange("(k p) t -> p k t", p=P)).then_inc(SX, 16)
            for dst, src in [(cb16_s[:], cb16), (cb32_s[:], cb32)]:
                e.dma_start(out=dst, in_=src).then_inc(SI, 16)
                C["i"] += 16
        XCNT[0] += 16
        consts_loaded = C["i"]

        @blk.tensor
        def _(e):
            e.wait_ge(SI, consts_loaded)
        @blk.scalar
        def _(e):
            e.wait_ge(SI, consts_loaded)

        @blk.vector
        def _(e):
            e.wait_ge(SI, consts_loaded)
            e.memset(esb[0][:], 0.0)
            e.memset(esb[1][:], 0.0)
            e.memset(rbandb[:], 0.0)
            v_inc(e.memset(rowb[:], 0.0))
        memset_done = C["v"]

        def emit_casts():
            @blk.gpsimd
            def _(e):
                for name, src_, dst, nrows in [("wout", wout, wbout_d, D),
                                               ("w1", w1T, wb1_d, D), ("w2", w2T, wb2_d, D),
                                               ("qkv", wqkv, wbqkv_d, D)]:
                    for r0 in range(0, nrows, 256):
                        e.dma_start(out=dst[r0:r0 + 256, :], in_=src_[r0:r0 + 256, :]).then_inc(WSEM[name], 16)
                        WCNT[name] += 16
                    e.wait_ge(WSEM[name], WCNT[name])
                for name, src_, dst, cstep in [("h1", h1T, hb1_d, 512), ("h2", h2T, hb2_d, 512),
                                               ("w3", w3T, wb3_d, 256)]:
                    ncols = src_.shape[1]
                    for c0 in range(0, ncols, cstep):
                        e.dma_start(out=dst[:, c0:c0 + cstep], in_=src_[:, c0:c0 + cstep]).then_inc(WSEM[name], 16)
                        WCNT[name] += 16
                    if name != "w3":
                        e.wait_ge(WSEM[name], WCNT[name])

        x_loaded = 0
        x_free = None
        x1_free = None
        x1_store = None
        xln_free = None
        x1ln_free = None
        qkv_free = None
        ctxb_free = None
        rowb_free = None
        pbank_free = [None] * 8
        slot_free = [None, None]
        slot_loaded = [0, 0]
        slot_idx = [0]
        tmpa_free = [0, 0]
        esb_free = [0, 0]
        psS_free = [None, None]
        rbandb_free = [0, 0]
        tmp2_free = [0, 0]
        sA_free = [None, None]
        hB_free = None
        g1_freeB = None
        h_ready = None
        g1_ready = None
        g2_ready = None
        hB_mm_last = 0
        out_stored = None
        x1ln_ready_box = [None]
        x1ln_chunks_box = [None]

        SLCNT = [0, 0]

        def load_slice(kind, pos, direct=False):
            j = slot_idx[0] % 2
            slot_idx[0] += 1
            if direct:
                @blk.gpsimd
                def _(e, pos=pos, j=j):
                    wt(e, slot_free[j])
                    e.dma_start(out=wsl[j][:].rearrange("p (k m) -> p k m", k=KC),
                                in_=wqkv[:, ts(pos, D)].rearrange("(k p) m -> p k m", p=P)).then_inc(SL[j], 16)
                SLCNT[j] += 16
                slot_loaded[j] = SLCNT[j]
                return j
            @blk.sync
            def _(e, kind=kind, pos=pos, j=j):
                wt(e, slot_free[j])
                if kind == "w12":
                    e.wait_ge(WSEM["w1"], WCNT["w1"])
                    e.wait_ge(WSEM["w2"], WCNT["w2"])
                elif kind in ("h1", "h2"):
                    e.wait_ge(WSEM[kind], 16 * (pos // 2 + 1))
                elif kind == "w3":
                    e.wait_ge(WSEM["w3"], 16 * (pos + 1))
                else:
                    wn = {"qkv": "qkv", "wout": "wout"}[kind]
                    e.wait_ge(WSEM[wn], WCNT[wn])
                if kind == "qkv":
                    e.dma_start(out=wsl[j][:].rearrange("p (k m) -> p k m", k=KC),
                                in_=wbqkv_d[:, ts(pos, D)].rearrange("(k p) m -> p k m", p=P)).then_inc(SL[j], 16)
                    SLCNT[j] += 16
                elif kind == "wout":
                    e.dma_start(out=wsl[j][:].rearrange("p (k m) -> p k m", k=KC),
                                in_=wbout_d.rearrange("(k p) m -> p k m", p=P)).then_inc(SL[j], 16)
                    SLCNT[j] += 16
                elif kind == "w12":
                    e.dma_start(out=wsl[j][:, 0:KC * T].rearrange("p (k m) -> p k m", k=KC),
                                in_=wb1_d[:, ts(pos, T)].rearrange("(k p) m -> p k m", p=P)).then_inc(SL[j], 16)
                    e.dma_start(out=wsl[j][:, KC * T:2 * KC * T].rearrange("p (k m) -> p k m", k=KC),
                                in_=wb2_d[:, ts(pos, T)].rearrange("(k p) m -> p k m", p=P)).then_inc(SL[j], 16)
                    SLCNT[j] += 32
                else:
                    src = {"h1": hb1_d, "h2": hb2_d, "w3": wb3_d}[kind]
                    e.dma_start(out=wsl[j][:].rearrange("p (k m) -> p k m", k=HC),
                                in_=src[:, ts(pos, 256)].rearrange("(k p) m -> p k m", p=P)).then_inc(SL[j], 16)
                    SLCNT[j] += 16
            slot_loaded[j] = SLCNT[j]
            return j

        def emit_ln(src_f32, dst_bf, pre_toks):
            nonlocal rowb_free
            mm_tok = [0, 0]
            first = [True]
            for c in range(KC):
                @blk.vector
                def _(e, c=c):
                    if first[0]:
                        for tok in pre_toks:
                            wt(e, tok)
                        first[0] = False
                    if mm_tok[c % 2] > 0:
                        e.wait_ge(STn, mm_tok[c % 2])
                    if tmp2_free[c % 2] > 0:
                        e.wait_ge(STn, tmp2_free[c % 2])
                    e.tensor_copy(out=xbb[c % 2][:], in_=src_f32[:, ts(c, T)])
                    v_inc(e.tensor_tensor(out=tmp2[c % 2][:], in0=xbb[c % 2][:], in1=xbb[c % 2][:], op=ALU.mult))
                sq_done = C["v"]

                @blk.tensor
                def _(e, c=c, sq_done=sq_done):
                    e.wait_ge(SV, sq_done)
                    if c == 0:
                        wt(e, pbank_free[4])
                        wt(e, pbank_free[5])
                    e.matmul(out=pb[4][:], lhsT=col0_s[:], rhs=xbb[c % 2][:],
                             start=(c == 0), stop=(c == KC - 1), skip_group_check=True)
                    t_inc(e.matmul(out=pb[5][:], lhsT=col0_s[:], rhs=tmp2[c % 2][:],
                                   start=(c == 0), stop=(c == KC - 1), skip_group_check=True))
                mm_tok[c % 2] = C["t"]
                tmp2_free[c % 2] = C["t"]
            stats_done = C["t"]

            m_row = scr[0:1, 0:T]
            v_row = scr[0:1, T:2 * T]
            w_row = rband[0:1, :]

            @blk.vector
            def _(e):
                e.wait_ge(STn, stats_done)
                e.tensor_scalar(out=m_row, in0=pb[4][0:1, :], scalar1=1.0 / D, scalar2=None, op0=ALU.mult)
                e.tensor_scalar(out=v_row, in0=pb[5][0:1, :], scalar1=1.0 / D, scalar2=None, op0=ALU.mult)
                e.tensor_tensor(out=w_row, in0=m_row, in1=m_row, op=ALU.mult)
                v_inc(e.tensor_tensor(out=v_row, in0=v_row, in1=w_row, op=ALU.subtract))
            var_done = C["v"]
            pbank_free[4] = ("v", var_done)
            pbank_free[5] = ("v", var_done)

            @blk.scalar
            def _(e):
                e.wait_ge(SV, var_done)
                e.activation(out=w_row, in_=v_row, func=AF.Ln, bias=eps_s[0:1, :])
                s_inc(e.activation(out=w_row, in_=w_row, func=AF.Exp, scale=-0.5))
            r_done = C["s"]

            @blk.vector
            def _(e):
                e.wait_ge(SS, r_done)
                wt(e, rowb_free)
                e.tensor_tensor(out=v_row, in0=m_row, in1=w_row, op=ALU.mult)
                e.tensor_copy(out=rowb[0:1, :], in_=w_row)
                v_inc(e.tensor_copy(out=rowb[32:33, :], in_=v_row))
            rowb_done = C["v"]

            @blk.tensor
            def _(e):
                e.wait_ge(SV, rowb_done)
                wt(e, pbank_free[2])
                wt(e, pbank_free[3])
                e.matmul(out=pb[2][:], lhsT=brow_s[:], rhs=rowb[:], start=True, stop=True)
                t_inc(e.matmul(out=pb[3][:], lhsT=rsel_s[:], rhs=rowb[:], start=True, stop=True))
            bcast_done = C["t"]
            rowb_free = ("t", bcast_done)

            apply_chunks = []
            for c in range(KC):
                @blk.vector
                def _(e, c=c):
                    if c == 0:
                        e.wait_ge(STn, bcast_done)
                    e.tensor_tensor(out=xbb[0][:], in0=src_f32[:, ts(c, T)], in1=pb[2][:], op=ALU.mult)
                    v_inc(e.tensor_tensor(out=dst_bf[:, ts(c, T)], in0=xbb[0][:], in1=pb[3][:], op=ALU.subtract))
                apply_chunks.append(C["v"])
            apply_done = C["v"]
            pbank_free[2] = ("v", apply_done)
            pbank_free[3] = ("v", apply_done)
            return apply_done, apply_chunks

        def emit_qkv_slice(jslot, sl, ln1_chunks):
            nonlocal qkv_free
            for mc in range(KC):
                m = sl * KC + mc
                bank = 4 + m % 2
                @blk.tensor
                def _(e, jslot=jslot, sl=sl, mc=mc, m=m, bank=bank):
                    if mc == 0:
                        e.wait_ge(SL[jslot], slot_loaded[jslot])
                    wt(e, pbank_free[bank])
                    ins = None
                    for k in range(KC):
                        if sl == 0 and mc == 0:
                            e.wait_ge(SV, ln1_chunks[k])
                        ins = e.matmul(out=pb[bank][:],
                                       lhsT=wsl[jslot][:, k * D + mc * P: k * D + (mc + 1) * P],
                                       rhs=xln_s[:, ts(k, T)], start=(k == 0), stop=(k == KC - 1))
                    t_inc(ins)
                mmd = C["t"]

                dst = (q_s, k_s, v_s)[m // KC]
                col = (m % KC) * T
                @blk.vector
                def _(e, m=m, bank=bank, dst=dst, col=col, mmd=mmd):
                    e.wait_ge(STn, mmd)
                    if m == 0:
                        wt(e, qkv_free)
                    v_inc(e.tensor_scalar(out=dst[:, col:col + T], in0=pb[bank][:],
                                          scalar1=bqkv_s[:, m:m + 1], scalar2=None, op0=ALU.add))
                pbank_free[bank] = ("v", C["v"])
            return C["v"]

        A = {}

        def attn_V1(h):
            c_q, r_q = h // 2, (h % 2) * DK
            prev_sc = A[h - 1]["sc"] if h >= 1 else (A[NH - 1]["sc"] if (NH - 1) in A and "sc" in A[NH - 1] else 0)
            @blk.vector
            def _(e, h=h, c_q=c_q, r_q=r_q, prev_sc=prev_sc):
                if prev_sc > 0:
                    e.wait_ge(STn, prev_sc)
                e.tensor_copy(out=qrep[0:DK, :], in_=q_s[r_q:r_q + DK, ts(c_q, T)])
                e.tensor_copy(out=qrep[DK:P, :], in_=q_s[r_q:r_q + DK, ts(c_q, T)])
                ins = None
                for c in range(KC):
                    ins = e.tensor_tensor(out=tmpa[:, ts(c, T)], in0=k_s[:, ts(c, T)],
                                          in1=qrep[:], op=ALU.mult)
                v_inc(ins)
            A[h] = {"erep": {}, "tmp2": {}}
            A[h]["tmp"] = C["v"]

        def attn_sc(h, qkv_done):
            sb = 6 + (h % 2)
            @blk.tensor
            def _(e, h=h, sb=sb, qkv_done=qkv_done):
                e.wait_ge(SV, A[h]["tmp"])
                if h >= 2:
                    e.wait_ge(SS, A[h - 2]["r"])
                ins = None
                for c in range(KC):
                    ins = e.matmul(out=pb[sb][:], lhsT=bdsel_s[:, ts(c, P)],
                                   rhs=tmpa[:, ts(c, T)],
                                   start=(c == 0), stop=(c == KC - 1), skip_group_check=True)
                t_inc(ins)
            A[h]["sc"] = C["t"]

        def attn_S1(h):
            sb = 6 + (h % 2)
            @blk.scalar
            def _(e, h=h, sb=sb):
                e.wait_ge(STn, A[h]["sc"])
                if h >= 2:
                    e.wait_ge(STn, A[h - 2]["erep"][KC - 1])
                e.activation(out=rband[0:1, 0:1], in_=eps_s[0:1, :], func=AF.Ln)
                s_inc(e.activation(out=esb[h % 2][0:NH, :], in_=pb[sb][0:NH, :],
                                   func=AF.Exp, scale=0.125))
            A[h]["exp"] = C["s"]

        def attn_erep_pair(h, j):
            sb = 6 + (h % 2)
            cpair = (2 * j, 2 * j + 1)
            if j == 0:
                tok2 = ("v", A[h - 2]["ctx"]) if h >= 2 else pbank_free[2]
                tok3 = ("v", A[h - 1]["tmp2"][7]) if h >= 1 else pbank_free[3]
            else:
                tok2 = ("v", A[h]["tmp2"][2 * j - 2])
                tok3 = ("v", A[h]["tmp2"][2 * j - 1])
            @blk.tensor
            def _(e, h=h, j=j, cpair=cpair, sb=sb, tok2=tok2, tok3=tok3):
                if j == 0:
                    e.wait_ge(SS, A[h]["exp"])
                    if h == 0:
                        e.wait_ge(SV, memset_done)
                wt(e, tok2)
                wt(e, tok3)
                for c in cpair:
                    gp = c % 2
                    t_inc(e.matmul(out=pb[2 + gp][:], lhsT=gsel_s[:, ts(c, P)],
                                   rhs=esb[h % 2][:], start=True, stop=True))
                    A[h]["erep"][c] = C["t"]
                if j == 0:
                    t_inc(e.matmul(out=pb[sb][:], lhsT=densel_s[:],
                                   rhs=esb[h % 2][:], start=True, stop=True))
                    A[h]["den"] = C["t"]

        def attn_tmp2_pair(h, j):
            cpair = (2 * j, 2 * j + 1)
            if j == 0:
                utok = A[h - 1]["u"] if h >= 1 else 0
            else:
                utok = A[h]["upair"][j - 1]
            @blk.vector
            def _(e, h=h, cpair=cpair, utok=utok):
                for c in cpair:
                    gp = c % 2
                    e.wait_ge(STn, A[h]["erep"][c])
                    if utok > 0:
                        e.wait_ge(STn, utok)
                    v_inc(e.tensor_tensor(out=tmp2[gp][:], in0=pb[2 + gp][:],
                                          in1=v_s[:, ts(c, T)], op=ALU.mult))
                    A[h]["tmp2"][c] = C["v"]
            if j == 3:
                A[h]["tmp2_done"] = C["v"]

        def attn_S2a(h):
            sb = 6 + (h % 2)
            @blk.scalar
            def _(e, h=h, sb=sb):
                e.wait_ge(STn, A[h]["den"])
                if h >= 2:
                    e.wait_ge(STn, A[h - 2]["rrep"])
                e.activation(out=rband[32:33, :], in_=pb[sb][32:33, :], func=AF.Ln)
                s_inc(e.activation(out=rbandb[32:33, ts(h % 2, T)], in_=rband[32:33, :],
                                   func=AF.Exp, scale=-1.0))
            A[h]["r"] = C["s"]

        def attn_u_pair(h, j):
            ub = 4 + (h % 2)
            cpair = (2 * j, 2 * j + 1)
            @blk.tensor
            def _(e, h=h, ub=ub, cpair=cpair):
                for c in cpair:
                    gp = c % 2
                    e.wait_ge(SV, A[h]["tmp2"][c])
                    if c == 0:
                        wt(e, pbank_free[4 + h % 2])
                    t_inc(e.matmul(out=pb[ub][:], lhsT=dsel_s[:], rhs=tmp2[gp][:],
                                   start=(c == 0), stop=(c == KC - 1), skip_group_check=True))
            A[h].setdefault("upair", {})[j] = C["t"]
            if j == 3:
                A[h]["u"] = C["t"]

        def attn_rrep(h):
            if (h + 1) in A and 6 in A[h + 1]["tmp2"]:
                tok = ("v", A[h + 1]["tmp2"][6])
            elif h >= 1 and "ctx" in A[h - 1]:
                tok = ("v", A[h - 1]["ctx"])
            else:
                tok = None
            @blk.tensor
            def _(e, h=h, tok=tok):
                e.wait_ge(SS, A[h]["r"])
                wt(e, tok)
                t_inc(e.matmul(out=pb[2][:], lhsT=rsel_s[:],
                               rhs=rbandb[:, ts(h % 2, T)], start=True, stop=True))
            A[h]["rrep"] = C["t"]

        def attn_S2b(h):
            c_q, r_q = h // 2, (h % 2) * DK
            ub = 4 + (h % 2)
            @blk.scalar
            def _(e, h=h, ub=ub, c_q=c_q, r_q=r_q):
                e.wait_ge(STn, A[h]["u"])
                if h < 2:
                    wt(e, ctxb_free)
                s_inc(e.activation(out=ctxb[r_q:r_q + DK, ts(c_q, T)], in_=pb[ub][0:DK, :],
                                   func=AF.Copy))
            A[h]["evac"] = C["s"]
            pbank_free[4 + h % 2] = ("s", A[h]["evac"])

        def attn_ctx(h):
            c_q, r_q = h // 2, (h % 2) * DK
            @blk.vector
            def _(e, h=h, c_q=c_q, r_q=r_q):
                e.wait_ge(STn, A[h]["rrep"])
                e.wait_ge(SS, A[h]["evac"])
                if h < 2:
                    wt(e, ctxb_free)
                v_inc(e.tensor_tensor(out=ctxb[r_q:r_q + DK, ts(c_q, T)],
                                      in0=ctxb[r_q:r_q + DK, ts(c_q, T)],
                                      in1=pb[2][r_q:r_q + DK, :], op=ALU.mult))
            A[h]["ctx"] = C["v"]
            pbank_free[2] = ("v", A[h]["ctx"])

        def emit_w12_slice(jslot, mr):
            nonlocal hB_free, h_ready, x1ln_free
            mm_toks = {}
            silu_base = C["s"]
            for mc in range(4):
                mcg = mr * 4 + mc
                @blk.tensor
                def _(e, jslot=jslot, mc=mc, mcg=mcg, mr=mr):
                    if mc == 0:
                        e.wait_ge(SL[jslot], slot_loaded[jslot])
                        if mr > 0:
                            wt(e, x1ln_ready_box[0])
                    wt(e, pbank_free[0] if mc % 2 == 0 else pbank_free[1])
                    bA = mc % 2
                    for k in range(KC):
                        if mr == 0 and mc == 0:
                            e.wait_ge(SV, x1ln_chunks_box[0][k])
                        ins = e.matmul(out=pb[bA][:],
                                       lhsT=wsl[jslot][:, k * T + mc * P: k * T + (mc + 1) * P],
                                       rhs=x1ln_s[:, ts(k, T)], start=(k == 0), stop=(k == KC - 1))
                    t_inc(ins)
                    wt(e, pbank_free[2 + bA])
                    for k in range(KC):
                        ins = e.matmul(out=pb[2 + bA][:],
                                       lhsT=wsl[jslot][:, KC * T + k * T + mc * P: KC * T + k * T + (mc + 1) * P],
                                       rhs=x1ln_s[:, ts(k, T)], start=(k == 0), stop=(k == KC - 1))
                    t_inc(ins)
                mm_toks[mc] = (C["t"] - 1, C["t"])
                pbank_free[mc % 2] = ("s", silu_base + mc + 1)

            stt_base = C["v"]
            prev_stt = (sA_free[0], sA_free[1])
            @blk.scalar
            def _(e, mr=mr, mm_toks=mm_toks, stt_base=stt_base, prev_stt=prev_stt):
                for mc in range(4):
                    mcg = mr * 4 + mc
                    e.wait_ge(STn, mm_toks[mc][0])
                    if mc >= 2:
                        e.wait_ge(SV, stt_base + mc - 1)
                    else:
                        wt(e, prev_stt[mc])
                    s_inc(e.activation(out=sA[mc % 2][:], in_=pb[mc % 2][:], func=AF.Silu,
                                       scale=s1_s[:, mcg:mcg + 1]))
            assert C["s"] == silu_base + 4

            for mc in range(4):
                mcg = mr * 4 + mc
                @blk.vector
                def _(e, mc=mc, mcg=mcg, mm_toks=mm_toks, silu_base=silu_base):
                    e.wait_ge(STn, mm_toks[mc][1])
                    e.wait_ge(SS, silu_base + mc + 1)
                    if mcg == 0:
                        wt(e, hB_free)
                    v_inc(e.scalar_tensor_tensor(out=hB[:, ts(mcg, T)], in0=pb[2 + mc % 2][:],
                                                 scalar=s2_s[:, mcg:mcg + 1], in1=sA[mc % 2][:],
                                                 op0=ALU.mult, op1=ALU.mult))
                pbank_free[2 + mc % 2] = ("v", C["v"])
                sA_free[mc % 2] = ("v", C["v"])
            if mr == 7:
                h_ready = ("v", C["v"])
                x1ln_free = ("t", C["t"])

        def hmul_mc(jslot, kind, sidx, mc):
            rhs_buf = hB if kind == "h1" else g1B
            rhs_ready = h_ready if kind == "h1" else g1_ready
            mcg = sidx * 2 + mc
            gelu_inc = C["s"] + 1
            @blk.tensor
            def _(e, jslot=jslot, mc=mc, rhs_buf=rhs_buf, rhs_ready=rhs_ready, sidx=sidx):
                if mc == 0:
                    e.wait_ge(SL[jslot], slot_loaded[jslot])
                    wt(e, rhs_ready)
                wt(e, pbank_free[mc % 2])
                ins = None
                for k in range(HC):
                    ins = e.matmul(out=pb[mc % 2][:],
                                   lhsT=wsl[jslot][:, k * 256 + mc * P: k * 256 + (mc + 1) * P],
                                   rhs=rhs_buf[:, ts(k, T)], start=(k == 0), stop=(k == HC - 1))
                t_inc(ins)
            mm_done = C["t"]

            dst_buf = g1B if kind == "h1" else hB
            scol = sh1_s if kind == "h1" else sh2_s
            @blk.scalar
            def _(e, kind=kind, mcg=mcg, mc=mc, dst_buf=dst_buf, scol=scol, mm_done=mm_done):
                e.wait_ge(STn, mm_done)
                if mcg == 0:
                    if kind == "h1":
                        wt(e, g1_freeB)
                    else:
                        e.wait_ge(STn, hB_mm_last)
                s_inc(e.activation(out=dst_buf[:, ts(mcg, T)], in_=pb[mc % 2][:], func=AF.Gelu,
                                   scale=scol[:, mcg:mcg + 1]))
            assert C["s"] == gelu_inc
            pbank_free[mc % 2] = ("s", gelu_inc)
            return mm_done

        def finish_hmul(kind, sidx, mm_done):
            nonlocal g1_ready, g2_ready, hB_mm_last, g1_freeB
            if kind == "h1" and sidx == 15:
                g1_ready = ("s", C["s"])
                hB_mm_last = C["t"]
            if kind == "h2" and sidx == 15:
                g2_ready = ("s", C["s"])
                g1_freeB = ("t", C["t"])

        def emit_hmul_slice(jslot, kind, sidx):
            for mc in range(2):
                hmul_mc(jslot, kind, sidx, mc)
            finish_hmul(kind, sidx, None)

        def emit_w3_slice(jslot, sidx, tile):
            nonlocal hB_free, x1_free
            for mc in range(2):
                mcg = sidx * 2 + mc
                @blk.tensor
                def _(e, jslot=jslot, mc=mc, sidx=sidx):
                    if mc == 0:
                        e.wait_ge(SL[jslot], slot_loaded[jslot])
                        wt(e, g2_ready)
                    wt(e, pbank_free[mc % 2])
                    for k in range(HC):
                        ins = e.matmul(out=pb[mc % 2][:],
                                       lhsT=wsl[jslot][:, k * 256 + mc * P: k * 256 + (mc + 1) * P],
                                       rhs=hB[:, ts(k, T)], start=(k == 0), stop=(k == HC - 1))
                    t_inc(ins)
                mmd = C["t"]

                @blk.vector
                def _(e, mc=mc, mcg=mcg, mmd=mmd):
                    e.wait_ge(STn, mmd)
                    v_inc(e.scalar_tensor_tensor(out=x1_s[:, ts(mcg, T)], in0=pb[mc % 2][:],
                                                 scalar=s3_s[:, mcg:mcg + 1], in1=x1_s[:, ts(mcg, T)],
                                                 op0=ALU.mult, op1=ALU.add))
                pbank_free[mc % 2] = ("v", C["v"])
            if sidx == 3:
                hB_free = ("t", C["t"])

        def emit_out_store(tile):
            nonlocal out_stored, x1_free
            @blk.sync
            def _(e, tile=tile):
                e.wait_ge(SV, C["v"])
                e.dma_start(out=outT[:, ts(tile, T)].rearrange("(k p) t -> p k t", p=P),
                            in_=x1_s[:].rearrange("p (k t) -> p k t", k=KC)).then_inc(SO, 16)
            C["o"] += 16
            out_stored = ("o", C["o"])
            x1_free = ("o", C["o"])

        def emit_wout(ctx_pair_toks, tile):
            nonlocal ctxb_free, x_free
            w_loaded = slot_loaded[wout_slot[0]]
            for m in range(KC):
                bank = m % 2
                @blk.tensor
                def _(e, m=m, bank=bank, w_loaded=w_loaded):
                    if m == 0:
                        e.wait_ge(SL[wout_slot[0]], w_loaded)
                    wt(e, pbank_free[bank])
                    ins = None
                    for k in range(KC):
                        if m == 0:
                            e.wait_ge(SV, ctx_pair_toks[k])
                        ins = e.matmul(out=pb[bank][:],
                                       lhsT=wsl[wout_slot[0]][:, k * D + m * P: k * D + (m + 1) * P],
                                       rhs=ctxb[:, ts(k, T)], start=(k == 0), stop=(k == KC - 1))
                    t_inc(ins)
                wmm = C["t"]
                if m == KC - 1:
                    ctxb_free = ("t", wmm)

                @blk.vector
                def _(e, m=m, bank=bank, wmm=wmm):
                    e.wait_ge(STn, wmm)
                    if m == 0:
                        wt(e, x1_free)
                    v_inc(e.scalar_tensor_tensor(out=x1_s[:, ts(m, T)], in0=pb[bank][:],
                                                 scalar=bout_s[:, m:m + 1], in1=xbuf[:, ts(m, T)],
                                                 op0=ALU.add, op1=ALU.add))
                pbank_free[bank] = ("v", C["v"])
            x_free = ("v", C["v"])
            return C["v"]

        x_loaded = XCNT[0]

        wout_slot = [0]
        w12_pre = [None, None]

        def dbg_barrier():
            @blk.tensor
            def _(e):
                e.wait_ge(SV, C["v"]); e.wait_ge(SS, C["s"]); e.wait_ge(SI, C["i"]); e.wait_ge(SO, C["o"])
                t_inc(e.engine_nop() if hasattr(e, "engine_nop") else e.nop())
            tcnt = C["t"]
            @blk.vector
            def _(e, tcnt=tcnt):
                e.wait_ge(STn, tcnt)
                v_inc(e.engine_nop())
            vcnt = C["v"]
            @blk.scalar
            def _(e, tcnt=tcnt, vcnt=vcnt):
                e.wait_ge(STn, tcnt); e.wait_ge(SV, vcnt)
                s_inc(e.nop())
            scnt = C["s"]
            @blk.sync
            def _(e, tcnt=tcnt, vcnt=vcnt, scnt=scnt):
                e.wait_ge(STn, tcnt); e.wait_ge(SV, vcnt); e.wait_ge(SS, scnt)

        def do_slice(kind, pos, tile):
            if kind == "w12" and pos in (0, 1) and w12_pre[pos] is not None:
                jj = w12_pre[pos]
                w12_pre[pos] = None
            else:
                jj = load_slice(kind, pos)
            if kind == "w12":
                emit_w12_slice(jj, pos)
            elif kind in ("h1", "h2"):
                emit_hmul_slice(jj, kind, pos)
            else:
                emit_w3_slice(jj, pos, tile)
            slot_free[jj] = ("t", C["t"])

        for i in range(NT):
            ffn_slices = []
            if i > 0:
                ffn_slices = [("w12", mr) for mr in range(8)] + \
                             [("h1", s_) for s_ in range(16)] + \
                             [("h2", s_) for s_ in range(16)]

            pre = [("x", x_loaded), xln_free]
            ln1_done, ln1_chunks = emit_ln(xbuf, xln_s, pre)

            qkv_done = 0
            fidx = 0
            for sl in range(3):
                if fidx < 8 and i > 0:
                    kind, pos = ffn_slices[fidx]; fidx += 1
                    do_slice(kind, pos, i - 1)
                jq = load_slice("qkv", sl, direct=(i == 0))
                qkv_done = emit_qkv_slice(jq, sl, ln1_chunks)
                slot_free[jq] = ("t", C["t"])
            xln_free = ("t", C["t"])
            if i == 0:
                emit_casts()

            while fidx < 8 and i > 0:
                kind, pos = ffn_slices[fidx]; fidx += 1
                do_slice(kind, pos, i - 1)

            pend = []

            def queue_ffn():
                nonlocal fidx
                if i > 0 and fidx < len(ffn_slices):
                    kind, pos = ffn_slices[fidx]; fidx += 1
                    jj = load_slice(kind, pos)
                    pend.append([jj, kind, pos, 0])
                    pend.append([jj, kind, pos, 1])

            def run_ffn_mc():
                if pend:
                    jj, kind, pos, mc = pend.pop(0)
                    mm = hmul_mc(jj, kind, pos, mc)
                    if mc == 1:
                        finish_hmul(kind, pos, mm)
                        slot_free[jj] = ("t", C["t"])

            for h in range(NH + 2):
                if h < NH:
                    attn_V1(h)
                    queue_ffn()
                    run_ffn_mc()
                    attn_sc(h, qkv_done)
                    attn_S1(h)
                if 1 <= h <= NH:
                    hh = h - 1
                    attn_erep_pair(hh, 0)
                    attn_tmp2_pair(hh, 0)
                    run_ffn_mc()
                    attn_erep_pair(hh, 1)
                    attn_u_pair(hh, 0)
                    attn_tmp2_pair(hh, 1)
                    queue_ffn()
                    run_ffn_mc()
                    attn_erep_pair(hh, 2)
                    attn_u_pair(hh, 1)
                    attn_tmp2_pair(hh, 2)
                    attn_S2a(hh)
                    run_ffn_mc()
                    attn_erep_pair(hh, 3)
                    attn_u_pair(hh, 2)
                    attn_tmp2_pair(hh, 3)
                    run_ffn_mc()
                    attn_u_pair(hh, 3)
                if 2 <= h <= NH + 1:
                    attn_rrep(h - 2)
                    attn_S2b(h - 2)
                    attn_ctx(h - 2)
            ctx_done = A[NH - 1]["ctx"]
            qkv_free = ("v", ctx_done)
            pbank_free[2] = ("v", A[NH - 1]["ctx"])
            pbank_free[3] = ("v", A[NH - 1]["tmp2_done"])
            while pend:
                run_ffn_mc()
            while i > 0 and fidx < len(ffn_slices):
                kind, pos = ffn_slices[fidx]; fidx += 1
                do_slice(kind, pos, i - 1)

            if i > 0:
                for s_ in range(4):
                    do_slice("w3", s_, i - 1)
            jw = load_slice("wout", 0)
            wout_slot[0] = jw
            wout_loaded = slot_loaded[jw]
            w12_pre[0] = load_slice("w12", 0)
            if i > 0:
                emit_out_store(i - 1)

            x1_done = emit_wout([A[2 * k_ + 1]["ctx"] for k_ in range(KC)], i)
            slot_free[wout_slot[0]] = ("t", C["t"])

            if i + 1 < NT:
                @blk.sync
                def _(e, i=i):
                    wt(e, x_free)
                    e.dma_start(out=xbuf[:].rearrange("p (k t) -> p k t", k=KC),
                                in_=xT[:, ts(i + 1, T)].rearrange("(k p) t -> p k t", p=P)).then_inc(SX, 16)
                XCNT[0] += 16
                x_loaded = XCNT[0]
            w12_pre[1] = load_slice("w12", 1)

            ln2_done, ln2_chunks = emit_ln(x1_s, x1ln_s, [x1ln_free, ("v", x1_done)])
            x1ln_ready_box[0] = ("v", ln2_done)
            x1ln_chunks_box[0] = ln2_chunks
            if DEBUG_BARRIER:
                dbg_barrier()

        @blk.tensor
        def _(e, ln2_done=ln2_done):
            e.wait_ge(SV, ln2_done)
        ffn_slices = [("w12", mr) for mr in range(8)] + \
                     [("h1", s_) for s_ in range(16)] + \
                     [("h2", s_) for s_ in range(16)]
        for kind, pos in ffn_slices:
            do_slice(kind, pos, NT - 1)
        for s_ in range(4):
            do_slice("w3", s_, NT - 1)
        emit_out_store(NT - 1)

        @blk.sync
        def _(e):
            e.wait_ge(SO, C["o"])

    return nc


_PROG_CACHE = {}

def _get_prog():
    if "nc" not in _PROG_CACHE:
        _PROG_CACHE["nc"] = build_program()
    return _PROG_CACHE["nc"]


def _consts():
    bf = ml_dtypes.bfloat16
    dsel = np.zeros((P, P), bf)
    for p in range(P):
        dsel[p, p % DK] = 1
    bdsel = np.zeros((P, KC * P), bf)
    gsel = np.zeros((P, KC * P), bf)
    for c in range(KC):
        for p in range(P):
            gg = 2 * c + p // DK
            bdsel[p, c * P + gg] = 1
            gsel[gg, c * P + p] = 1
    densel = np.zeros((P, P), bf); densel[0:NH, 32] = 1
    rsel = np.zeros((P, P), bf); rsel[32, :] = 1
    brow = np.zeros((P, P), bf); brow[0, :] = 1
    col0 = np.zeros((P, P), bf); col0[:, 0] = 1
    eps = np.full((P, 1), EPS, np.float32)
    return dsel, bdsel, gsel, densel, rsel, brow, col0, eps


def kernel(L0_attn_x, L0_attn_W_qkv, L0_attn_b_qkv, L0_attn_W_out, L0_attn_b_out,
           L0_ffn_w1_w, L0_ffn_w1_s, L0_ffn_w2_w, L0_ffn_w2_s,
           L0_ffn_h1_w, L0_ffn_h1_s, L0_ffn_h2_w, L0_ffn_h2_s,
           L0_ffn_w3_w, L0_ffn_w3_s):
    x = np.asarray(L0_attn_x)
    B = x.shape[0]
    wq = np.asarray(L0_attn_W_qkv)
    wq3 = wq.reshape(D, NH, 3, DK)
    wqkv = np.ascontiguousarray(wq3.transpose(0, 2, 1, 3).reshape(D, 3 * D))
    bq3 = np.asarray(L0_attn_b_qkv).reshape(NH, 3, DK)
    bqkv = np.ascontiguousarray(bq3.transpose(1, 0, 2).reshape(3 * D // P, P).T)
    bout = np.ascontiguousarray(np.asarray(L0_attn_b_out).reshape(D // P, P).T)
    w1T = np.ascontiguousarray(np.asarray(L0_ffn_w1_w).T)
    w2T = np.ascontiguousarray(np.asarray(L0_ffn_w2_w).T)
    h1T = np.ascontiguousarray(np.asarray(L0_ffn_h1_w).T)
    h2T = np.ascontiguousarray(np.asarray(L0_ffn_h2_w).T)
    w3T = np.ascontiguousarray(np.asarray(L0_ffn_w3_w).T)
    s1 = np.ascontiguousarray(np.asarray(L0_ffn_w1_s).reshape(HC, P).T)
    s2 = np.ascontiguousarray(np.asarray(L0_ffn_w2_s).reshape(HC, P).T)
    sh1 = np.ascontiguousarray(np.asarray(L0_ffn_h1_s).reshape(HC, P).T)
    sh2 = np.ascontiguousarray(np.asarray(L0_ffn_h2_s).reshape(HC, P).T)
    s3 = np.ascontiguousarray(np.asarray(L0_ffn_w3_s).reshape(D // P, P).T)
    dsel, bdsel, gsel, densel, rsel, brow, col0, eps = _consts()
    cb16 = np.ascontiguousarray(np.concatenate(
        [dsel, bdsel, gsel, densel, rsel, brow, col0], axis=1))
    cb32 = np.ascontiguousarray(np.concatenate(
        [eps, bqkv, bout, s1, s2, sh1, sh2, s3], axis=1))

    shared = dict(wqkv=wqkv, wout=np.ascontiguousarray(np.asarray(L0_attn_W_out)),
                  w1T=w1T, w2T=w2T, h1T=h1T, h2T=h2T, w3T=w3T,
                  cb16=cb16, cb32=cb32)

    in_maps = []
    for c in range(B):
        m = dict(shared)
        m["xT"] = np.ascontiguousarray(x[c].T)
        in_maps.append(m)

    nc = _get_prog()
    res = run_bass_kernel_spmd(nc, in_maps, core_ids=list(range(N_CORES)))
    out = np.stack([np.asarray(r["outT"], dtype=np.float32).T for r in res.results], axis=0)
    return out


# version 21
# speedup vs baseline: 1.1044x; 1.0635x over previous
import numpy as np
import ml_dtypes
from contextlib import ExitStack

import concourse.bass as bass
import concourse.mybir as mybir
from concourse.bass_utils import run_bass_kernel_spmd

F32 = mybir.dt.float32
BF16 = mybir.dt.bfloat16
AF = mybir.ActivationFunctionType
ALU = mybir.AluOpType
ts = bass.ts

P = 128
D = 1024
DH = 4096
NH, DK = 16, 64
S = 4096
T = 512
NT = S // T
KC = D // P
HC = DH // P
N_CORES = 8
EPS = 1e-5
DEBUG_BARRIER = False
SLOT = 8192


def build_program():
    nc = bass.Bass("TRN2", target_bir_lowering=False)

    xT = nc.dram_tensor("xT", [D, S], F32, kind="ExternalInput").ap()
    wqkv = nc.dram_tensor("wqkv", [D, 3 * D], F32, kind="ExternalInput").ap()
    wout = nc.dram_tensor("wout", [D, D], F32, kind="ExternalInput").ap()
    w1T = nc.dram_tensor("w1T", [D, DH], F32, kind="ExternalInput").ap()
    w2T = nc.dram_tensor("w2T", [D, DH], F32, kind="ExternalInput").ap()
    h1T = nc.dram_tensor("h1T", [DH, DH], F32, kind="ExternalInput").ap()
    h2T = nc.dram_tensor("h2T", [DH, DH], F32, kind="ExternalInput").ap()
    w3T = nc.dram_tensor("w3T", [DH, D], F32, kind="ExternalInput").ap()
    cb16 = nc.dram_tensor("cb16", [P, 2688], BF16, kind="ExternalInput").ap()
    cb32 = nc.dram_tensor("cb32", [P, 169], F32, kind="ExternalInput").ap()

    outT = nc.dram_tensor("outT", [D, S], F32, kind="ExternalOutput").ap()

    wbqkv_d = nc.dram_tensor("wbqkv_d", [D, 3 * D], BF16, kind="Internal").ap()
    wbout_d = nc.dram_tensor("wbout_d", [D, D], BF16, kind="Internal").ap()
    wb1_d = nc.dram_tensor("wb1_d", [D, DH], BF16, kind="Internal").ap()
    wb2_d = nc.dram_tensor("wb2_d", [D, DH], BF16, kind="Internal").ap()
    hb1_d = nc.dram_tensor("hb1_d", [DH, DH], BF16, kind="Internal").ap()
    hb2_d = nc.dram_tensor("hb2_d", [DH, DH], BF16, kind="Internal").ap()
    wb3_d = nc.dram_tensor("wb3_d", [DH, D], BF16, kind="Internal").ap()

    with ExitStack() as g:
        E = g.enter_context
        cb16_s = E(nc.sbuf_tensor("cb16_s", [P, 2688], BF16))
        cb32_s = E(nc.sbuf_tensor("cb32_s", [P, 169], F32))
        dsel_s = cb16_s[:, 0:P]
        bdsel_s = cb16_s[:, P:P + KC * P]
        gsel_s = cb16_s[:, P + KC * P:P + 2 * KC * P]
        densel_s = cb16_s[:, 2176:2304]
        rsel_s = cb16_s[:, 2304:2432]
        brow_s = cb16_s[:, 2432:2560]
        col0_s = cb16_s[:, 2560:2688]
        eps_s = cb32_s[:, 0:1]
        bqkv_s = cb32_s[:, 1:25]
        bout_s = cb32_s[:, 25:33]
        s1_s = cb32_s[:, 33:65]
        s2_s = cb32_s[:, 65:97]
        sh1_s = cb32_s[:, 97:129]
        sh2_s = cb32_s[:, 129:161]
        s3_s = cb32_s[:, 161:169]

        wsl = [E(nc.sbuf_tensor(f"wsl{j}", [P, SLOT], BF16)) for j in range(2)]
        xbuf = E(nc.sbuf_tensor("xbuf", [P, KC * T], F32))
        xln_s = E(nc.sbuf_tensor("xln_s", [P, KC * T], BF16))
        x1ln_s = E(nc.sbuf_tensor("x1ln_s", [P, KC * T], BF16))
        q_s = E(nc.sbuf_tensor("q_s", [P, KC * T], BF16))
        k_s = E(nc.sbuf_tensor("k_s", [P, KC * T], BF16))
        v_s = E(nc.sbuf_tensor("v_s", [P, KC * T], BF16))
        qrep = E(nc.sbuf_tensor("qrep", [P, T], BF16))
        tmpa = E(nc.sbuf_tensor("tmpa", [P, KC * T], BF16))
        esb = [E(nc.sbuf_tensor(f"esb{j}", [P, T], BF16)) for j in range(2)]
        tmp2 = [E(nc.sbuf_tensor(f"tmp2_{j}", [P, T], BF16)) for j in range(2)]
        rband = E(nc.sbuf_tensor("rband", [33, T], F32))
        rbandb = E(nc.sbuf_tensor("rbandb", [P, 2 * T], BF16))
        scr = E(nc.sbuf_tensor("scr", [1, 2 * T], F32))
        rowb = E(nc.sbuf_tensor("rowb", [P, T], BF16))

        sA = [E(nc.sbuf_tensor(f"sA{j}", [P, T], BF16)) for j in range(2)]
        xbb = [E(nc.sbuf_tensor("xbb0", [P, T], BF16)), qrep]
        ctxb = E(nc.sbuf_tensor("ctxb", [P, KC * T], BF16))
        x1_s = E(nc.sbuf_tensor("x1_s", [P, KC * T], F32))
        hB = E(nc.sbuf_tensor("hB", [P, HC * T], BF16))
        g1B = E(nc.sbuf_tensor("g1B", [P, HC * T], BF16))

        pb = [E(nc.psum_tensor(f"pb{i}", [P, T], F32)) for i in range(8)]

        SW = E(nc.semaphore("SW"))
        SI = E(nc.semaphore("SI"))
        SO = E(nc.semaphore("SO"))
        STn = E(nc.semaphore("STn"))
        SV = E(nc.semaphore("SV"))
        SS = E(nc.semaphore("SS"))
        WSEM = {name: E(nc.semaphore(f"W_{name}")) for name in
                ("qkv", "wout", "w1", "w2", "h1", "h2", "w3")}
        WCNT = {name: 0 for name in WSEM}
        SL = [E(nc.semaphore(f"SL{j}")) for j in range(2)]
        SX = E(nc.semaphore("SX"))
        blk = E(nc.Block())
        C = dict(w=0, i=0, o=0, t=0, v=0, s=0)
        SEMS = {"t": STn, "v": SV, "s": SS, "i": SI, "o": SO, "w": SW, "x": None}

        def t_inc(instr):
            C["t"] += 1
            return instr.then_inc(STn, 1)

        def v_inc(instr):
            C["v"] += 1
            return instr.then_inc(SV, 1)

        def s_inc(instr):
            C["s"] += 1
            return instr.then_inc(SS, 1)

        def wt(e, tok):
            if tok is None:
                return
            if tok[1] > 0:
                sem = SX if tok[0] == "x" else SEMS[tok[0]]
                e.wait_ge(sem, tok[1])

        XCNT = [0]
        @blk.sync
        def _(e):
            e.dma_start(out=xbuf[:].rearrange("p (k t) -> p k t", k=KC),
                        in_=xT[:, 0:T].rearrange("(k p) t -> p k t", p=P)).then_inc(SX, 16)
            for dst, src in [(cb16_s[:], cb16), (cb32_s[:], cb32)]:
                e.dma_start(out=dst, in_=src).then_inc(SI, 16)
                C["i"] += 16
        XCNT[0] += 16
        consts_loaded = C["i"]

        @blk.tensor
        def _(e):
            e.wait_ge(SI, consts_loaded)
        @blk.scalar
        def _(e):
            e.wait_ge(SI, consts_loaded)

        @blk.vector
        def _(e):
            e.wait_ge(SI, consts_loaded)
            e.memset(esb[0][:], 0.0)
            e.memset(esb[1][:], 0.0)
            e.memset(rbandb[:], 0.0)
            v_inc(e.memset(rowb[:], 0.0))
        memset_done = C["v"]

        def emit_casts():
            @blk.gpsimd
            def _(e):
                for name, src_, dst, nrows in [("wout", wout, wbout_d, D),
                                               ("w1", w1T, wb1_d, D), ("w2", w2T, wb2_d, D),
                                               ("qkv", wqkv, wbqkv_d, D)]:
                    for r0 in range(0, nrows, 256):
                        e.dma_start(out=dst[r0:r0 + 256, :], in_=src_[r0:r0 + 256, :]).then_inc(WSEM[name], 16)
                        WCNT[name] += 16
                    e.wait_ge(WSEM[name], WCNT[name])
                for name, src_, dst, cstep in [("h1", h1T, hb1_d, 512), ("h2", h2T, hb2_d, 512),
                                               ("w3", w3T, wb3_d, 256)]:
                    ncols = src_.shape[1]
                    for c0 in range(0, ncols, cstep):
                        if c0 > 0:
                            e.wait_ge(WSEM[name], WCNT[name])
                        e.dma_start(out=dst[:, c0:c0 + cstep], in_=src_[:, c0:c0 + cstep]).then_inc(WSEM[name], 16)
                        WCNT[name] += 16
                    if name != "w3":
                        e.wait_ge(WSEM[name], WCNT[name])

        x_loaded = 0
        x_free = None
        x1_free = None
        x1_store = None
        xln_free = None
        x1ln_free = None
        qkv_free = None
        ctxb_free = None
        rowb_free = None
        pbank_free = [None] * 8
        slot_free = [None, None]
        slot_loaded = [0, 0]
        slot_idx = [0]
        tmpa_free = [0, 0]
        esb_free = [0, 0]
        psS_free = [None, None]
        rbandb_free = [0, 0]
        tmp2_free = [0, 0]
        sA_free = [None, None]
        hB_free = None
        g1_freeB = None
        h_ready = None
        g1_ready = None
        g2_ready = None
        hB_mm_last = 0
        out_stored = None
        x1ln_ready_box = [None]
        x1ln_chunks_box = [None]

        SLCNT = [0, 0]

        def load_slice(kind, pos, direct=False):
            j = slot_idx[0] % 2
            slot_idx[0] += 1
            if direct:
                @blk.gpsimd
                def _(e, pos=pos, j=j):
                    wt(e, slot_free[j])
                    e.dma_start(out=wsl[j][:].rearrange("p (k m) -> p k m", k=KC),
                                in_=wqkv[:, ts(pos, D)].rearrange("(k p) m -> p k m", p=P)).then_inc(SL[j], 16)
                SLCNT[j] += 16
                slot_loaded[j] = SLCNT[j]
                return j
            @blk.sync
            def _(e, kind=kind, pos=pos, j=j):
                wt(e, slot_free[j])
                if kind == "w12":
                    e.wait_ge(WSEM["w1"], WCNT["w1"])
                    e.wait_ge(WSEM["w2"], WCNT["w2"])
                elif kind in ("h1", "h2"):
                    e.wait_ge(WSEM[kind], 16 * (pos // 2 + 1))
                elif kind == "w3":
                    e.wait_ge(WSEM["w3"], 16 * (pos + 1))
                else:
                    wn = {"qkv": "qkv", "wout": "wout"}[kind]
                    e.wait_ge(WSEM[wn], WCNT[wn])
                if kind == "qkv":
                    e.dma_start(out=wsl[j][:].rearrange("p (k m) -> p k m", k=KC),
                                in_=wbqkv_d[:, ts(pos, D)].rearrange("(k p) m -> p k m", p=P)).then_inc(SL[j], 16)
                    SLCNT[j] += 16
                elif kind == "wout":
                    for m_ in range(KC):
                        if m_ > 0:
                            e.wait_ge(SL[j], SLCNT[j] + 16 * m_)
                        e.dma_start(out=wsl[j][:, m_ * D:(m_ + 1) * D].rearrange("p (k c) -> p k c", k=KC),
                                    in_=wbout_d[:, m_ * P:(m_ + 1) * P].rearrange("(k p) c -> p k c", p=P)
                                    ).then_inc(SL[j], 16)
                    SLCNT[j] += 128
                elif kind == "w12":
                    e.dma_start(out=wsl[j][:, 0:KC * T].rearrange("p (k m) -> p k m", k=KC),
                                in_=wb1_d[:, ts(pos, T)].rearrange("(k p) m -> p k m", p=P)).then_inc(SL[j], 16)
                    e.dma_start(out=wsl[j][:, KC * T:2 * KC * T].rearrange("p (k m) -> p k m", k=KC),
                                in_=wb2_d[:, ts(pos, T)].rearrange("(k p) m -> p k m", p=P)).then_inc(SL[j], 16)
                    SLCNT[j] += 32
                else:
                    src = {"h1": hb1_d, "h2": hb2_d, "w3": wb3_d}[kind]
                    e.dma_start(out=wsl[j][:].rearrange("p (k m) -> p k m", k=HC),
                                in_=src[:, ts(pos, 256)].rearrange("(k p) m -> p k m", p=P)).then_inc(SL[j], 16)
                    SLCNT[j] += 16
            slot_loaded[j] = SLCNT[j]
            return j

        def emit_ln(src_f32, dst_bf, pre_toks):
            nonlocal rowb_free
            mm_tok = [0, 0]
            first_s = [True]
            for c in range(KC):
                @blk.scalar
                def _(e, c=c):
                    if first_s[0]:
                        for tok in pre_toks:
                            wt(e, tok)
                        first_s[0] = False
                    if mm_tok[c % 2] > 0:
                        e.wait_ge(STn, mm_tok[c % 2])
                    s_inc(e.activation(out=xbb[c % 2][:], in_=src_f32[:, ts(c, T)], func=AF.Copy))
                cp_done = C["s"]
                @blk.vector
                def _(e, c=c, cp_done=cp_done):
                    e.wait_ge(SS, cp_done)
                    if tmp2_free[c % 2] > 0:
                        e.wait_ge(STn, tmp2_free[c % 2])
                    v_inc(e.tensor_tensor(out=tmp2[c % 2][:], in0=xbb[c % 2][:], in1=xbb[c % 2][:], op=ALU.mult))
                sq_done = C["v"]

                @blk.tensor
                def _(e, c=c, sq_done=sq_done):
                    e.wait_ge(SV, sq_done)
                    if c == 0:
                        wt(e, pbank_free[4])
                        wt(e, pbank_free[5])
                    e.matmul(out=pb[4][:], lhsT=col0_s[:], rhs=xbb[c % 2][:],
                             start=(c == 0), stop=(c == KC - 1), skip_group_check=True)
                    t_inc(e.matmul(out=pb[5][:], lhsT=col0_s[:], rhs=tmp2[c % 2][:],
                                   start=(c == 0), stop=(c == KC - 1), skip_group_check=True))
                mm_tok[c % 2] = C["t"]
                tmp2_free[c % 2] = C["t"]
            stats_done = C["t"]

            m_row = scr[0:1, 0:T]
            v_row = scr[0:1, T:2 * T]
            w_row = rband[0:1, :]

            @blk.vector
            def _(e):
                e.wait_ge(STn, stats_done)
                e.tensor_scalar(out=m_row, in0=pb[4][0:1, :], scalar1=1.0 / D, scalar2=None, op0=ALU.mult)
                e.tensor_scalar(out=v_row, in0=pb[5][0:1, :], scalar1=1.0 / D, scalar2=None, op0=ALU.mult)
                e.tensor_tensor(out=w_row, in0=m_row, in1=m_row, op=ALU.mult)
                v_inc(e.tensor_tensor(out=v_row, in0=v_row, in1=w_row, op=ALU.subtract))
            var_done = C["v"]
            pbank_free[4] = ("v", var_done)
            pbank_free[5] = ("v", var_done)

            @blk.scalar
            def _(e):
                e.wait_ge(SV, var_done)
                e.activation(out=w_row, in_=v_row, func=AF.Ln, bias=eps_s[0:1, :])
                s_inc(e.activation(out=w_row, in_=w_row, func=AF.Exp, scale=-0.5))
            r_done = C["s"]

            @blk.vector
            def _(e):
                e.wait_ge(SS, r_done)
                wt(e, rowb_free)
                e.tensor_tensor(out=v_row, in0=m_row, in1=w_row, op=ALU.mult)
                e.tensor_copy(out=rowb[0:1, :], in_=w_row)
                v_inc(e.tensor_copy(out=rowb[32:33, :], in_=v_row))
            rowb_done = C["v"]

            @blk.tensor
            def _(e):
                e.wait_ge(SV, rowb_done)
                wt(e, pbank_free[2])
                wt(e, pbank_free[3])
                e.matmul(out=pb[2][:], lhsT=brow_s[:], rhs=rowb[:], start=True, stop=True)
                t_inc(e.matmul(out=pb[3][:], lhsT=rsel_s[:], rhs=rowb[:], start=True, stop=True))
            bcast_done = C["t"]
            rowb_free = ("t", bcast_done)

            apply_chunks = []
            for c in range(KC):
                @blk.vector
                def _(e, c=c):
                    if c == 0:
                        e.wait_ge(STn, bcast_done)
                    e.tensor_tensor(out=xbb[0][:], in0=src_f32[:, ts(c, T)], in1=pb[2][:], op=ALU.mult)
                    v_inc(e.tensor_tensor(out=dst_bf[:, ts(c, T)], in0=xbb[0][:], in1=pb[3][:], op=ALU.subtract))
                apply_chunks.append(C["v"])
            apply_done = C["v"]
            pbank_free[2] = ("v", apply_done)
            pbank_free[3] = ("v", apply_done)
            return apply_done, apply_chunks

        def emit_qkv_slice(jslot, sl, ln1_chunks):
            nonlocal qkv_free
            for mc in range(KC):
                m = sl * KC + mc
                bank = 4 + m % 2
                @blk.tensor
                def _(e, jslot=jslot, sl=sl, mc=mc, m=m, bank=bank):
                    if mc == 0:
                        e.wait_ge(SL[jslot], slot_loaded[jslot])
                    wt(e, pbank_free[bank])
                    ins = None
                    for k in range(KC):
                        if sl == 0 and mc == 0:
                            e.wait_ge(SV, ln1_chunks[k])
                        ins = e.matmul(out=pb[bank][:],
                                       lhsT=wsl[jslot][:, k * D + mc * P: k * D + (mc + 1) * P],
                                       rhs=xln_s[:, ts(k, T)], start=(k == 0), stop=(k == KC - 1))
                    t_inc(ins)
                mmd = C["t"]

                dst = (q_s, k_s, v_s)[m // KC]
                col = (m % KC) * T
                @blk.vector
                def _(e, m=m, bank=bank, dst=dst, col=col, mmd=mmd):
                    e.wait_ge(STn, mmd)
                    if m == 0:
                        wt(e, qkv_free)
                    v_inc(e.tensor_scalar(out=dst[:, col:col + T], in0=pb[bank][:],
                                          scalar1=bqkv_s[:, m:m + 1], scalar2=None, op0=ALU.add))
                pbank_free[bank] = ("v", C["v"])
            return C["v"]

        A = {}

        def attn_V1(h):
            c_q, r_q = h // 2, (h % 2) * DK
            prev_sc = A[h - 1]["sc"] if h >= 1 else (A[NH - 1]["sc"] if (NH - 1) in A and "sc" in A[NH - 1] else 0)
            @blk.vector
            def _(e, h=h, c_q=c_q, r_q=r_q, prev_sc=prev_sc):
                if prev_sc > 0:
                    e.wait_ge(STn, prev_sc)
                e.tensor_copy(out=qrep[0:DK, :], in_=q_s[r_q:r_q + DK, ts(c_q, T)])
                e.tensor_copy(out=qrep[DK:P, :], in_=q_s[r_q:r_q + DK, ts(c_q, T)])
                ins = None
                for c in range(KC):
                    ins = e.tensor_tensor(out=tmpa[:, ts(c, T)], in0=k_s[:, ts(c, T)],
                                          in1=qrep[:], op=ALU.mult)
                v_inc(ins)
            A[h] = {"erep": {}, "tmp2": {}}
            A[h]["tmp"] = C["v"]

        def attn_sc(h, qkv_done):
            sb = 6 + (h % 2)
            @blk.tensor
            def _(e, h=h, sb=sb, qkv_done=qkv_done):
                e.wait_ge(SV, A[h]["tmp"])
                if h >= 2:
                    e.wait_ge(SS, A[h - 2]["r"])
                ins = None
                for c in range(KC):
                    ins = e.matmul(out=pb[sb][:], lhsT=bdsel_s[:, ts(c, P)],
                                   rhs=tmpa[:, ts(c, T)],
                                   start=(c == 0), stop=(c == KC - 1), skip_group_check=True)
                t_inc(ins)
            A[h]["sc"] = C["t"]

        def attn_S1(h):
            sb = 6 + (h % 2)
            @blk.scalar
            def _(e, h=h, sb=sb):
                e.wait_ge(STn, A[h]["sc"])
                if h >= 2:
                    e.wait_ge(STn, A[h - 2]["erep"][KC - 1])
                e.activation(out=rband[0:1, 0:1], in_=eps_s[0:1, :], func=AF.Ln)
                s_inc(e.activation(out=esb[h % 2][0:NH, :], in_=pb[sb][0:NH, :],
                                   func=AF.Exp, scale=0.125))
            A[h]["exp"] = C["s"]

        def attn_erep_pair(h, j):
            sb = 6 + (h % 2)
            cpair = (2 * j, 2 * j + 1)
            if j == 0:
                tok2 = ("v", A[h - 2]["ctx"]) if h >= 2 else pbank_free[2]
                tok3 = ("v", A[h - 1]["tmp2"][7]) if h >= 1 else pbank_free[3]
            else:
                tok2 = ("v", A[h]["tmp2"][2 * j - 2])
                tok3 = ("v", A[h]["tmp2"][2 * j - 1])
            @blk.tensor
            def _(e, h=h, j=j, cpair=cpair, sb=sb, tok2=tok2, tok3=tok3):
                if j == 0:
                    e.wait_ge(SS, A[h]["exp"])
                    if h == 0:
                        e.wait_ge(SV, memset_done)
                wt(e, tok2)
                wt(e, tok3)
                for c in cpair:
                    gp = c % 2
                    t_inc(e.matmul(out=pb[2 + gp][:], lhsT=gsel_s[:, ts(c, P)],
                                   rhs=esb[h % 2][:], start=True, stop=True))
                    A[h]["erep"][c] = C["t"]
                if j == 0:
                    t_inc(e.matmul(out=pb[sb][:], lhsT=densel_s[:],
                                   rhs=esb[h % 2][:], start=True, stop=True))
                    A[h]["den"] = C["t"]

        def attn_tmp2_pair(h, j):
            cpair = (2 * j, 2 * j + 1)
            if j == 0:
                utok = A[h - 1]["u"] if h >= 1 else 0
            else:
                utok = A[h]["upair"][j - 1]
            @blk.vector
            def _(e, h=h, cpair=cpair, utok=utok):
                for c in cpair:
                    gp = c % 2
                    e.wait_ge(STn, A[h]["erep"][c])
                    if utok > 0:
                        e.wait_ge(STn, utok)
                    v_inc(e.tensor_tensor(out=tmp2[gp][:], in0=pb[2 + gp][:],
                                          in1=v_s[:, ts(c, T)], op=ALU.mult))
                    A[h]["tmp2"][c] = C["v"]
            if j == 3:
                A[h]["tmp2_done"] = C["v"]

        def attn_S2a(h):
            sb = 6 + (h % 2)
            @blk.scalar
            def _(e, h=h, sb=sb):
                e.wait_ge(STn, A[h]["den"])
                if h >= 2:
                    e.wait_ge(STn, A[h - 2]["rrep"])
                e.activation(out=rband[32:33, :], in_=pb[sb][32:33, :], func=AF.Ln)
                s_inc(e.activation(out=rbandb[32:33, ts(h % 2, T)], in_=rband[32:33, :],
                                   func=AF.Exp, scale=-1.0))
            A[h]["r"] = C["s"]

        def attn_u_pair(h, j):
            ub = 4 + (h % 2)
            cpair = (2 * j, 2 * j + 1)
            @blk.tensor
            def _(e, h=h, ub=ub, cpair=cpair):
                for c in cpair:
                    gp = c % 2
                    e.wait_ge(SV, A[h]["tmp2"][c])
                    if c == 0:
                        wt(e, pbank_free[4 + h % 2])
                    t_inc(e.matmul(out=pb[ub][:], lhsT=dsel_s[:], rhs=tmp2[gp][:],
                                   start=(c == 0), stop=(c == KC - 1), skip_group_check=True))
            A[h].setdefault("upair", {})[j] = C["t"]
            if j == 3:
                A[h]["u"] = C["t"]

        def attn_rrep(h):
            if (h + 1) in A and 6 in A[h + 1]["tmp2"]:
                tok = ("v", A[h + 1]["tmp2"][6])
            elif h >= 1 and "ctx" in A[h - 1]:
                tok = ("v", A[h - 1]["ctx"])
            else:
                tok = None
            @blk.tensor
            def _(e, h=h, tok=tok):
                e.wait_ge(SS, A[h]["r"])
                wt(e, tok)
                t_inc(e.matmul(out=pb[2][:], lhsT=rsel_s[:],
                               rhs=rbandb[:, ts(h % 2, T)], start=True, stop=True))
            A[h]["rrep"] = C["t"]

        def attn_S2b(h):
            c_q, r_q = h // 2, (h % 2) * DK
            ub = 4 + (h % 2)
            @blk.scalar
            def _(e, h=h, ub=ub, c_q=c_q, r_q=r_q):
                e.wait_ge(STn, A[h]["u"])
                if h < 2:
                    wt(e, ctxb_free)
                s_inc(e.activation(out=ctxb[r_q:r_q + DK, ts(c_q, T)], in_=pb[ub][0:DK, :],
                                   func=AF.Copy))
            A[h]["evac"] = C["s"]
            pbank_free[4 + h % 2] = ("s", A[h]["evac"])

        def attn_ctx(h):
            c_q, r_q = h // 2, (h % 2) * DK
            @blk.vector
            def _(e, h=h, c_q=c_q, r_q=r_q):
                e.wait_ge(STn, A[h]["rrep"])
                e.wait_ge(SS, A[h]["evac"])
                if h < 2:
                    wt(e, ctxb_free)
                v_inc(e.tensor_tensor(out=ctxb[r_q:r_q + DK, ts(c_q, T)],
                                      in0=ctxb[r_q:r_q + DK, ts(c_q, T)],
                                      in1=pb[2][r_q:r_q + DK, :], op=ALU.mult))
            A[h]["ctx"] = C["v"]
            pbank_free[2] = ("v", A[h]["ctx"])

        def emit_w12_slice(jslot, mr):
            nonlocal hB_free, h_ready, x1ln_free
            mm_toks = {}
            silu_base = C["s"]
            for mc in range(4):
                mcg = mr * 4 + mc
                @blk.tensor
                def _(e, jslot=jslot, mc=mc, mcg=mcg, mr=mr):
                    if mc == 0:
                        e.wait_ge(SL[jslot], slot_loaded[jslot])
                        if mr > 0:
                            wt(e, x1ln_ready_box[0])
                    wt(e, pbank_free[0] if mc % 2 == 0 else pbank_free[1])
                    bA = mc % 2
                    for k in range(KC):
                        if mr == 0 and mc == 0:
                            e.wait_ge(SV, x1ln_chunks_box[0][k])
                        ins = e.matmul(out=pb[bA][:],
                                       lhsT=wsl[jslot][:, k * T + mc * P: k * T + (mc + 1) * P],
                                       rhs=x1ln_s[:, ts(k, T)], start=(k == 0), stop=(k == KC - 1))
                    t_inc(ins)
                    wt(e, pbank_free[2 + bA])
                    for k in range(KC):
                        ins = e.matmul(out=pb[2 + bA][:],
                                       lhsT=wsl[jslot][:, KC * T + k * T + mc * P: KC * T + k * T + (mc + 1) * P],
                                       rhs=x1ln_s[:, ts(k, T)], start=(k == 0), stop=(k == KC - 1))
                    t_inc(ins)
                mm_toks[mc] = (C["t"] - 1, C["t"])
                pbank_free[mc % 2] = ("s", silu_base + mc + 1)

            stt_base = C["v"]
            prev_stt = (sA_free[0], sA_free[1])
            @blk.scalar
            def _(e, mr=mr, mm_toks=mm_toks, stt_base=stt_base, prev_stt=prev_stt):
                for mc in range(4):
                    mcg = mr * 4 + mc
                    e.wait_ge(STn, mm_toks[mc][0])
                    if mc >= 2:
                        e.wait_ge(SV, stt_base + mc - 1)
                    else:
                        wt(e, prev_stt[mc])
                    s_inc(e.activation(out=sA[mc % 2][:], in_=pb[mc % 2][:], func=AF.Silu,
                                       scale=s1_s[:, mcg:mcg + 1]))
            assert C["s"] == silu_base + 4

            for mc in range(4):
                mcg = mr * 4 + mc
                @blk.vector
                def _(e, mc=mc, mcg=mcg, mm_toks=mm_toks, silu_base=silu_base):
                    e.wait_ge(STn, mm_toks[mc][1])
                    e.wait_ge(SS, silu_base + mc + 1)
                    if mcg == 0:
                        wt(e, hB_free)
                    v_inc(e.scalar_tensor_tensor(out=hB[:, ts(mcg, T)], in0=pb[2 + mc % 2][:],
                                                 scalar=s2_s[:, mcg:mcg + 1], in1=sA[mc % 2][:],
                                                 op0=ALU.mult, op1=ALU.mult))
                pbank_free[2 + mc % 2] = ("v", C["v"])
                sA_free[mc % 2] = ("v", C["v"])
            if mr == 7:
                h_ready = ("v", C["v"])
                x1ln_free = ("t", C["t"])

        def hmul_mc(jslot, kind, sidx, mc):
            rhs_buf = hB if kind == "h1" else g1B
            rhs_ready = h_ready if kind == "h1" else g1_ready
            mcg = sidx * 2 + mc
            gelu_inc = C["s"] + 1
            @blk.tensor
            def _(e, jslot=jslot, mc=mc, rhs_buf=rhs_buf, rhs_ready=rhs_ready, sidx=sidx):
                if mc == 0:
                    e.wait_ge(SL[jslot], slot_loaded[jslot])
                    wt(e, rhs_ready)
                wt(e, pbank_free[mc % 2])
                ins = None
                for k in range(HC):
                    ins = e.matmul(out=pb[mc % 2][:],
                                   lhsT=wsl[jslot][:, k * 256 + mc * P: k * 256 + (mc + 1) * P],
                                   rhs=rhs_buf[:, ts(k, T)], start=(k == 0), stop=(k == HC - 1))
                t_inc(ins)
            mm_done = C["t"]

            dst_buf = g1B if kind == "h1" else hB
            scol = sh1_s if kind == "h1" else sh2_s
            @blk.scalar
            def _(e, kind=kind, mcg=mcg, mc=mc, dst_buf=dst_buf, scol=scol, mm_done=mm_done):
                e.wait_ge(STn, mm_done)
                if mcg == 0:
                    if kind == "h1":
                        wt(e, g1_freeB)
                    else:
                        e.wait_ge(STn, hB_mm_last)
                s_inc(e.activation(out=dst_buf[:, ts(mcg, T)], in_=pb[mc % 2][:], func=AF.Gelu,
                                   scale=scol[:, mcg:mcg + 1]))
            assert C["s"] == gelu_inc
            pbank_free[mc % 2] = ("s", gelu_inc)
            return mm_done

        def finish_hmul(kind, sidx, mm_done):
            nonlocal g1_ready, g2_ready, hB_mm_last, g1_freeB
            if kind == "h1" and sidx == 15:
                g1_ready = ("s", C["s"])
                hB_mm_last = C["t"]
            if kind == "h2" and sidx == 15:
                g2_ready = ("s", C["s"])
                g1_freeB = ("t", C["t"])

        def emit_hmul_slice(jslot, kind, sidx):
            for mc in range(2):
                hmul_mc(jslot, kind, sidx, mc)
            finish_hmul(kind, sidx, None)

        def emit_w3_slice(jslot, sidx, tile):
            nonlocal hB_free, x1_free
            for mc in range(2):
                mcg = sidx * 2 + mc
                @blk.tensor
                def _(e, jslot=jslot, mc=mc, sidx=sidx):
                    if mc == 0:
                        e.wait_ge(SL[jslot], slot_loaded[jslot])
                        wt(e, g2_ready)
                    wt(e, pbank_free[mc % 2])
                    for k in range(HC):
                        ins = e.matmul(out=pb[mc % 2][:],
                                       lhsT=wsl[jslot][:, k * 256 + mc * P: k * 256 + (mc + 1) * P],
                                       rhs=hB[:, ts(k, T)], start=(k == 0), stop=(k == HC - 1))
                    t_inc(ins)
                mmd = C["t"]

                @blk.vector
                def _(e, mc=mc, mcg=mcg, mmd=mmd):
                    e.wait_ge(STn, mmd)
                    v_inc(e.scalar_tensor_tensor(out=x1_s[:, ts(mcg, T)], in0=pb[mc % 2][:],
                                                 scalar=s3_s[:, mcg:mcg + 1], in1=x1_s[:, ts(mcg, T)],
                                                 op0=ALU.mult, op1=ALU.add))
                pbank_free[mc % 2] = ("v", C["v"])
            if sidx == 3:
                hB_free = ("t", C["t"])

        def emit_out_store(tile):
            nonlocal out_stored, x1_free
            @blk.sync
            def _(e, tile=tile):
                e.wait_ge(SV, C["v"])
                e.dma_start(out=outT[:, ts(tile, T)].rearrange("(k p) t -> p k t", p=P),
                            in_=x1_s[:].rearrange("p (k t) -> p k t", k=KC)).then_inc(SO, 16)
            C["o"] += 16
            out_stored = ("o", C["o"])
            x1_free = ("o", C["o"])

        def emit_wout(ctx_pair_toks, tile):
            nonlocal ctxb_free, x_free
            w_loaded = slot_loaded[wout_slot[0]]
            for m in range(KC):
                bank = m % 2
                @blk.tensor
                def _(e, m=m, bank=bank, w_loaded=w_loaded):
                    e.wait_ge(SL[wout_slot[0]], w_loaded - 16 * (KC - 1 - m))
                    wt(e, pbank_free[bank])
                    ins = None
                    for k in range(KC):
                        if m == 0:
                            e.wait_ge(SV, ctx_pair_toks[k])
                        ins = e.matmul(out=pb[bank][:],
                                       lhsT=wsl[wout_slot[0]][:, m * D + k * P: m * D + (k + 1) * P],
                                       rhs=ctxb[:, ts(k, T)], start=(k == 0), stop=(k == KC - 1))
                    t_inc(ins)
                wmm = C["t"]
                if m == KC - 1:
                    ctxb_free = ("t", wmm)

                @blk.vector
                def _(e, m=m, bank=bank, wmm=wmm):
                    e.wait_ge(STn, wmm)
                    if m == 0:
                        wt(e, x1_free)
                    v_inc(e.scalar_tensor_tensor(out=x1_s[:, ts(m, T)], in0=pb[bank][:],
                                                 scalar=bout_s[:, m:m + 1], in1=xbuf[:, ts(m, T)],
                                                 op0=ALU.add, op1=ALU.add))
                pbank_free[bank] = ("v", C["v"])
            x_free = ("v", C["v"])
            return C["v"]

        x_loaded = XCNT[0]

        wout_slot = [0]
        w12_pre = [None, None]

        def dbg_barrier():
            @blk.tensor
            def _(e):
                e.wait_ge(SV, C["v"]); e.wait_ge(SS, C["s"]); e.wait_ge(SI, C["i"]); e.wait_ge(SO, C["o"])
                t_inc(e.engine_nop() if hasattr(e, "engine_nop") else e.nop())
            tcnt = C["t"]
            @blk.vector
            def _(e, tcnt=tcnt):
                e.wait_ge(STn, tcnt)
                v_inc(e.engine_nop())
            vcnt = C["v"]
            @blk.scalar
            def _(e, tcnt=tcnt, vcnt=vcnt):
                e.wait_ge(STn, tcnt); e.wait_ge(SV, vcnt)
                s_inc(e.nop())
            scnt = C["s"]
            @blk.sync
            def _(e, tcnt=tcnt, vcnt=vcnt, scnt=scnt):
                e.wait_ge(STn, tcnt); e.wait_ge(SV, vcnt); e.wait_ge(SS, scnt)

        def do_slice(kind, pos, tile):
            if kind == "w12" and pos in (0, 1) and w12_pre[pos] is not None:
                jj = w12_pre[pos]
                w12_pre[pos] = None
            else:
                jj = load_slice(kind, pos)
            if kind == "w12":
                emit_w12_slice(jj, pos)
            elif kind in ("h1", "h2"):
                emit_hmul_slice(jj, kind, pos)
            else:
                emit_w3_slice(jj, pos, tile)
            slot_free[jj] = ("t", C["t"])

        for i in range(NT):
            ffn_slices = []
            if i > 0:
                ffn_slices = [("w12", mr) for mr in range(8)] + \
                             [("h1", s_) for s_ in range(16)] + \
                             [("h2", s_) for s_ in range(16)]

            qg = ("v", A[NH - 1]["tmp"]) if (NH - 1) in A and "tmp" in A[NH - 1] else None
            pre = [("x", x_loaded), xln_free, qg, x1ln_ready_box[0]]
            ln1_done, ln1_chunks = emit_ln(xbuf, xln_s, pre)

            qkv_done = 0
            fidx = 0
            for sl in range(3):
                if fidx < 8 and i > 0:
                    kind, pos = ffn_slices[fidx]; fidx += 1
                    do_slice(kind, pos, i - 1)
                jq = load_slice("qkv", sl, direct=(i == 0))
                qkv_done = emit_qkv_slice(jq, sl, ln1_chunks)
                slot_free[jq] = ("t", C["t"])
            xln_free = ("t", C["t"])
            if i == 0:
                emit_casts()

            while fidx < 8 and i > 0:
                kind, pos = ffn_slices[fidx]; fidx += 1
                do_slice(kind, pos, i - 1)

            pend = []

            def queue_ffn():
                nonlocal fidx
                if i > 0 and fidx < len(ffn_slices):
                    kind, pos = ffn_slices[fidx]; fidx += 1
                    jj = load_slice(kind, pos)
                    pend.append([jj, kind, pos, 0])
                    pend.append([jj, kind, pos, 1])

            def run_ffn_mc():
                if pend:
                    jj, kind, pos, mc = pend.pop(0)
                    mm = hmul_mc(jj, kind, pos, mc)
                    if mc == 1:
                        finish_hmul(kind, pos, mm)
                        slot_free[jj] = ("t", C["t"])

            for h in range(NH + 2):
                if h < NH:
                    attn_V1(h)
                    queue_ffn()
                    run_ffn_mc()
                    attn_sc(h, qkv_done)
                    attn_S1(h)
                if 1 <= h <= NH:
                    hh = h - 1
                    attn_erep_pair(hh, 0)
                    attn_tmp2_pair(hh, 0)
                    run_ffn_mc()
                    attn_erep_pair(hh, 1)
                    attn_u_pair(hh, 0)
                    attn_tmp2_pair(hh, 1)
                    queue_ffn()
                    run_ffn_mc()
                    attn_erep_pair(hh, 2)
                    attn_u_pair(hh, 1)
                    attn_tmp2_pair(hh, 2)
                    attn_S2a(hh)
                    run_ffn_mc()
                    attn_erep_pair(hh, 3)
                    attn_u_pair(hh, 2)
                    attn_tmp2_pair(hh, 3)
                    run_ffn_mc()
                    attn_u_pair(hh, 3)
                if 2 <= h <= NH + 1:
                    attn_rrep(h - 2)
                    attn_S2b(h - 2)
                    attn_ctx(h - 2)
            ctx_done = A[NH - 1]["ctx"]
            qkv_free = ("v", ctx_done)
            pbank_free[2] = ("v", A[NH - 1]["ctx"])
            pbank_free[3] = ("v", A[NH - 1]["tmp2_done"])
            while pend:
                run_ffn_mc()
            while i > 0 and fidx < len(ffn_slices):
                kind, pos = ffn_slices[fidx]; fidx += 1
                do_slice(kind, pos, i - 1)

            if i > 0:
                for s_ in range(4):
                    do_slice("w3", s_, i - 1)
            jw = load_slice("wout", 0)
            wout_slot[0] = jw
            wout_loaded = slot_loaded[jw]
            w12_pre[0] = load_slice("w12", 0)
            if i > 0:
                emit_out_store(i - 1)

            x1_done = emit_wout([A[2 * k_ + 1]["ctx"] for k_ in range(KC)], i)
            slot_free[wout_slot[0]] = ("t", C["t"])

            if i + 1 < NT:
                @blk.sync
                def _(e, i=i):
                    wt(e, x_free)
                    e.dma_start(out=xbuf[:].rearrange("p (k t) -> p k t", k=KC),
                                in_=xT[:, ts(i + 1, T)].rearrange("(k p) t -> p k t", p=P)).then_inc(SX, 16)
                XCNT[0] += 16
                x_loaded = XCNT[0]
            w12_pre[1] = load_slice("w12", 1)

            qg2 = ("v", A[NH - 1]["tmp"])
            ln2_done, ln2_chunks = emit_ln(x1_s, x1ln_s, [x1ln_free, ("v", x1_done), qg2])
            x1ln_ready_box[0] = ("v", ln2_done)
            x1ln_chunks_box[0] = ln2_chunks
            if DEBUG_BARRIER:
                dbg_barrier()

        @blk.tensor
        def _(e, ln2_done=ln2_done):
            e.wait_ge(SV, ln2_done)
        ffn_slices = [("w12", mr) for mr in range(8)] + \
                     [("h1", s_) for s_ in range(16)] + \
                     [("h2", s_) for s_ in range(16)]
        for kind, pos in ffn_slices:
            do_slice(kind, pos, NT - 1)
        for s_ in range(4):
            do_slice("w3", s_, NT - 1)
        emit_out_store(NT - 1)

        @blk.sync
        def _(e):
            e.wait_ge(SO, C["o"])

    return nc


_PROG_CACHE = {}

def _get_prog():
    if "nc" not in _PROG_CACHE:
        _PROG_CACHE["nc"] = build_program()
    return _PROG_CACHE["nc"]


def _consts():
    bf = ml_dtypes.bfloat16
    dsel = np.zeros((P, P), bf)
    for p in range(P):
        dsel[p, p % DK] = 1
    bdsel = np.zeros((P, KC * P), bf)
    gsel = np.zeros((P, KC * P), bf)
    for c in range(KC):
        for p in range(P):
            gg = 2 * c + p // DK
            bdsel[p, c * P + gg] = 1
            gsel[gg, c * P + p] = 1
    densel = np.zeros((P, P), bf); densel[0:NH, 32] = 1
    rsel = np.zeros((P, P), bf); rsel[32, :] = 1
    brow = np.zeros((P, P), bf); brow[0, :] = 1
    col0 = np.zeros((P, P), bf); col0[:, 0] = 1
    eps = np.full((P, 1), EPS, np.float32)
    return dsel, bdsel, gsel, densel, rsel, brow, col0, eps


def kernel(L0_attn_x, L0_attn_W_qkv, L0_attn_b_qkv, L0_attn_W_out, L0_attn_b_out,
           L0_ffn_w1_w, L0_ffn_w1_s, L0_ffn_w2_w, L0_ffn_w2_s,
           L0_ffn_h1_w, L0_ffn_h1_s, L0_ffn_h2_w, L0_ffn_h2_s,
           L0_ffn_w3_w, L0_ffn_w3_s):
    x = np.asarray(L0_attn_x)
    B = x.shape[0]
    wq = np.asarray(L0_attn_W_qkv)
    wq3 = wq.reshape(D, NH, 3, DK)
    wqkv = np.ascontiguousarray(wq3.transpose(0, 2, 1, 3).reshape(D, 3 * D))
    bq3 = np.asarray(L0_attn_b_qkv).reshape(NH, 3, DK)
    bqkv = np.ascontiguousarray(bq3.transpose(1, 0, 2).reshape(3 * D // P, P).T)
    bout = np.ascontiguousarray(np.asarray(L0_attn_b_out).reshape(D // P, P).T)
    w1T = np.ascontiguousarray(np.asarray(L0_ffn_w1_w).T)
    w2T = np.ascontiguousarray(np.asarray(L0_ffn_w2_w).T)
    h1T = np.ascontiguousarray(np.asarray(L0_ffn_h1_w).T)
    h2T = np.ascontiguousarray(np.asarray(L0_ffn_h2_w).T)
    w3T = np.ascontiguousarray(np.asarray(L0_ffn_w3_w).T)
    s1 = np.ascontiguousarray(np.asarray(L0_ffn_w1_s).reshape(HC, P).T)
    s2 = np.ascontiguousarray(np.asarray(L0_ffn_w2_s).reshape(HC, P).T)
    sh1 = np.ascontiguousarray(np.asarray(L0_ffn_h1_s).reshape(HC, P).T)
    sh2 = np.ascontiguousarray(np.asarray(L0_ffn_h2_s).reshape(HC, P).T)
    s3 = np.ascontiguousarray(np.asarray(L0_ffn_w3_s).reshape(D // P, P).T)
    dsel, bdsel, gsel, densel, rsel, brow, col0, eps = _consts()
    cb16 = np.ascontiguousarray(np.concatenate(
        [dsel, bdsel, gsel, densel, rsel, brow, col0], axis=1))
    cb32 = np.ascontiguousarray(np.concatenate(
        [eps, bqkv, bout, s1, s2, sh1, sh2, s3], axis=1))

    shared = dict(wqkv=wqkv, wout=np.ascontiguousarray(np.asarray(L0_attn_W_out)),
                  w1T=w1T, w2T=w2T, h1T=h1T, h2T=h2T, w3T=w3T,
                  cb16=cb16, cb32=cb32)

    in_maps = []
    for c in range(B):
        m = dict(shared)
        m["xT"] = np.ascontiguousarray(x[c].T)
        in_maps.append(m)

    nc = _get_prog()
    res = run_bass_kernel_spmd(nc, in_maps, core_ids=list(range(N_CORES)))
    out = np.stack([np.asarray(r["outT"], dtype=np.float32).T for r in res.results], axis=0)
    return out


# version 22
# speedup vs baseline: 1.1051x; 1.0006x over previous
import numpy as np
import ml_dtypes
from contextlib import ExitStack

import concourse.bass as bass
import concourse.mybir as mybir
from concourse.bass_utils import run_bass_kernel_spmd

F32 = mybir.dt.float32
BF16 = mybir.dt.bfloat16
AF = mybir.ActivationFunctionType
ALU = mybir.AluOpType
ts = bass.ts

P = 128
D = 1024
DH = 4096
NH, DK = 16, 64
S = 4096
T = 512
NT = S // T
KC = D // P
HC = DH // P
N_CORES = 8
EPS = 1e-5
DEBUG_BARRIER = False
SLOT = 8192


def build_program():
    nc = bass.Bass("TRN2", target_bir_lowering=False)

    xT = nc.dram_tensor("xT", [D, S], F32, kind="ExternalInput").ap()
    wqkv = nc.dram_tensor("wqkv", [D, 3 * D], F32, kind="ExternalInput").ap()
    wout = nc.dram_tensor("wout", [D, D], F32, kind="ExternalInput").ap()
    w1T = nc.dram_tensor("w1T", [D, DH], F32, kind="ExternalInput").ap()
    w2T = nc.dram_tensor("w2T", [D, DH], F32, kind="ExternalInput").ap()
    h1T = nc.dram_tensor("h1T", [DH, DH], F32, kind="ExternalInput").ap()
    h2T = nc.dram_tensor("h2T", [DH, DH], F32, kind="ExternalInput").ap()
    w3T = nc.dram_tensor("w3T", [DH, D], F32, kind="ExternalInput").ap()
    cb16 = nc.dram_tensor("cb16", [P, 2688], BF16, kind="ExternalInput").ap()
    cb32 = nc.dram_tensor("cb32", [P, 169], F32, kind="ExternalInput").ap()

    outT = nc.dram_tensor("outT", [D, S], F32, kind="ExternalOutput").ap()

    wbqkv_d = nc.dram_tensor("wbqkv_d", [D, 3 * D], BF16, kind="Internal").ap()
    wbout_d = nc.dram_tensor("wbout_d", [D, D], BF16, kind="Internal").ap()
    wb1_d = nc.dram_tensor("wb1_d", [D, DH], BF16, kind="Internal").ap()
    wb2_d = nc.dram_tensor("wb2_d", [D, DH], BF16, kind="Internal").ap()
    hb1_d = nc.dram_tensor("hb1_d", [DH, DH], BF16, kind="Internal").ap()
    hb2_d = nc.dram_tensor("hb2_d", [DH, DH], BF16, kind="Internal").ap()
    wb3_d = nc.dram_tensor("wb3_d", [DH, D], BF16, kind="Internal").ap()

    with ExitStack() as g:
        E = g.enter_context
        cb16_s = E(nc.sbuf_tensor("cb16_s", [P, 2688], BF16))
        cb32_s = E(nc.sbuf_tensor("cb32_s", [P, 169], F32))
        dsel_s = cb16_s[:, 0:P]
        bdsel_s = cb16_s[:, P:P + KC * P]
        gsel_s = cb16_s[:, P + KC * P:P + 2 * KC * P]
        densel_s = cb16_s[:, 2176:2304]
        rsel_s = cb16_s[:, 2304:2432]
        brow_s = cb16_s[:, 2432:2560]
        col0_s = cb16_s[:, 2560:2688]
        eps_s = cb32_s[:, 0:1]
        bqkv_s = cb32_s[:, 1:25]
        bout_s = cb32_s[:, 25:33]
        s1_s = cb32_s[:, 33:65]
        s2_s = cb32_s[:, 65:97]
        sh1_s = cb32_s[:, 97:129]
        sh2_s = cb32_s[:, 129:161]
        s3_s = cb32_s[:, 161:169]

        wsl = [E(nc.sbuf_tensor(f"wsl{j}", [P, SLOT], BF16)) for j in range(2)]
        xbuf = E(nc.sbuf_tensor("xbuf", [P, KC * T], F32))
        xln_s = E(nc.sbuf_tensor("xln_s", [P, KC * T], BF16))
        x1ln_s = E(nc.sbuf_tensor("x1ln_s", [P, KC * T], BF16))
        q_s = E(nc.sbuf_tensor("q_s", [P, KC * T], BF16))
        k_s = E(nc.sbuf_tensor("k_s", [P, KC * T], BF16))
        v_s = E(nc.sbuf_tensor("v_s", [P, KC * T], BF16))
        qrep = E(nc.sbuf_tensor("qrep", [P, T], BF16))
        tmpa = E(nc.sbuf_tensor("tmpa", [P, KC * T], BF16))
        esb = [E(nc.sbuf_tensor(f"esb{j}", [P, T], BF16)) for j in range(2)]
        tmp2 = [E(nc.sbuf_tensor(f"tmp2_{j}", [P, T], BF16)) for j in range(2)]
        rband = E(nc.sbuf_tensor("rband", [33, T], F32))
        rbandb = E(nc.sbuf_tensor("rbandb", [P, 2 * T], BF16))
        scr = E(nc.sbuf_tensor("scr", [1, 2 * T], F32))
        rowb = E(nc.sbuf_tensor("rowb", [P, T], BF16))

        sA = [E(nc.sbuf_tensor(f"sA{j}", [P, T], BF16)) for j in range(2)]
        xbb = [E(nc.sbuf_tensor("xbb0", [P, T], BF16)), qrep]
        ctxb = E(nc.sbuf_tensor("ctxb", [P, KC * T], BF16))
        x1_s = E(nc.sbuf_tensor("x1_s", [P, KC * T], F32))
        hB = E(nc.sbuf_tensor("hB", [P, HC * T], BF16))
        g1B = E(nc.sbuf_tensor("g1B", [P, HC * T], BF16))

        pb = [E(nc.psum_tensor(f"pb{i}", [P, T], F32)) for i in range(8)]

        SW = E(nc.semaphore("SW"))
        SI = E(nc.semaphore("SI"))
        SO = E(nc.semaphore("SO"))
        STn = E(nc.semaphore("STn"))
        SV = E(nc.semaphore("SV"))
        SS = E(nc.semaphore("SS"))
        WSEM = {name: E(nc.semaphore(f"W_{name}")) for name in
                ("qkv", "wout", "w1", "w2", "h1", "h2", "w3")}
        WCNT = {name: 0 for name in WSEM}
        SL = [E(nc.semaphore(f"SL{j}")) for j in range(2)]
        SX = E(nc.semaphore("SX"))
        blk = E(nc.Block())
        C = dict(w=0, i=0, o=0, t=0, v=0, s=0)
        SEMS = {"t": STn, "v": SV, "s": SS, "i": SI, "o": SO, "w": SW, "x": None}

        def t_inc(instr):
            C["t"] += 1
            return instr.then_inc(STn, 1)

        def v_inc(instr):
            C["v"] += 1
            return instr.then_inc(SV, 1)

        def s_inc(instr):
            C["s"] += 1
            return instr.then_inc(SS, 1)

        def wt(e, tok):
            if tok is None:
                return
            if tok[1] > 0:
                sem = SX if tok[0] == "x" else SEMS[tok[0]]
                e.wait_ge(sem, tok[1])

        XCNT = [0]
        @blk.sync
        def _(e):
            e.dma_start(out=xbuf[:].rearrange("p (k t) -> p k t", k=KC),
                        in_=xT[:, 0:T].rearrange("(k p) t -> p k t", p=P)).then_inc(SX, 16)
            for dst, src in [(cb16_s[:], cb16), (cb32_s[:], cb32)]:
                e.dma_start(out=dst, in_=src).then_inc(SI, 16)
                C["i"] += 16
        XCNT[0] += 16
        consts_loaded = C["i"]

        @blk.tensor
        def _(e):
            e.wait_ge(SI, consts_loaded)
        @blk.scalar
        def _(e):
            e.wait_ge(SI, consts_loaded)

        @blk.vector
        def _(e):
            e.wait_ge(SI, consts_loaded)
            e.memset(esb[0][:], 0.0)
            e.memset(esb[1][:], 0.0)
            e.memset(rbandb[:], 0.0)
            v_inc(e.memset(rowb[:], 0.0))
        memset_done = C["v"]

        def emit_casts():
            @blk.gpsimd
            def _(e):
                for name, src_, dst, nrows in [("wout", wout, wbout_d, D),
                                               ("w1", w1T, wb1_d, D), ("w2", w2T, wb2_d, D),
                                               ("qkv", wqkv, wbqkv_d, D)]:
                    for r0 in range(0, nrows, 256):
                        e.dma_start(out=dst[r0:r0 + 256, :], in_=src_[r0:r0 + 256, :]).then_inc(WSEM[name], 16)
                        WCNT[name] += 16
                    e.wait_ge(WSEM[name], WCNT[name])
                for name, src_, dst, cstep in [("h1", h1T, hb1_d, 512), ("h2", h2T, hb2_d, 512),
                                               ("w3", w3T, wb3_d, 256)]:
                    ncols = src_.shape[1]
                    for c0 in range(0, ncols, cstep):
                        if c0 > 0:
                            e.wait_ge(WSEM[name], WCNT[name])
                        e.dma_start(out=dst[:, c0:c0 + cstep], in_=src_[:, c0:c0 + cstep]).then_inc(WSEM[name], 16)
                        WCNT[name] += 16
                    if name != "w3":
                        e.wait_ge(WSEM[name], WCNT[name])

        x_loaded = 0
        x_free = None
        x1_free = None
        x1_store = None
        xln_free = None
        x1ln_free = None
        qkv_free = None
        ctxb_free = None
        rowb_free = None
        pbank_free = [None] * 8
        slot_free = [None, None]
        slot_loaded = [0, 0]
        slot_idx = [0]
        tmpa_free = [0, 0]
        esb_free = [0, 0]
        psS_free = [None, None]
        rbandb_free = [0, 0]
        tmp2_free = [0, 0]
        sA_free = [None, None]
        hB_free = None
        g1_freeB = None
        h_ready = None
        g1_ready = None
        g2_ready = None
        hB_mm_last = 0
        out_stored = None
        x1ln_ready_box = [None]
        x1ln_chunks_box = [None]

        SLCNT = [0, 0]

        def load_slice(kind, pos, direct=False):
            j = slot_idx[0] % 2
            slot_idx[0] += 1
            if direct:
                @blk.gpsimd
                def _(e, pos=pos, j=j):
                    wt(e, slot_free[j])
                    e.dma_start(out=wsl[j][:].rearrange("p (k m) -> p k m", k=KC),
                                in_=wqkv[:, ts(pos, D)].rearrange("(k p) m -> p k m", p=P)).then_inc(SL[j], 16)
                SLCNT[j] += 16
                slot_loaded[j] = SLCNT[j]
                return j
            @blk.sync
            def _(e, kind=kind, pos=pos, j=j):
                wt(e, slot_free[j])
                if kind == "w12":
                    e.wait_ge(WSEM["w1"], WCNT["w1"])
                    e.wait_ge(WSEM["w2"], WCNT["w2"])
                elif kind in ("h1", "h2"):
                    e.wait_ge(WSEM[kind], 16 * (pos // 2 + 1))
                elif kind == "w3":
                    e.wait_ge(WSEM["w3"], 16 * (pos + 1))
                else:
                    wn = {"qkv": "qkv", "wout": "wout"}[kind]
                    e.wait_ge(WSEM[wn], WCNT[wn])
                if kind == "qkv":
                    e.dma_start(out=wsl[j][:].rearrange("p (k m) -> p k m", k=KC),
                                in_=wbqkv_d[:, ts(pos, D)].rearrange("(k p) m -> p k m", p=P)).then_inc(SL[j], 16)
                    SLCNT[j] += 16
                elif kind == "wout":
                    e.dma_start(out=wsl[j][:].rearrange("p (k m) -> p k m", k=KC),
                                in_=wbout_d.rearrange("(k p) m -> p k m", p=P)).then_inc(SL[j], 16)
                    SLCNT[j] += 16
                elif kind == "w12":
                    e.dma_start(out=wsl[j][:, 0:KC * T].rearrange("p (k m) -> p k m", k=KC),
                                in_=wb1_d[:, ts(pos, T)].rearrange("(k p) m -> p k m", p=P)).then_inc(SL[j], 16)
                    e.dma_start(out=wsl[j][:, KC * T:2 * KC * T].rearrange("p (k m) -> p k m", k=KC),
                                in_=wb2_d[:, ts(pos, T)].rearrange("(k p) m -> p k m", p=P)).then_inc(SL[j], 16)
                    SLCNT[j] += 32
                else:
                    src = {"h1": hb1_d, "h2": hb2_d, "w3": wb3_d}[kind]
                    e.dma_start(out=wsl[j][:].rearrange("p (k m) -> p k m", k=HC),
                                in_=src[:, ts(pos, 256)].rearrange("(k p) m -> p k m", p=P)).then_inc(SL[j], 16)
                    SLCNT[j] += 16
            slot_loaded[j] = SLCNT[j]
            return j

        def emit_ln(src_f32, dst_bf, pre_toks, pre_chunks=None, mid_cb=None):
            nonlocal rowb_free
            mm_tok = [0, 0]
            first = [True]
            for c in range(KC):
                @blk.vector
                def _(e, c=c):
                    if first[0]:
                        for tok in pre_toks:
                            wt(e, tok)
                        first[0] = False
                    else:
                        wt(e, pre_chunks[c] if pre_chunks else None)
                    if mm_tok[c % 2] > 0:
                        e.wait_ge(STn, mm_tok[c % 2])
                    if tmp2_free[c % 2] > 0:
                        e.wait_ge(STn, tmp2_free[c % 2])
                    e.tensor_copy(out=xbb[c % 2][:], in_=src_f32[:, ts(c, T)])
                    v_inc(e.tensor_tensor(out=tmp2[c % 2][:], in0=xbb[c % 2][:], in1=xbb[c % 2][:], op=ALU.mult))
                sq_done = C["v"]

                @blk.tensor
                def _(e, c=c, sq_done=sq_done):
                    e.wait_ge(SV, sq_done)
                    if c == 0:
                        wt(e, pbank_free[4])
                        wt(e, pbank_free[5])
                    e.matmul(out=pb[4][:], lhsT=col0_s[:], rhs=xbb[c % 2][:],
                             start=(c == 0), stop=(c == KC - 1), skip_group_check=True)
                    t_inc(e.matmul(out=pb[5][:], lhsT=col0_s[:], rhs=tmp2[c % 2][:],
                                   start=(c == 0), stop=(c == KC - 1), skip_group_check=True))
                mm_tok[c % 2] = C["t"]
                tmp2_free[c % 2] = C["t"]
            stats_done = C["t"]
            if mid_cb is not None:
                mid_cb()

            m_row = scr[0:1, 0:T]
            v_row = scr[0:1, T:2 * T]
            w_row = rband[0:1, :]

            @blk.vector
            def _(e):
                e.wait_ge(STn, stats_done)
                e.tensor_scalar(out=m_row, in0=pb[4][0:1, :], scalar1=1.0 / D, scalar2=None, op0=ALU.mult)
                e.tensor_scalar(out=v_row, in0=pb[5][0:1, :], scalar1=1.0 / D, scalar2=None, op0=ALU.mult)
                e.tensor_tensor(out=w_row, in0=m_row, in1=m_row, op=ALU.mult)
                v_inc(e.tensor_tensor(out=v_row, in0=v_row, in1=w_row, op=ALU.subtract))
            var_done = C["v"]
            pbank_free[4] = ("v", var_done)
            pbank_free[5] = ("v", var_done)

            @blk.scalar
            def _(e):
                e.wait_ge(SV, var_done)
                e.activation(out=w_row, in_=v_row, func=AF.Ln, bias=eps_s[0:1, :])
                s_inc(e.activation(out=w_row, in_=w_row, func=AF.Exp, scale=-0.5))
            r_done = C["s"]

            @blk.vector
            def _(e):
                e.wait_ge(SS, r_done)
                wt(e, rowb_free)
                e.tensor_tensor(out=v_row, in0=m_row, in1=w_row, op=ALU.mult)
                e.tensor_copy(out=rowb[0:1, :], in_=w_row)
                v_inc(e.tensor_copy(out=rowb[32:33, :], in_=v_row))
            rowb_done = C["v"]

            @blk.tensor
            def _(e):
                e.wait_ge(SV, rowb_done)
                wt(e, pbank_free[2])
                wt(e, pbank_free[3])
                e.matmul(out=pb[2][:], lhsT=brow_s[:], rhs=rowb[:], start=True, stop=True)
                t_inc(e.matmul(out=pb[3][:], lhsT=rsel_s[:], rhs=rowb[:], start=True, stop=True))
            bcast_done = C["t"]
            rowb_free = ("t", bcast_done)

            apply_chunks = []
            for c in range(KC):
                @blk.vector
                def _(e, c=c):
                    if c == 0:
                        e.wait_ge(STn, bcast_done)
                    e.tensor_tensor(out=xbb[0][:], in0=src_f32[:, ts(c, T)], in1=pb[2][:], op=ALU.mult)
                    v_inc(e.tensor_tensor(out=dst_bf[:, ts(c, T)], in0=xbb[0][:], in1=pb[3][:], op=ALU.subtract))
                apply_chunks.append(C["v"])
            apply_done = C["v"]
            pbank_free[2] = ("v", apply_done)
            pbank_free[3] = ("v", apply_done)
            return apply_done, apply_chunks

        def emit_qkv_slice(jslot, sl, ln1_chunks):
            nonlocal qkv_free
            for mc in range(KC):
                m = sl * KC + mc
                bank = 4 + m % 2
                @blk.tensor
                def _(e, jslot=jslot, sl=sl, mc=mc, m=m, bank=bank):
                    if mc == 0:
                        e.wait_ge(SL[jslot], slot_loaded[jslot])
                    wt(e, pbank_free[bank])
                    ins = None
                    for k in range(KC):
                        if sl == 0 and mc == 0:
                            e.wait_ge(SV, ln1_chunks[k])
                        ins = e.matmul(out=pb[bank][:],
                                       lhsT=wsl[jslot][:, k * D + mc * P: k * D + (mc + 1) * P],
                                       rhs=xln_s[:, ts(k, T)], start=(k == 0), stop=(k == KC - 1))
                    t_inc(ins)
                mmd = C["t"]

                dst = (q_s, k_s, v_s)[m // KC]
                col = (m % KC) * T
                @blk.vector
                def _(e, m=m, bank=bank, dst=dst, col=col, mmd=mmd):
                    e.wait_ge(STn, mmd)
                    if m == 0:
                        wt(e, qkv_free)
                    v_inc(e.tensor_scalar(out=dst[:, col:col + T], in0=pb[bank][:],
                                          scalar1=bqkv_s[:, m:m + 1], scalar2=None, op0=ALU.add))
                pbank_free[bank] = ("v", C["v"])
            return C["v"]

        A = {}

        def attn_V1(h):
            c_q, r_q = h // 2, (h % 2) * DK
            prev_sc = A[h - 1]["sc"] if h >= 1 else (A[NH - 1]["sc"] if (NH - 1) in A and "sc" in A[NH - 1] else 0)
            @blk.vector
            def _(e, h=h, c_q=c_q, r_q=r_q, prev_sc=prev_sc):
                if prev_sc > 0:
                    e.wait_ge(STn, prev_sc)
                e.tensor_copy(out=qrep[0:DK, :], in_=q_s[r_q:r_q + DK, ts(c_q, T)])
                e.tensor_copy(out=qrep[DK:P, :], in_=q_s[r_q:r_q + DK, ts(c_q, T)])
                ins = None
                for c in range(KC):
                    ins = e.tensor_tensor(out=tmpa[:, ts(c, T)], in0=k_s[:, ts(c, T)],
                                          in1=qrep[:], op=ALU.mult)
                v_inc(ins)
            A[h] = {"erep": {}, "tmp2": {}}
            A[h]["tmp"] = C["v"]

        def attn_sc(h, qkv_done):
            sb = 6 + (h % 2)
            @blk.tensor
            def _(e, h=h, sb=sb, qkv_done=qkv_done):
                e.wait_ge(SV, A[h]["tmp"])
                if h >= 2:
                    e.wait_ge(SS, A[h - 2]["r"])
                ins = None
                for c in range(KC):
                    ins = e.matmul(out=pb[sb][:], lhsT=bdsel_s[:, ts(c, P)],
                                   rhs=tmpa[:, ts(c, T)],
                                   start=(c == 0), stop=(c == KC - 1), skip_group_check=True)
                t_inc(ins)
            A[h]["sc"] = C["t"]

        def attn_S1(h):
            sb = 6 + (h % 2)
            @blk.scalar
            def _(e, h=h, sb=sb):
                e.wait_ge(STn, A[h]["sc"])
                if h >= 2:
                    e.wait_ge(STn, A[h - 2]["erep"][KC - 1])
                e.activation(out=rband[0:1, 0:1], in_=eps_s[0:1, :], func=AF.Ln)
                s_inc(e.activation(out=esb[h % 2][0:NH, :], in_=pb[sb][0:NH, :],
                                   func=AF.Exp, scale=0.125))
            A[h]["exp"] = C["s"]

        def attn_erep_pair(h, j):
            sb = 6 + (h % 2)
            cpair = (2 * j, 2 * j + 1)
            if j == 0:
                tok2 = ("v", A[h - 2]["ctx"]) if h >= 2 else pbank_free[2]
                tok3 = ("v", A[h - 1]["tmp2"][7]) if h >= 1 else pbank_free[3]
            else:
                tok2 = ("v", A[h]["tmp2"][2 * j - 2])
                tok3 = ("v", A[h]["tmp2"][2 * j - 1])
            @blk.tensor
            def _(e, h=h, j=j, cpair=cpair, sb=sb, tok2=tok2, tok3=tok3):
                if j == 0:
                    e.wait_ge(SS, A[h]["exp"])
                    if h == 0:
                        e.wait_ge(SV, memset_done)
                wt(e, tok2)
                wt(e, tok3)
                for c in cpair:
                    gp = c % 2
                    t_inc(e.matmul(out=pb[2 + gp][:], lhsT=gsel_s[:, ts(c, P)],
                                   rhs=esb[h % 2][:], start=True, stop=True))
                    A[h]["erep"][c] = C["t"]
                if j == 0:
                    t_inc(e.matmul(out=pb[sb][:], lhsT=densel_s[:],
                                   rhs=esb[h % 2][:], start=True, stop=True))
                    A[h]["den"] = C["t"]

        def attn_tmp2_pair(h, j):
            cpair = (2 * j, 2 * j + 1)
            if j == 0:
                utok = A[h - 1]["u"] if h >= 1 else 0
            else:
                utok = A[h]["upair"][j - 1]
            @blk.vector
            def _(e, h=h, cpair=cpair, utok=utok):
                for c in cpair:
                    gp = c % 2
                    e.wait_ge(STn, A[h]["erep"][c])
                    if utok > 0:
                        e.wait_ge(STn, utok)
                    v_inc(e.tensor_tensor(out=tmp2[gp][:], in0=pb[2 + gp][:],
                                          in1=v_s[:, ts(c, T)], op=ALU.mult))
                    A[h]["tmp2"][c] = C["v"]
            if j == 3:
                A[h]["tmp2_done"] = C["v"]

        def attn_S2a(h):
            sb = 6 + (h % 2)
            @blk.scalar
            def _(e, h=h, sb=sb):
                e.wait_ge(STn, A[h]["den"])
                if h >= 2:
                    e.wait_ge(STn, A[h - 2]["rrep"])
                e.activation(out=rband[32:33, :], in_=pb[sb][32:33, :], func=AF.Ln)
                s_inc(e.activation(out=rbandb[32:33, ts(h % 2, T)], in_=rband[32:33, :],
                                   func=AF.Exp, scale=-1.0))
            A[h]["r"] = C["s"]

        def attn_u_pair(h, j):
            ub = 4 + (h % 2)
            cpair = (2 * j, 2 * j + 1)
            @blk.tensor
            def _(e, h=h, ub=ub, cpair=cpair):
                for c in cpair:
                    gp = c % 2
                    e.wait_ge(SV, A[h]["tmp2"][c])
                    if c == 0:
                        wt(e, pbank_free[4 + h % 2])
                    t_inc(e.matmul(out=pb[ub][:], lhsT=dsel_s[:], rhs=tmp2[gp][:],
                                   start=(c == 0), stop=(c == KC - 1), skip_group_check=True))
            A[h].setdefault("upair", {})[j] = C["t"]
            if j == 3:
                A[h]["u"] = C["t"]

        def attn_rrep(h):
            if (h + 1) in A and 6 in A[h + 1]["tmp2"]:
                tok = ("v", A[h + 1]["tmp2"][6])
            elif h >= 1 and "ctx" in A[h - 1]:
                tok = ("v", A[h - 1]["ctx"])
            else:
                tok = None
            @blk.tensor
            def _(e, h=h, tok=tok):
                e.wait_ge(SS, A[h]["r"])
                wt(e, tok)
                t_inc(e.matmul(out=pb[2][:], lhsT=rsel_s[:],
                               rhs=rbandb[:, ts(h % 2, T)], start=True, stop=True))
            A[h]["rrep"] = C["t"]

        def attn_S2b(h):
            c_q, r_q = h // 2, (h % 2) * DK
            ub = 4 + (h % 2)
            @blk.scalar
            def _(e, h=h, ub=ub, c_q=c_q, r_q=r_q):
                e.wait_ge(STn, A[h]["u"])
                if h < 2:
                    wt(e, ctxb_free)
                s_inc(e.activation(out=ctxb[r_q:r_q + DK, ts(c_q, T)], in_=pb[ub][0:DK, :],
                                   func=AF.Copy))
            A[h]["evac"] = C["s"]
            pbank_free[4 + h % 2] = ("s", A[h]["evac"])

        def attn_ctx(h):
            c_q, r_q = h // 2, (h % 2) * DK
            @blk.vector
            def _(e, h=h, c_q=c_q, r_q=r_q):
                e.wait_ge(STn, A[h]["rrep"])
                e.wait_ge(SS, A[h]["evac"])
                if h < 2:
                    wt(e, ctxb_free)
                v_inc(e.tensor_tensor(out=ctxb[r_q:r_q + DK, ts(c_q, T)],
                                      in0=ctxb[r_q:r_q + DK, ts(c_q, T)],
                                      in1=pb[2][r_q:r_q + DK, :], op=ALU.mult))
            A[h]["ctx"] = C["v"]
            pbank_free[2] = ("v", A[h]["ctx"])

        def emit_w12_slice(jslot, mr):
            nonlocal hB_free, h_ready, x1ln_free
            mm_toks = {}
            silu_base = C["s"]
            for mc in range(4):
                mcg = mr * 4 + mc
                @blk.tensor
                def _(e, jslot=jslot, mc=mc, mcg=mcg, mr=mr):
                    if mc == 0:
                        e.wait_ge(SL[jslot], slot_loaded[jslot])
                        if mr > 0:
                            wt(e, x1ln_ready_box[0])
                    wt(e, pbank_free[0] if mc % 2 == 0 else pbank_free[1])
                    bA = mc % 2
                    for k in range(KC):
                        if mr == 0 and mc == 0:
                            e.wait_ge(SV, x1ln_chunks_box[0][k])
                        ins = e.matmul(out=pb[bA][:],
                                       lhsT=wsl[jslot][:, k * T + mc * P: k * T + (mc + 1) * P],
                                       rhs=x1ln_s[:, ts(k, T)], start=(k == 0), stop=(k == KC - 1))
                    t_inc(ins)
                    wt(e, pbank_free[2 + bA])
                    for k in range(KC):
                        ins = e.matmul(out=pb[2 + bA][:],
                                       lhsT=wsl[jslot][:, KC * T + k * T + mc * P: KC * T + k * T + (mc + 1) * P],
                                       rhs=x1ln_s[:, ts(k, T)], start=(k == 0), stop=(k == KC - 1))
                    t_inc(ins)
                mm_toks[mc] = (C["t"] - 1, C["t"])
                pbank_free[mc % 2] = ("s", silu_base + mc + 1)

            stt_base = C["v"]
            prev_stt = (sA_free[0], sA_free[1])
            @blk.scalar
            def _(e, mr=mr, mm_toks=mm_toks, stt_base=stt_base, prev_stt=prev_stt):
                for mc in range(4):
                    mcg = mr * 4 + mc
                    e.wait_ge(STn, mm_toks[mc][0])
                    if mc >= 2:
                        e.wait_ge(SV, stt_base + mc - 1)
                    else:
                        wt(e, prev_stt[mc])
                    s_inc(e.activation(out=sA[mc % 2][:], in_=pb[mc % 2][:], func=AF.Silu,
                                       scale=s1_s[:, mcg:mcg + 1]))
            assert C["s"] == silu_base + 4

            for mc in range(4):
                mcg = mr * 4 + mc
                @blk.vector
                def _(e, mc=mc, mcg=mcg, mm_toks=mm_toks, silu_base=silu_base):
                    e.wait_ge(STn, mm_toks[mc][1])
                    e.wait_ge(SS, silu_base + mc + 1)
                    if mcg == 0:
                        wt(e, hB_free)
                    v_inc(e.scalar_tensor_tensor(out=hB[:, ts(mcg, T)], in0=pb[2 + mc % 2][:],
                                                 scalar=s2_s[:, mcg:mcg + 1], in1=sA[mc % 2][:],
                                                 op0=ALU.mult, op1=ALU.mult))
                pbank_free[2 + mc % 2] = ("v", C["v"])
                sA_free[mc % 2] = ("v", C["v"])
            if mr == 7:
                h_ready = ("v", C["v"])
                x1ln_free = ("t", C["t"])

        def hmul_mc(jslot, kind, sidx, mc):
            rhs_buf = hB if kind == "h1" else g1B
            rhs_ready = h_ready if kind == "h1" else g1_ready
            mcg = sidx * 2 + mc
            gelu_inc = C["s"] + 1
            @blk.tensor
            def _(e, jslot=jslot, mc=mc, rhs_buf=rhs_buf, rhs_ready=rhs_ready, sidx=sidx):
                if mc == 0:
                    e.wait_ge(SL[jslot], slot_loaded[jslot])
                    wt(e, rhs_ready)
                wt(e, pbank_free[mc % 2])
                ins = None
                for k in range(HC):
                    ins = e.matmul(out=pb[mc % 2][:],
                                   lhsT=wsl[jslot][:, k * 256 + mc * P: k * 256 + (mc + 1) * P],
                                   rhs=rhs_buf[:, ts(k, T)], start=(k == 0), stop=(k == HC - 1))
                t_inc(ins)
            mm_done = C["t"]

            dst_buf = g1B if kind == "h1" else hB
            scol = sh1_s if kind == "h1" else sh2_s
            @blk.scalar
            def _(e, kind=kind, mcg=mcg, mc=mc, dst_buf=dst_buf, scol=scol, mm_done=mm_done):
                e.wait_ge(STn, mm_done)
                if mcg == 0:
                    if kind == "h1":
                        wt(e, g1_freeB)
                    else:
                        e.wait_ge(STn, hB_mm_last)
                s_inc(e.activation(out=dst_buf[:, ts(mcg, T)], in_=pb[mc % 2][:], func=AF.Gelu,
                                   scale=scol[:, mcg:mcg + 1]))
            assert C["s"] == gelu_inc
            pbank_free[mc % 2] = ("s", gelu_inc)
            return mm_done

        def finish_hmul(kind, sidx, mm_done):
            nonlocal g1_ready, g2_ready, hB_mm_last, g1_freeB
            if kind == "h1" and sidx == 15:
                g1_ready = ("s", C["s"])
                hB_mm_last = C["t"]
            if kind == "h2" and sidx == 15:
                g2_ready = ("s", C["s"])
                g1_freeB = ("t", C["t"])

        def emit_hmul_slice(jslot, kind, sidx):
            for mc in range(2):
                hmul_mc(jslot, kind, sidx, mc)
            finish_hmul(kind, sidx, None)

        def emit_w3_slice(jslot, sidx, tile):
            nonlocal hB_free, x1_free
            for mc in range(2):
                mcg = sidx * 2 + mc
                @blk.tensor
                def _(e, jslot=jslot, mc=mc, sidx=sidx):
                    if mc == 0:
                        e.wait_ge(SL[jslot], slot_loaded[jslot])
                        wt(e, g2_ready)
                    wt(e, pbank_free[mc % 2])
                    for k in range(HC):
                        ins = e.matmul(out=pb[mc % 2][:],
                                       lhsT=wsl[jslot][:, k * 256 + mc * P: k * 256 + (mc + 1) * P],
                                       rhs=hB[:, ts(k, T)], start=(k == 0), stop=(k == HC - 1))
                    t_inc(ins)
                mmd = C["t"]

                @blk.vector
                def _(e, mc=mc, mcg=mcg, mmd=mmd):
                    e.wait_ge(STn, mmd)
                    v_inc(e.scalar_tensor_tensor(out=x1_s[:, ts(mcg, T)], in0=pb[mc % 2][:],
                                                 scalar=s3_s[:, mcg:mcg + 1], in1=x1_s[:, ts(mcg, T)],
                                                 op0=ALU.mult, op1=ALU.add))
                pbank_free[mc % 2] = ("v", C["v"])
            if sidx == 3:
                hB_free = ("t", C["t"])

        def emit_out_store(tile):
            nonlocal out_stored, x1_free
            @blk.sync
            def _(e, tile=tile):
                e.wait_ge(SV, C["v"])
                e.dma_start(out=outT[:, ts(tile, T)].rearrange("(k p) t -> p k t", p=P),
                            in_=x1_s[:].rearrange("p (k t) -> p k t", k=KC)).then_inc(SO, 16)
            C["o"] += 16
            out_stored = ("o", C["o"])
            x1_free = ("o", C["o"])

        def emit_wout(ctx_pair_toks, tile):
            nonlocal ctxb_free, x_free
            x1_chunks = []
            w_loaded = slot_loaded[wout_slot[0]]
            for m in range(KC):
                bank = m % 2
                @blk.tensor
                def _(e, m=m, bank=bank, w_loaded=w_loaded):
                    if m == 0:
                        e.wait_ge(SL[wout_slot[0]], w_loaded)
                    wt(e, pbank_free[bank])
                    ins = None
                    for k in range(KC):
                        if m == 0:
                            e.wait_ge(SV, ctx_pair_toks[k])
                        ins = e.matmul(out=pb[bank][:],
                                       lhsT=wsl[wout_slot[0]][:, k * D + m * P: k * D + (m + 1) * P],
                                       rhs=ctxb[:, ts(k, T)], start=(k == 0), stop=(k == KC - 1))
                    t_inc(ins)
                wmm = C["t"]
                if m == KC - 1:
                    ctxb_free = ("t", wmm)

                @blk.vector
                def _(e, m=m, bank=bank, wmm=wmm):
                    e.wait_ge(STn, wmm)
                    if m == 0:
                        wt(e, x1_free)
                    v_inc(e.scalar_tensor_tensor(out=x1_s[:, ts(m, T)], in0=pb[bank][:],
                                                 scalar=bout_s[:, m:m + 1], in1=xbuf[:, ts(m, T)],
                                                 op0=ALU.add, op1=ALU.add))
                pbank_free[bank] = ("v", C["v"])
                x1_chunks.append(C["v"])
            x_free = ("v", C["v"])
            return C["v"], x1_chunks

        x_loaded = XCNT[0]

        wout_slot = [0]
        w12_pre = [None, None]

        def dbg_barrier():
            @blk.tensor
            def _(e):
                e.wait_ge(SV, C["v"]); e.wait_ge(SS, C["s"]); e.wait_ge(SI, C["i"]); e.wait_ge(SO, C["o"])
                t_inc(e.engine_nop() if hasattr(e, "engine_nop") else e.nop())
            tcnt = C["t"]
            @blk.vector
            def _(e, tcnt=tcnt):
                e.wait_ge(STn, tcnt)
                v_inc(e.engine_nop())
            vcnt = C["v"]
            @blk.scalar
            def _(e, tcnt=tcnt, vcnt=vcnt):
                e.wait_ge(STn, tcnt); e.wait_ge(SV, vcnt)
                s_inc(e.nop())
            scnt = C["s"]
            @blk.sync
            def _(e, tcnt=tcnt, vcnt=vcnt, scnt=scnt):
                e.wait_ge(STn, tcnt); e.wait_ge(SV, vcnt); e.wait_ge(SS, scnt)

        def do_slice(kind, pos, tile):
            if kind == "w12" and pos in (0, 1) and w12_pre[pos] is not None:
                jj = w12_pre[pos]
                w12_pre[pos] = None
            else:
                jj = load_slice(kind, pos)
            if kind == "w12":
                emit_w12_slice(jj, pos)
            elif kind in ("h1", "h2"):
                emit_hmul_slice(jj, kind, pos)
            else:
                emit_w3_slice(jj, pos, tile)
            slot_free[jj] = ("t", C["t"])

        for i in range(NT):
            ffn_slices = []
            if i > 0:
                ffn_slices = [("w12", mr) for mr in range(8)] + \
                             [("h1", s_) for s_ in range(16)] + \
                             [("h2", s_) for s_ in range(16)]

            pre = [("x", x_loaded), xln_free]
            mid = None
            if i > 0:
                def mid(i=i):
                    kind, pos = ffn_slices[0]
                    do_slice(kind, pos, i - 1)
            ln1_done, ln1_chunks = emit_ln(xbuf, xln_s, pre, mid_cb=mid)

            qkv_done = 0
            fidx = 1 if i > 0 else 0
            for sl in range(3):
                if fidx < 8 and i > 0:
                    kind, pos = ffn_slices[fidx]; fidx += 1
                    do_slice(kind, pos, i - 1)
                jq = load_slice("qkv", sl, direct=(i == 0))
                qkv_done = emit_qkv_slice(jq, sl, ln1_chunks)
                slot_free[jq] = ("t", C["t"])
            xln_free = ("t", C["t"])
            if i == 0:
                emit_casts()

            while fidx < 8 and i > 0:
                kind, pos = ffn_slices[fidx]; fidx += 1
                do_slice(kind, pos, i - 1)

            pend = []

            def queue_ffn():
                nonlocal fidx
                if i > 0 and fidx < len(ffn_slices):
                    kind, pos = ffn_slices[fidx]; fidx += 1
                    jj = load_slice(kind, pos)
                    pend.append([jj, kind, pos, 0])
                    pend.append([jj, kind, pos, 1])

            def run_ffn_mc():
                if pend:
                    jj, kind, pos, mc = pend.pop(0)
                    mm = hmul_mc(jj, kind, pos, mc)
                    if mc == 1:
                        finish_hmul(kind, pos, mm)
                        slot_free[jj] = ("t", C["t"])

            for h in range(NH + 2):
                if h < NH:
                    attn_V1(h)
                    queue_ffn()
                    run_ffn_mc()
                    attn_sc(h, qkv_done)
                    attn_S1(h)
                if 1 <= h <= NH:
                    hh = h - 1
                    attn_erep_pair(hh, 0)
                    attn_tmp2_pair(hh, 0)
                    run_ffn_mc()
                    attn_erep_pair(hh, 1)
                    attn_u_pair(hh, 0)
                    attn_tmp2_pair(hh, 1)
                    queue_ffn()
                    run_ffn_mc()
                    attn_erep_pair(hh, 2)
                    attn_u_pair(hh, 1)
                    attn_tmp2_pair(hh, 2)
                    attn_S2a(hh)
                    run_ffn_mc()
                    attn_erep_pair(hh, 3)
                    attn_u_pair(hh, 2)
                    attn_tmp2_pair(hh, 3)
                    run_ffn_mc()
                    attn_u_pair(hh, 3)
                if 2 <= h <= NH + 1:
                    attn_rrep(h - 2)
                    attn_S2b(h - 2)
                    attn_ctx(h - 2)
            ctx_done = A[NH - 1]["ctx"]
            qkv_free = ("v", ctx_done)
            pbank_free[2] = ("v", A[NH - 1]["ctx"])
            pbank_free[3] = ("v", A[NH - 1]["tmp2_done"])
            while pend:
                run_ffn_mc()
            while i > 0 and fidx < len(ffn_slices):
                kind, pos = ffn_slices[fidx]; fidx += 1
                do_slice(kind, pos, i - 1)

            if i > 0:
                for s_ in range(4):
                    do_slice("w3", s_, i - 1)
            jw = load_slice("wout", 0)
            wout_slot[0] = jw
            wout_loaded = slot_loaded[jw]
            w12_pre[0] = load_slice("w12", 0)
            if i > 0:
                emit_out_store(i - 1)

            x1_done, x1_chunks_w = emit_wout([A[2 * k_ + 1]["ctx"] for k_ in range(KC)], i)
            slot_free[wout_slot[0]] = ("t", C["t"])

            if i + 1 < NT:
                @blk.sync
                def _(e, i=i):
                    wt(e, x_free)
                    e.dma_start(out=xbuf[:].rearrange("p (k t) -> p k t", k=KC),
                                in_=xT[:, ts(i + 1, T)].rearrange("(k p) t -> p k t", p=P)).then_inc(SX, 16)
                XCNT[0] += 16
                x_loaded = XCNT[0]
            w12_pre[1] = load_slice("w12", 1)

            ln2_done, ln2_chunks = emit_ln(x1_s, x1ln_s, [x1ln_free, ("v", x1_chunks_w[0])],
                                           pre_chunks=[("v", t_) for t_ in x1_chunks_w])
            x1ln_ready_box[0] = ("v", ln2_done)
            x1ln_chunks_box[0] = ln2_chunks
            if DEBUG_BARRIER:
                dbg_barrier()

        @blk.tensor
        def _(e, ln2_done=ln2_done):
            e.wait_ge(SV, ln2_done)
        ffn_slices = [("w12", mr) for mr in range(8)] + \
                     [("h1", s_) for s_ in range(16)] + \
                     [("h2", s_) for s_ in range(16)]
        for kind, pos in ffn_slices:
            do_slice(kind, pos, NT - 1)
        for s_ in range(4):
            do_slice("w3", s_, NT - 1)
        emit_out_store(NT - 1)

        @blk.sync
        def _(e):
            e.wait_ge(SO, C["o"])

    return nc


_PROG_CACHE = {}

def _get_prog():
    if "nc" not in _PROG_CACHE:
        _PROG_CACHE["nc"] = build_program()
    return _PROG_CACHE["nc"]


def _consts():
    bf = ml_dtypes.bfloat16
    dsel = np.zeros((P, P), bf)
    for p in range(P):
        dsel[p, p % DK] = 1
    bdsel = np.zeros((P, KC * P), bf)
    gsel = np.zeros((P, KC * P), bf)
    for c in range(KC):
        for p in range(P):
            gg = 2 * c + p // DK
            bdsel[p, c * P + gg] = 1
            gsel[gg, c * P + p] = 1
    densel = np.zeros((P, P), bf); densel[0:NH, 32] = 1
    rsel = np.zeros((P, P), bf); rsel[32, :] = 1
    brow = np.zeros((P, P), bf); brow[0, :] = 1
    col0 = np.zeros((P, P), bf); col0[:, 0] = 1
    eps = np.full((P, 1), EPS, np.float32)
    return dsel, bdsel, gsel, densel, rsel, brow, col0, eps


def kernel(L0_attn_x, L0_attn_W_qkv, L0_attn_b_qkv, L0_attn_W_out, L0_attn_b_out,
           L0_ffn_w1_w, L0_ffn_w1_s, L0_ffn_w2_w, L0_ffn_w2_s,
           L0_ffn_h1_w, L0_ffn_h1_s, L0_ffn_h2_w, L0_ffn_h2_s,
           L0_ffn_w3_w, L0_ffn_w3_s):
    x = np.asarray(L0_attn_x)
    B = x.shape[0]
    wq = np.asarray(L0_attn_W_qkv)
    wq3 = wq.reshape(D, NH, 3, DK)
    wqkv = np.ascontiguousarray(wq3.transpose(0, 2, 1, 3).reshape(D, 3 * D))
    bq3 = np.asarray(L0_attn_b_qkv).reshape(NH, 3, DK)
    bqkv = np.ascontiguousarray(bq3.transpose(1, 0, 2).reshape(3 * D // P, P).T)
    bout = np.ascontiguousarray(np.asarray(L0_attn_b_out).reshape(D // P, P).T)
    w1T = np.ascontiguousarray(np.asarray(L0_ffn_w1_w).T)
    w2T = np.ascontiguousarray(np.asarray(L0_ffn_w2_w).T)
    h1T = np.ascontiguousarray(np.asarray(L0_ffn_h1_w).T)
    h2T = np.ascontiguousarray(np.asarray(L0_ffn_h2_w).T)
    w3T = np.ascontiguousarray(np.asarray(L0_ffn_w3_w).T)
    s1 = np.ascontiguousarray(np.asarray(L0_ffn_w1_s).reshape(HC, P).T)
    s2 = np.ascontiguousarray(np.asarray(L0_ffn_w2_s).reshape(HC, P).T)
    sh1 = np.ascontiguousarray(np.asarray(L0_ffn_h1_s).reshape(HC, P).T)
    sh2 = np.ascontiguousarray(np.asarray(L0_ffn_h2_s).reshape(HC, P).T)
    s3 = np.ascontiguousarray(np.asarray(L0_ffn_w3_s).reshape(D // P, P).T)
    dsel, bdsel, gsel, densel, rsel, brow, col0, eps = _consts()
    cb16 = np.ascontiguousarray(np.concatenate(
        [dsel, bdsel, gsel, densel, rsel, brow, col0], axis=1))
    cb32 = np.ascontiguousarray(np.concatenate(
        [eps, bqkv, bout, s1, s2, sh1, sh2, s3], axis=1))

    shared = dict(wqkv=wqkv, wout=np.ascontiguousarray(np.asarray(L0_attn_W_out)),
                  w1T=w1T, w2T=w2T, h1T=h1T, h2T=h2T, w3T=w3T,
                  cb16=cb16, cb32=cb32)

    in_maps = []
    for c in range(B):
        m = dict(shared)
        m["xT"] = np.ascontiguousarray(x[c].T)
        in_maps.append(m)

    nc = _get_prog()
    res = run_bass_kernel_spmd(nc, in_maps, core_ids=list(range(N_CORES)))
    out = np.stack([np.asarray(r["outT"], dtype=np.float32).T for r in res.results], axis=0)
    return out


# version 23
# speedup vs baseline: 1.1159x; 1.0098x over previous
import numpy as np
import ml_dtypes
from contextlib import ExitStack

import concourse.bass as bass
import concourse.mybir as mybir
from concourse.bass_utils import run_bass_kernel_spmd

F32 = mybir.dt.float32
BF16 = mybir.dt.bfloat16
AF = mybir.ActivationFunctionType
ALU = mybir.AluOpType
ts = bass.ts

P = 128
D = 1024
DH = 4096
NH, DK = 16, 64
S = 4096
T = 512
NT = S // T
KC = D // P
HC = DH // P
N_CORES = 8
EPS = 1e-5
DEBUG_BARRIER = False
SLOT = 8192


def build_program():
    nc = bass.Bass("TRN2", target_bir_lowering=False)

    xT = nc.dram_tensor("xT", [D, S], F32, kind="ExternalInput").ap()
    wqkv = nc.dram_tensor("wqkv", [D, 3 * D], F32, kind="ExternalInput").ap()
    wout = nc.dram_tensor("wout", [D, D], F32, kind="ExternalInput").ap()
    w1T = nc.dram_tensor("w1T", [D, DH], F32, kind="ExternalInput").ap()
    w2T = nc.dram_tensor("w2T", [D, DH], F32, kind="ExternalInput").ap()
    h1T = nc.dram_tensor("h1T", [DH, DH], F32, kind="ExternalInput").ap()
    h2T = nc.dram_tensor("h2T", [DH, DH], F32, kind="ExternalInput").ap()
    w3T = nc.dram_tensor("w3T", [DH, D], F32, kind="ExternalInput").ap()
    cb16 = nc.dram_tensor("cb16", [P, 2688], BF16, kind="ExternalInput").ap()
    cb32 = nc.dram_tensor("cb32", [P, 169], F32, kind="ExternalInput").ap()

    outT = nc.dram_tensor("outT", [D, S], F32, kind="ExternalOutput").ap()

    wbqkv_d = nc.dram_tensor("wbqkv_d", [D, 3 * D], BF16, kind="Internal").ap()
    wbout_d = nc.dram_tensor("wbout_d", [D, D], BF16, kind="Internal").ap()
    wb1_d = nc.dram_tensor("wb1_d", [D, DH], BF16, kind="Internal").ap()
    wb2_d = nc.dram_tensor("wb2_d", [D, DH], BF16, kind="Internal").ap()
    hb1_d = nc.dram_tensor("hb1_d", [DH, DH], BF16, kind="Internal").ap()
    hb2_d = nc.dram_tensor("hb2_d", [DH, DH], BF16, kind="Internal").ap()
    wb3_d = nc.dram_tensor("wb3_d", [DH, D], BF16, kind="Internal").ap()

    with ExitStack() as g:
        E = g.enter_context
        cb16_s = E(nc.sbuf_tensor("cb16_s", [P, 2688], BF16))
        cb32_s = E(nc.sbuf_tensor("cb32_s", [P, 169], F32))
        dsel_s = cb16_s[:, 0:P]
        bdsel_s = cb16_s[:, P:P + KC * P]
        gsel_s = cb16_s[:, P + KC * P:P + 2 * KC * P]
        densel_s = cb16_s[:, 2176:2304]
        rsel_s = cb16_s[:, 2304:2432]
        brow_s = cb16_s[:, 2432:2560]
        col0_s = cb16_s[:, 2560:2688]
        eps_s = cb32_s[:, 0:1]
        bqkv_s = cb32_s[:, 1:25]
        bout_s = cb32_s[:, 25:33]
        s1_s = cb32_s[:, 33:65]
        s2_s = cb32_s[:, 65:97]
        sh1_s = cb32_s[:, 97:129]
        sh2_s = cb32_s[:, 129:161]
        s3_s = cb32_s[:, 161:169]

        wsl = [E(nc.sbuf_tensor(f"wsl{j}", [P, SLOT], BF16)) for j in range(2)]
        xbuf = E(nc.sbuf_tensor("xbuf", [P, KC * T], F32))
        xln_s = E(nc.sbuf_tensor("xln_s", [P, KC * T], BF16))
        x1ln_s = E(nc.sbuf_tensor("x1ln_s", [P, KC * T], BF16))
        q_s = E(nc.sbuf_tensor("q_s", [P, KC * T], BF16))
        k_s = E(nc.sbuf_tensor("k_s", [P, KC * T], BF16))
        v_s = E(nc.sbuf_tensor("v_s", [P, KC * T], BF16))
        qrep = E(nc.sbuf_tensor("qrep", [P, T], BF16))
        tmpa = E(nc.sbuf_tensor("tmpa", [P, KC * T], BF16))
        esb = [E(nc.sbuf_tensor(f"esb{j}", [P, T], BF16)) for j in range(2)]
        tmp2 = [E(nc.sbuf_tensor(f"tmp2_{j}", [P, T], BF16)) for j in range(2)]
        rband = E(nc.sbuf_tensor("rband", [33, T], F32))
        rbandb = E(nc.sbuf_tensor("rbandb", [P, 2 * T], BF16))
        scr = E(nc.sbuf_tensor("scr", [1, 2 * T], F32))
        rowb = E(nc.sbuf_tensor("rowb", [P, T], BF16))

        sA = [E(nc.sbuf_tensor(f"sA{j}", [P, T], BF16)) for j in range(2)]
        xbb = [E(nc.sbuf_tensor("xbb0", [P, T], BF16)), qrep]
        ctxb = E(nc.sbuf_tensor("ctxb", [P, KC * T], BF16))
        x1_s = E(nc.sbuf_tensor("x1_s", [P, KC * T], F32))
        hB = E(nc.sbuf_tensor("hB", [P, HC * T], BF16))
        g1B = E(nc.sbuf_tensor("g1B", [P, HC * T], BF16))

        pb = [E(nc.psum_tensor(f"pb{i}", [P, T], F32)) for i in range(8)]

        SW = E(nc.semaphore("SW"))
        SI = E(nc.semaphore("SI"))
        SO = E(nc.semaphore("SO"))
        STn = E(nc.semaphore("STn"))
        SV = E(nc.semaphore("SV"))
        SS = E(nc.semaphore("SS"))
        WSEM = {name: E(nc.semaphore(f"W_{name}")) for name in
                ("qkv", "wout", "w1", "w2", "h1", "h2", "w3")}
        WCNT = {name: 0 for name in WSEM}
        SL = [E(nc.semaphore(f"SL{j}")) for j in range(2)]
        SX = E(nc.semaphore("SX"))
        blk = E(nc.Block())
        C = dict(w=0, i=0, o=0, t=0, v=0, s=0)
        SEMS = {"t": STn, "v": SV, "s": SS, "i": SI, "o": SO, "w": SW, "x": None}

        def t_inc(instr):
            C["t"] += 1
            return instr.then_inc(STn, 1)

        def v_inc(instr):
            C["v"] += 1
            return instr.then_inc(SV, 1)

        def s_inc(instr):
            C["s"] += 1
            return instr.then_inc(SS, 1)

        def wt(e, tok):
            if tok is None:
                return
            if tok[1] > 0:
                sem = SX if tok[0] == "x" else SEMS[tok[0]]
                e.wait_ge(sem, tok[1])

        XCNT = [0]
        @blk.sync
        def _(e):
            e.dma_start(out=xbuf[:].rearrange("p (k t) -> p k t", k=KC),
                        in_=xT[:, 0:T].rearrange("(k p) t -> p k t", p=P)).then_inc(SX, 16)
            for dst, src in [(cb16_s[:], cb16), (cb32_s[:], cb32)]:
                e.dma_start(out=dst, in_=src).then_inc(SI, 16)
                C["i"] += 16
        XCNT[0] += 16
        consts_loaded = C["i"]

        @blk.tensor
        def _(e):
            e.wait_ge(SI, consts_loaded)
        @blk.scalar
        def _(e):
            e.wait_ge(SI, consts_loaded)

        @blk.vector
        def _(e):
            e.wait_ge(SI, consts_loaded)
            e.memset(esb[0][:], 0.0)
            e.memset(esb[1][:], 0.0)
            e.memset(rbandb[:], 0.0)
            v_inc(e.memset(rowb[:], 0.0))
        memset_done = C["v"]

        def emit_casts():
            @blk.gpsimd
            def _(e):
                for name, src_, dst, nrows in [("wout", wout, wbout_d, D),
                                               ("w1", w1T, wb1_d, D), ("w2", w2T, wb2_d, D),
                                               ("qkv", wqkv, wbqkv_d, D)]:
                    for r0 in range(0, nrows, 256):
                        e.dma_start(out=dst[r0:r0 + 256, :], in_=src_[r0:r0 + 256, :]).then_inc(WSEM[name], 16)
                        WCNT[name] += 16
                    e.wait_ge(WSEM[name], WCNT[name])
                for name, src_, dst, cstep in [("h1", h1T, hb1_d, 512), ("h2", h2T, hb2_d, 512),
                                               ("w3", w3T, wb3_d, 256)]:
                    ncols = src_.shape[1]
                    for c0 in range(0, ncols, cstep):
                        if c0 > 0:
                            e.wait_ge(WSEM[name], WCNT[name])
                        e.dma_start(out=dst[:, c0:c0 + cstep], in_=src_[:, c0:c0 + cstep]).then_inc(WSEM[name], 16)
                        WCNT[name] += 16
                    if name != "w3":
                        e.wait_ge(WSEM[name], WCNT[name])

        x_loaded = 0
        x_free = None
        x1_free = None
        x1_store = None
        xln_free = None
        x1ln_free = None
        qkv_free = None
        ctxb_free = None
        rowb_free = None
        pbank_free = [None] * 8
        slot_free = [None, None]
        slot_loaded = [0, 0]
        slot_idx = [0]
        tmpa_free = [0, 0]
        esb_free = [0, 0]
        psS_free = [None, None]
        rbandb_free = [0, 0]
        tmp2_free = [0, 0]
        sA_free = [None, None]
        hB_free = None
        g1_freeB = None
        h_ready = None
        g1_ready = None
        g2_ready = None
        hB_mm_last = 0
        out_stored = None
        x1ln_ready_box = [None]
        x1ln_chunks_box = [None]

        SLCNT = [0, 0]

        def load_slice(kind, pos, direct=False):
            j = slot_idx[0] % 2
            slot_idx[0] += 1
            if direct:
                @blk.gpsimd
                def _(e, pos=pos, j=j):
                    wt(e, slot_free[j])
                    e.dma_start(out=wsl[j][:].rearrange("p (k m) -> p k m", k=KC),
                                in_=wqkv[:, ts(pos, D)].rearrange("(k p) m -> p k m", p=P)).then_inc(SL[j], 16)
                SLCNT[j] += 16
                slot_loaded[j] = SLCNT[j]
                return j
            @blk.sync
            def _(e, kind=kind, pos=pos, j=j):
                wt(e, slot_free[j])
                if kind == "w12":
                    e.wait_ge(WSEM["w1"], WCNT["w1"])
                    e.wait_ge(WSEM["w2"], WCNT["w2"])
                elif kind in ("h1", "h2"):
                    e.wait_ge(WSEM[kind], 16 * (pos // 2 + 1))
                elif kind == "w3":
                    e.wait_ge(WSEM["w3"], 16 * (pos + 1))
                else:
                    wn = {"qkv": "qkv", "wout": "wout"}[kind]
                    e.wait_ge(WSEM[wn], WCNT[wn])
                if kind == "qkv":
                    e.dma_start(out=wsl[j][:].rearrange("p (k m) -> p k m", k=KC),
                                in_=wbqkv_d[:, ts(pos, D)].rearrange("(k p) m -> p k m", p=P)).then_inc(SL[j], 16)
                    SLCNT[j] += 16
                elif kind == "wout":
                    e.dma_start(out=wsl[j][:].rearrange("p (k m) -> p k m", k=KC),
                                in_=wbout_d.rearrange("(k p) m -> p k m", p=P)).then_inc(SL[j], 16)
                    SLCNT[j] += 16
                elif kind == "w12":
                    e.dma_start(out=wsl[j][:, 0:KC * T].rearrange("p (k m) -> p k m", k=KC),
                                in_=wb1_d[:, ts(pos, T)].rearrange("(k p) m -> p k m", p=P)).then_inc(SL[j], 16)
                    e.wait_ge(SL[j], SLCNT[j] + 16)
                    e.dma_start(out=wsl[j][:, KC * T:2 * KC * T].rearrange("p (k m) -> p k m", k=KC),
                                in_=wb2_d[:, ts(pos, T)].rearrange("(k p) m -> p k m", p=P)).then_inc(SL[j], 16)
                    SLCNT[j] += 32
                else:
                    src = {"h1": hb1_d, "h2": hb2_d, "w3": wb3_d}[kind]
                    half = HC // 2
                    e.dma_start(out=wsl[j][:, 0:half * 256].rearrange("p (k m) -> p k m", k=half),
                                in_=src[0:half * P, ts(pos, 256)].rearrange("(k p) m -> p k m", p=P)
                                ).then_inc(SL[j], 16)
                    e.wait_ge(SL[j], SLCNT[j] + 16)
                    e.dma_start(out=wsl[j][:, half * 256:HC * 256].rearrange("p (k m) -> p k m", k=half),
                                in_=src[half * P:HC * P, ts(pos, 256)].rearrange("(k p) m -> p k m", p=P)
                                ).then_inc(SL[j], 16)
                    SLCNT[j] += 32
            slot_loaded[j] = SLCNT[j]
            return j

        def emit_ln(src_f32, dst_bf, pre_toks, pre_chunks=None, mid_cb=None):
            nonlocal rowb_free
            mm_tok = [0, 0]
            first = [True]
            for c in range(KC):
                @blk.vector
                def _(e, c=c):
                    if first[0]:
                        for tok in pre_toks:
                            wt(e, tok)
                        first[0] = False
                    else:
                        wt(e, pre_chunks[c] if pre_chunks else None)
                    if mm_tok[c % 2] > 0:
                        e.wait_ge(STn, mm_tok[c % 2])
                    if tmp2_free[c % 2] > 0:
                        e.wait_ge(STn, tmp2_free[c % 2])
                    e.tensor_copy(out=xbb[c % 2][:], in_=src_f32[:, ts(c, T)])
                    v_inc(e.tensor_tensor(out=tmp2[c % 2][:], in0=xbb[c % 2][:], in1=xbb[c % 2][:], op=ALU.mult))
                sq_done = C["v"]

                @blk.tensor
                def _(e, c=c, sq_done=sq_done):
                    e.wait_ge(SV, sq_done)
                    if c == 0:
                        wt(e, pbank_free[4])
                        wt(e, pbank_free[5])
                    e.matmul(out=pb[4][:], lhsT=col0_s[:], rhs=xbb[c % 2][:],
                             start=(c == 0), stop=(c == KC - 1), skip_group_check=True)
                    t_inc(e.matmul(out=pb[5][:], lhsT=col0_s[:], rhs=tmp2[c % 2][:],
                                   start=(c == 0), stop=(c == KC - 1), skip_group_check=True))
                mm_tok[c % 2] = C["t"]
                tmp2_free[c % 2] = C["t"]
            stats_done = C["t"]
            if mid_cb is not None:
                mid_cb()

            m_row = scr[0:1, 0:T]
            v_row = scr[0:1, T:2 * T]
            w_row = rband[0:1, :]

            @blk.vector
            def _(e):
                e.wait_ge(STn, stats_done)
                e.tensor_scalar(out=m_row, in0=pb[4][0:1, :], scalar1=1.0 / D, scalar2=None, op0=ALU.mult)
                e.tensor_scalar(out=v_row, in0=pb[5][0:1, :], scalar1=1.0 / D, scalar2=None, op0=ALU.mult)
                e.tensor_tensor(out=w_row, in0=m_row, in1=m_row, op=ALU.mult)
                v_inc(e.tensor_tensor(out=v_row, in0=v_row, in1=w_row, op=ALU.subtract))
            var_done = C["v"]
            pbank_free[4] = ("v", var_done)
            pbank_free[5] = ("v", var_done)

            @blk.scalar
            def _(e):
                e.wait_ge(SV, var_done)
                e.activation(out=w_row, in_=v_row, func=AF.Ln, bias=eps_s[0:1, :])
                s_inc(e.activation(out=w_row, in_=w_row, func=AF.Exp, scale=-0.5))
            r_done = C["s"]

            @blk.vector
            def _(e):
                e.wait_ge(SS, r_done)
                wt(e, rowb_free)
                e.tensor_tensor(out=v_row, in0=m_row, in1=w_row, op=ALU.mult)
                e.tensor_copy(out=rowb[0:1, :], in_=w_row)
                v_inc(e.tensor_copy(out=rowb[32:33, :], in_=v_row))
            rowb_done = C["v"]

            @blk.tensor
            def _(e):
                e.wait_ge(SV, rowb_done)
                wt(e, pbank_free[2])
                wt(e, pbank_free[3])
                e.matmul(out=pb[2][:], lhsT=brow_s[:], rhs=rowb[:], start=True, stop=True)
                t_inc(e.matmul(out=pb[3][:], lhsT=rsel_s[:], rhs=rowb[:], start=True, stop=True))
            bcast_done = C["t"]
            rowb_free = ("t", bcast_done)

            apply_chunks = []
            for c in range(KC):
                @blk.vector
                def _(e, c=c):
                    if c == 0:
                        e.wait_ge(STn, bcast_done)
                    e.tensor_tensor(out=xbb[0][:], in0=src_f32[:, ts(c, T)], in1=pb[2][:], op=ALU.mult)
                    v_inc(e.tensor_tensor(out=dst_bf[:, ts(c, T)], in0=xbb[0][:], in1=pb[3][:], op=ALU.subtract))
                apply_chunks.append(C["v"])
            apply_done = C["v"]
            pbank_free[2] = ("v", apply_done)
            pbank_free[3] = ("v", apply_done)
            return apply_done, apply_chunks

        def emit_qkv_slice(jslot, sl, ln1_chunks):
            nonlocal qkv_free
            for mc in range(KC):
                m = sl * KC + mc
                bank = 4 + m % 2
                @blk.tensor
                def _(e, jslot=jslot, sl=sl, mc=mc, m=m, bank=bank):
                    if mc == 0:
                        e.wait_ge(SL[jslot], slot_loaded[jslot])
                    wt(e, pbank_free[bank])
                    ins = None
                    for k in range(KC):
                        if sl == 0 and mc == 0:
                            e.wait_ge(SV, ln1_chunks[k])
                        ins = e.matmul(out=pb[bank][:],
                                       lhsT=wsl[jslot][:, k * D + mc * P: k * D + (mc + 1) * P],
                                       rhs=xln_s[:, ts(k, T)], start=(k == 0), stop=(k == KC - 1))
                    t_inc(ins)
                mmd = C["t"]

                dst = (q_s, k_s, v_s)[m // KC]
                col = (m % KC) * T
                @blk.vector
                def _(e, m=m, bank=bank, dst=dst, col=col, mmd=mmd):
                    e.wait_ge(STn, mmd)
                    if m == 0:
                        wt(e, qkv_free)
                    v_inc(e.tensor_scalar(out=dst[:, col:col + T], in0=pb[bank][:],
                                          scalar1=bqkv_s[:, m:m + 1], scalar2=None, op0=ALU.add))
                pbank_free[bank] = ("v", C["v"])
            return C["v"]

        A = {}

        def attn_V1(h):
            c_q, r_q = h // 2, (h % 2) * DK
            prev_sc = A[h - 1]["sc"] if h >= 1 else (A[NH - 1]["sc"] if (NH - 1) in A and "sc" in A[NH - 1] else 0)
            @blk.vector
            def _(e, h=h, c_q=c_q, r_q=r_q, prev_sc=prev_sc):
                if prev_sc > 0:
                    e.wait_ge(STn, prev_sc)
                e.tensor_copy(out=qrep[0:DK, :], in_=q_s[r_q:r_q + DK, ts(c_q, T)])
                e.tensor_copy(out=qrep[DK:P, :], in_=q_s[r_q:r_q + DK, ts(c_q, T)])
                ins = None
                for c in range(KC):
                    ins = e.tensor_tensor(out=tmpa[:, ts(c, T)], in0=k_s[:, ts(c, T)],
                                          in1=qrep[:], op=ALU.mult)
                v_inc(ins)
            A[h] = {"erep": {}, "tmp2": {}}
            A[h]["tmp"] = C["v"]

        def attn_sc(h, qkv_done):
            sb = 6 + (h % 2)
            @blk.tensor
            def _(e, h=h, sb=sb, qkv_done=qkv_done):
                e.wait_ge(SV, A[h]["tmp"])
                if h >= 2:
                    e.wait_ge(SS, A[h - 2]["r"])
                ins = None
                for c in range(KC):
                    ins = e.matmul(out=pb[sb][:], lhsT=bdsel_s[:, ts(c, P)],
                                   rhs=tmpa[:, ts(c, T)],
                                   start=(c == 0), stop=(c == KC - 1), skip_group_check=True)
                t_inc(ins)
            A[h]["sc"] = C["t"]

        def attn_S1(h):
            sb = 6 + (h % 2)
            @blk.scalar
            def _(e, h=h, sb=sb):
                e.wait_ge(STn, A[h]["sc"])
                if h >= 2:
                    e.wait_ge(STn, A[h - 2]["erep"][KC - 1])
                e.activation(out=rband[0:1, 0:1], in_=eps_s[0:1, :], func=AF.Ln)
                s_inc(e.activation(out=esb[h % 2][0:NH, :], in_=pb[sb][0:NH, :],
                                   func=AF.Exp, scale=0.125))
            A[h]["exp"] = C["s"]

        def attn_erep_pair(h, j):
            sb = 6 + (h % 2)
            cpair = (2 * j, 2 * j + 1)
            if j == 0:
                tok2 = ("v", A[h - 2]["ctx"]) if h >= 2 else pbank_free[2]
                tok3 = ("v", A[h - 1]["tmp2"][7]) if h >= 1 else pbank_free[3]
            else:
                tok2 = ("v", A[h]["tmp2"][2 * j - 2])
                tok3 = ("v", A[h]["tmp2"][2 * j - 1])
            @blk.tensor
            def _(e, h=h, j=j, cpair=cpair, sb=sb, tok2=tok2, tok3=tok3):
                if j == 0:
                    e.wait_ge(SS, A[h]["exp"])
                    if h == 0:
                        e.wait_ge(SV, memset_done)
                wt(e, tok2)
                wt(e, tok3)
                for c in cpair:
                    gp = c % 2
                    t_inc(e.matmul(out=pb[2 + gp][:], lhsT=gsel_s[:, ts(c, P)],
                                   rhs=esb[h % 2][:], start=True, stop=True))
                    A[h]["erep"][c] = C["t"]
                if j == 0:
                    t_inc(e.matmul(out=pb[sb][:], lhsT=densel_s[:],
                                   rhs=esb[h % 2][:], start=True, stop=True))
                    A[h]["den"] = C["t"]

        def attn_tmp2_pair(h, j):
            cpair = (2 * j, 2 * j + 1)
            if j == 0:
                utok = A[h - 1]["u"] if h >= 1 else 0
            else:
                utok = A[h]["upair"][j - 1]
            @blk.vector
            def _(e, h=h, cpair=cpair, utok=utok):
                for c in cpair:
                    gp = c % 2
                    e.wait_ge(STn, A[h]["erep"][c])
                    if utok > 0:
                        e.wait_ge(STn, utok)
                    v_inc(e.tensor_tensor(out=tmp2[gp][:], in0=pb[2 + gp][:],
                                          in1=v_s[:, ts(c, T)], op=ALU.mult))
                    A[h]["tmp2"][c] = C["v"]
            if j == 3:
                A[h]["tmp2_done"] = C["v"]

        def attn_S2a(h):
            sb = 6 + (h % 2)
            @blk.scalar
            def _(e, h=h, sb=sb):
                e.wait_ge(STn, A[h]["den"])
                if h >= 2:
                    e.wait_ge(STn, A[h - 2]["rrep"])
                e.activation(out=rband[32:33, :], in_=pb[sb][32:33, :], func=AF.Ln)
                s_inc(e.activation(out=rbandb[32:33, ts(h % 2, T)], in_=rband[32:33, :],
                                   func=AF.Exp, scale=-1.0))
            A[h]["r"] = C["s"]

        def attn_u_pair(h, j):
            ub = 4 + (h % 2)
            cpair = (2 * j, 2 * j + 1)
            @blk.tensor
            def _(e, h=h, ub=ub, cpair=cpair):
                for c in cpair:
                    gp = c % 2
                    e.wait_ge(SV, A[h]["tmp2"][c])
                    if c == 0:
                        wt(e, pbank_free[4 + h % 2])
                    t_inc(e.matmul(out=pb[ub][:], lhsT=dsel_s[:], rhs=tmp2[gp][:],
                                   start=(c == 0), stop=(c == KC - 1), skip_group_check=True))
            A[h].setdefault("upair", {})[j] = C["t"]
            if j == 3:
                A[h]["u"] = C["t"]

        def attn_rrep(h):
            if (h + 1) in A and 6 in A[h + 1]["tmp2"]:
                tok = ("v", A[h + 1]["tmp2"][6])
            elif h >= 1 and "ctx" in A[h - 1]:
                tok = ("v", A[h - 1]["ctx"])
            else:
                tok = None
            @blk.tensor
            def _(e, h=h, tok=tok):
                e.wait_ge(SS, A[h]["r"])
                wt(e, tok)
                t_inc(e.matmul(out=pb[2][:], lhsT=rsel_s[:],
                               rhs=rbandb[:, ts(h % 2, T)], start=True, stop=True))
            A[h]["rrep"] = C["t"]

        def attn_S2b(h):
            c_q, r_q = h // 2, (h % 2) * DK
            ub = 4 + (h % 2)
            @blk.scalar
            def _(e, h=h, ub=ub, c_q=c_q, r_q=r_q):
                e.wait_ge(STn, A[h]["u"])
                if h < 2:
                    wt(e, ctxb_free)
                s_inc(e.activation(out=ctxb[r_q:r_q + DK, ts(c_q, T)], in_=pb[ub][0:DK, :],
                                   func=AF.Copy))
            A[h]["evac"] = C["s"]
            pbank_free[4 + h % 2] = ("s", A[h]["evac"])

        def attn_ctx(h):
            c_q, r_q = h // 2, (h % 2) * DK
            @blk.vector
            def _(e, h=h, c_q=c_q, r_q=r_q):
                e.wait_ge(STn, A[h]["rrep"])
                e.wait_ge(SS, A[h]["evac"])
                if h < 2:
                    wt(e, ctxb_free)
                v_inc(e.tensor_tensor(out=ctxb[r_q:r_q + DK, ts(c_q, T)],
                                      in0=ctxb[r_q:r_q + DK, ts(c_q, T)],
                                      in1=pb[2][r_q:r_q + DK, :], op=ALU.mult))
            A[h]["ctx"] = C["v"]
            pbank_free[2] = ("v", A[h]["ctx"])

        def emit_w12_slice(jslot, mr):
            nonlocal hB_free, h_ready, x1ln_free
            mm_toks = {}
            silu_base = C["s"]
            for mc in range(4):
                mcg = mr * 4 + mc
                @blk.tensor
                def _(e, jslot=jslot, mc=mc, mcg=mcg, mr=mr):
                    if mc == 0:
                        e.wait_ge(SL[jslot], slot_loaded[jslot] - 16)
                        if mr > 0:
                            wt(e, x1ln_ready_box[0])
                    wt(e, pbank_free[0] if mc % 2 == 0 else pbank_free[1])
                    bA = mc % 2
                    for k in range(KC):
                        if mr == 0 and mc == 0:
                            e.wait_ge(SV, x1ln_chunks_box[0][k])
                        ins = e.matmul(out=pb[bA][:],
                                       lhsT=wsl[jslot][:, k * T + mc * P: k * T + (mc + 1) * P],
                                       rhs=x1ln_s[:, ts(k, T)], start=(k == 0), stop=(k == KC - 1))
                    t_inc(ins)
                    if mc == 0:
                        e.wait_ge(SL[jslot], slot_loaded[jslot])
                    wt(e, pbank_free[2 + bA])
                    for k in range(KC):
                        ins = e.matmul(out=pb[2 + bA][:],
                                       lhsT=wsl[jslot][:, KC * T + k * T + mc * P: KC * T + k * T + (mc + 1) * P],
                                       rhs=x1ln_s[:, ts(k, T)], start=(k == 0), stop=(k == KC - 1))
                    t_inc(ins)
                mm_toks[mc] = (C["t"] - 1, C["t"])
                pbank_free[mc % 2] = ("s", silu_base + mc + 1)

            stt_base = C["v"]
            prev_stt = (sA_free[0], sA_free[1])
            @blk.scalar
            def _(e, mr=mr, mm_toks=mm_toks, stt_base=stt_base, prev_stt=prev_stt):
                for mc in range(4):
                    mcg = mr * 4 + mc
                    e.wait_ge(STn, mm_toks[mc][0])
                    if mc >= 2:
                        e.wait_ge(SV, stt_base + mc - 1)
                    else:
                        wt(e, prev_stt[mc])
                    s_inc(e.activation(out=sA[mc % 2][:], in_=pb[mc % 2][:], func=AF.Silu,
                                       scale=s1_s[:, mcg:mcg + 1]))
            assert C["s"] == silu_base + 4

            for mc in range(4):
                mcg = mr * 4 + mc
                @blk.vector
                def _(e, mc=mc, mcg=mcg, mm_toks=mm_toks, silu_base=silu_base):
                    e.wait_ge(STn, mm_toks[mc][1])
                    e.wait_ge(SS, silu_base + mc + 1)
                    if mcg == 0:
                        wt(e, hB_free)
                    v_inc(e.scalar_tensor_tensor(out=hB[:, ts(mcg, T)], in0=pb[2 + mc % 2][:],
                                                 scalar=s2_s[:, mcg:mcg + 1], in1=sA[mc % 2][:],
                                                 op0=ALU.mult, op1=ALU.mult))
                pbank_free[2 + mc % 2] = ("v", C["v"])
                sA_free[mc % 2] = ("v", C["v"])
            if mr == 7:
                h_ready = ("v", C["v"])
                x1ln_free = ("t", C["t"])

        def hmul_mc(jslot, kind, sidx, mc):
            rhs_buf = hB if kind == "h1" else g1B
            rhs_ready = h_ready if kind == "h1" else g1_ready
            mcg = sidx * 2 + mc
            gelu_inc = C["s"] + 1
            @blk.tensor
            def _(e, jslot=jslot, mc=mc, rhs_buf=rhs_buf, rhs_ready=rhs_ready, sidx=sidx):
                if mc == 0:
                    e.wait_ge(SL[jslot], slot_loaded[jslot] - 16)
                    wt(e, rhs_ready)
                wt(e, pbank_free[mc % 2])
                ins = None
                for k in range(HC):
                    if mc == 0 and k == HC // 2:
                        e.wait_ge(SL[jslot], slot_loaded[jslot])
                    ins = e.matmul(out=pb[mc % 2][:],
                                   lhsT=wsl[jslot][:, k * 256 + mc * P: k * 256 + (mc + 1) * P],
                                   rhs=rhs_buf[:, ts(k, T)], start=(k == 0), stop=(k == HC - 1))
                t_inc(ins)
            mm_done = C["t"]

            dst_buf = g1B if kind == "h1" else hB
            scol = sh1_s if kind == "h1" else sh2_s
            @blk.scalar
            def _(e, kind=kind, mcg=mcg, mc=mc, dst_buf=dst_buf, scol=scol, mm_done=mm_done):
                e.wait_ge(STn, mm_done)
                if mcg == 0:
                    if kind == "h1":
                        wt(e, g1_freeB)
                    else:
                        e.wait_ge(STn, hB_mm_last)
                s_inc(e.activation(out=dst_buf[:, ts(mcg, T)], in_=pb[mc % 2][:], func=AF.Gelu,
                                   scale=scol[:, mcg:mcg + 1]))
            assert C["s"] == gelu_inc
            pbank_free[mc % 2] = ("s", gelu_inc)
            return mm_done

        def finish_hmul(kind, sidx, mm_done):
            nonlocal g1_ready, g2_ready, hB_mm_last, g1_freeB
            if kind == "h1" and sidx == 15:
                g1_ready = ("s", C["s"])
                hB_mm_last = C["t"]
            if kind == "h2" and sidx == 15:
                g2_ready = ("s", C["s"])
                g1_freeB = ("t", C["t"])

        def emit_hmul_slice(jslot, kind, sidx):
            for mc in range(2):
                hmul_mc(jslot, kind, sidx, mc)
            finish_hmul(kind, sidx, None)

        def emit_w3_slice(jslot, sidx, tile):
            nonlocal hB_free, x1_free
            for mc in range(2):
                mcg = sidx * 2 + mc
                @blk.tensor
                def _(e, jslot=jslot, mc=mc, sidx=sidx):
                    if mc == 0:
                        e.wait_ge(SL[jslot], slot_loaded[jslot] - 16)
                        wt(e, g2_ready)
                    wt(e, pbank_free[mc % 2])
                    for k in range(HC):
                        if mc == 0 and k == HC // 2:
                            e.wait_ge(SL[jslot], slot_loaded[jslot])
                        ins = e.matmul(out=pb[mc % 2][:],
                                       lhsT=wsl[jslot][:, k * 256 + mc * P: k * 256 + (mc + 1) * P],
                                       rhs=hB[:, ts(k, T)], start=(k == 0), stop=(k == HC - 1))
                    t_inc(ins)
                mmd = C["t"]

                @blk.vector
                def _(e, mc=mc, mcg=mcg, mmd=mmd):
                    e.wait_ge(STn, mmd)
                    v_inc(e.scalar_tensor_tensor(out=x1_s[:, ts(mcg, T)], in0=pb[mc % 2][:],
                                                 scalar=s3_s[:, mcg:mcg + 1], in1=x1_s[:, ts(mcg, T)],
                                                 op0=ALU.mult, op1=ALU.add))
                pbank_free[mc % 2] = ("v", C["v"])
            if sidx == 3:
                hB_free = ("t", C["t"])

        def emit_out_store(tile):
            nonlocal out_stored, x1_free
            @blk.sync
            def _(e, tile=tile):
                e.wait_ge(SV, C["v"])
                e.dma_start(out=outT[:, ts(tile, T)].rearrange("(k p) t -> p k t", p=P),
                            in_=x1_s[:].rearrange("p (k t) -> p k t", k=KC)).then_inc(SO, 16)
            C["o"] += 16
            out_stored = ("o", C["o"])
            x1_free = ("o", C["o"])

        def emit_wout(ctx_pair_toks, tile):
            nonlocal ctxb_free, x_free
            x1_chunks = []
            w_loaded = slot_loaded[wout_slot[0]]
            for m in range(KC):
                bank = m % 2
                @blk.tensor
                def _(e, m=m, bank=bank, w_loaded=w_loaded):
                    if m == 0:
                        e.wait_ge(SL[wout_slot[0]], w_loaded)
                    wt(e, pbank_free[bank])
                    ins = None
                    for k in range(KC):
                        if m == 0:
                            e.wait_ge(SV, ctx_pair_toks[k])
                        ins = e.matmul(out=pb[bank][:],
                                       lhsT=wsl[wout_slot[0]][:, k * D + m * P: k * D + (m + 1) * P],
                                       rhs=ctxb[:, ts(k, T)], start=(k == 0), stop=(k == KC - 1))
                    t_inc(ins)
                wmm = C["t"]
                if m == KC - 1:
                    ctxb_free = ("t", wmm)

                @blk.vector
                def _(e, m=m, bank=bank, wmm=wmm):
                    e.wait_ge(STn, wmm)
                    if m == 0:
                        wt(e, x1_free)
                    v_inc(e.scalar_tensor_tensor(out=x1_s[:, ts(m, T)], in0=pb[bank][:],
                                                 scalar=bout_s[:, m:m + 1], in1=xbuf[:, ts(m, T)],
                                                 op0=ALU.add, op1=ALU.add))
                pbank_free[bank] = ("v", C["v"])
                x1_chunks.append(C["v"])
            x_free = ("v", C["v"])
            return C["v"], x1_chunks

        x_loaded = XCNT[0]

        wout_slot = [0]
        w12_pre = [None, None]

        def dbg_barrier():
            @blk.tensor
            def _(e):
                e.wait_ge(SV, C["v"]); e.wait_ge(SS, C["s"]); e.wait_ge(SI, C["i"]); e.wait_ge(SO, C["o"])
                t_inc(e.engine_nop() if hasattr(e, "engine_nop") else e.nop())
            tcnt = C["t"]
            @blk.vector
            def _(e, tcnt=tcnt):
                e.wait_ge(STn, tcnt)
                v_inc(e.engine_nop())
            vcnt = C["v"]
            @blk.scalar
            def _(e, tcnt=tcnt, vcnt=vcnt):
                e.wait_ge(STn, tcnt); e.wait_ge(SV, vcnt)
                s_inc(e.nop())
            scnt = C["s"]
            @blk.sync
            def _(e, tcnt=tcnt, vcnt=vcnt, scnt=scnt):
                e.wait_ge(STn, tcnt); e.wait_ge(SV, vcnt); e.wait_ge(SS, scnt)

        def do_slice(kind, pos, tile):
            if kind == "w12" and pos in (0, 1) and w12_pre[pos] is not None:
                jj = w12_pre[pos]
                w12_pre[pos] = None
            else:
                jj = load_slice(kind, pos)
            if kind == "w12":
                emit_w12_slice(jj, pos)
            elif kind in ("h1", "h2"):
                emit_hmul_slice(jj, kind, pos)
            else:
                emit_w3_slice(jj, pos, tile)
            slot_free[jj] = ("t", C["t"])

        for i in range(NT):
            ffn_slices = []
            if i > 0:
                ffn_slices = [("w12", mr) for mr in range(8)] + \
                             [("h1", s_) for s_ in range(16)] + \
                             [("h2", s_) for s_ in range(16)]

            pre = [("x", x_loaded), xln_free]
            mid = None
            if i > 0:
                def mid(i=i):
                    kind, pos = ffn_slices[0]
                    do_slice(kind, pos, i - 1)
            ln1_done, ln1_chunks = emit_ln(xbuf, xln_s, pre, mid_cb=mid)

            qkv_done = 0
            fidx = 1 if i > 0 else 0
            for sl in range(3):
                if fidx < 8 and i > 0:
                    kind, pos = ffn_slices[fidx]; fidx += 1
                    do_slice(kind, pos, i - 1)
                jq = load_slice("qkv", sl, direct=(i == 0))
                qkv_done = emit_qkv_slice(jq, sl, ln1_chunks)
                slot_free[jq] = ("t", C["t"])
            xln_free = ("t", C["t"])
            if i == 0:
                emit_casts()

            while fidx < 8 and i > 0:
                kind, pos = ffn_slices[fidx]; fidx += 1
                do_slice(kind, pos, i - 1)

            pend = []

            def queue_ffn():
                nonlocal fidx
                if i > 0 and fidx < len(ffn_slices):
                    kind, pos = ffn_slices[fidx]; fidx += 1
                    jj = load_slice(kind, pos)
                    pend.append([jj, kind, pos, 0])
                    pend.append([jj, kind, pos, 1])

            def run_ffn_mc():
                if pend:
                    jj, kind, pos, mc = pend.pop(0)
                    mm = hmul_mc(jj, kind, pos, mc)
                    if mc == 1:
                        finish_hmul(kind, pos, mm)
                        slot_free[jj] = ("t", C["t"])

            for h in range(NH + 2):
                if h < NH:
                    attn_V1(h)
                    queue_ffn()
                    run_ffn_mc()
                    attn_sc(h, qkv_done)
                    attn_S1(h)
                if 1 <= h <= NH:
                    hh = h - 1
                    attn_erep_pair(hh, 0)
                    attn_tmp2_pair(hh, 0)
                    run_ffn_mc()
                    attn_erep_pair(hh, 1)
                    attn_u_pair(hh, 0)
                    attn_tmp2_pair(hh, 1)
                    queue_ffn()
                    run_ffn_mc()
                    attn_erep_pair(hh, 2)
                    attn_u_pair(hh, 1)
                    attn_tmp2_pair(hh, 2)
                    attn_S2a(hh)
                    run_ffn_mc()
                    attn_erep_pair(hh, 3)
                    attn_u_pair(hh, 2)
                    attn_tmp2_pair(hh, 3)
                    run_ffn_mc()
                    attn_u_pair(hh, 3)
                if 2 <= h <= NH + 1:
                    attn_rrep(h - 2)
                    attn_S2b(h - 2)
                    attn_ctx(h - 2)
            ctx_done = A[NH - 1]["ctx"]
            qkv_free = ("v", ctx_done)
            pbank_free[2] = ("v", A[NH - 1]["ctx"])
            pbank_free[3] = ("v", A[NH - 1]["tmp2_done"])
            while pend:
                run_ffn_mc()
            while i > 0 and fidx < len(ffn_slices):
                kind, pos = ffn_slices[fidx]; fidx += 1
                do_slice(kind, pos, i - 1)

            if i > 0:
                for s_ in range(4):
                    do_slice("w3", s_, i - 1)
            jw = load_slice("wout", 0)
            wout_slot[0] = jw
            wout_loaded = slot_loaded[jw]
            w12_pre[0] = load_slice("w12", 0)
            if i > 0:
                emit_out_store(i - 1)

            x1_done, x1_chunks_w = emit_wout([A[2 * k_ + 1]["ctx"] for k_ in range(KC)], i)
            slot_free[wout_slot[0]] = ("t", C["t"])

            if i + 1 < NT:
                @blk.sync
                def _(e, i=i):
                    wt(e, x_free)
                    e.dma_start(out=xbuf[:].rearrange("p (k t) -> p k t", k=KC),
                                in_=xT[:, ts(i + 1, T)].rearrange("(k p) t -> p k t", p=P)).then_inc(SX, 16)
                XCNT[0] += 16
                x_loaded = XCNT[0]
            w12_pre[1] = load_slice("w12", 1)

            ln2_done, ln2_chunks = emit_ln(x1_s, x1ln_s, [x1ln_free, ("v", x1_chunks_w[0])],
                                           pre_chunks=[("v", t_) for t_ in x1_chunks_w])
            x1ln_ready_box[0] = ("v", ln2_done)
            x1ln_chunks_box[0] = ln2_chunks
            if DEBUG_BARRIER:
                dbg_barrier()

        @blk.tensor
        def _(e, ln2_done=ln2_done):
            e.wait_ge(SV, ln2_done)
        ffn_slices = [("w12", mr) for mr in range(8)] + \
                     [("h1", s_) for s_ in range(16)] + \
                     [("h2", s_) for s_ in range(16)]
        for kind, pos in ffn_slices:
            do_slice(kind, pos, NT - 1)
        for s_ in range(4):
            do_slice("w3", s_, NT - 1)
        emit_out_store(NT - 1)

        @blk.sync
        def _(e):
            e.wait_ge(SO, C["o"])

    return nc


_PROG_CACHE = {}

def _get_prog():
    if "nc" not in _PROG_CACHE:
        _PROG_CACHE["nc"] = build_program()
    return _PROG_CACHE["nc"]


def _consts():
    bf = ml_dtypes.bfloat16
    dsel = np.zeros((P, P), bf)
    for p in range(P):
        dsel[p, p % DK] = 1
    bdsel = np.zeros((P, KC * P), bf)
    gsel = np.zeros((P, KC * P), bf)
    for c in range(KC):
        for p in range(P):
            gg = 2 * c + p // DK
            bdsel[p, c * P + gg] = 1
            gsel[gg, c * P + p] = 1
    densel = np.zeros((P, P), bf); densel[0:NH, 32] = 1
    rsel = np.zeros((P, P), bf); rsel[32, :] = 1
    brow = np.zeros((P, P), bf); brow[0, :] = 1
    col0 = np.zeros((P, P), bf); col0[:, 0] = 1
    eps = np.full((P, 1), EPS, np.float32)
    return dsel, bdsel, gsel, densel, rsel, brow, col0, eps


def kernel(L0_attn_x, L0_attn_W_qkv, L0_attn_b_qkv, L0_attn_W_out, L0_attn_b_out,
           L0_ffn_w1_w, L0_ffn_w1_s, L0_ffn_w2_w, L0_ffn_w2_s,
           L0_ffn_h1_w, L0_ffn_h1_s, L0_ffn_h2_w, L0_ffn_h2_s,
           L0_ffn_w3_w, L0_ffn_w3_s):
    x = np.asarray(L0_attn_x)
    B = x.shape[0]
    wq = np.asarray(L0_attn_W_qkv)
    wq3 = wq.reshape(D, NH, 3, DK)
    wqkv = np.ascontiguousarray(wq3.transpose(0, 2, 1, 3).reshape(D, 3 * D))
    bq3 = np.asarray(L0_attn_b_qkv).reshape(NH, 3, DK)
    bqkv = np.ascontiguousarray(bq3.transpose(1, 0, 2).reshape(3 * D // P, P).T)
    bout = np.ascontiguousarray(np.asarray(L0_attn_b_out).reshape(D // P, P).T)
    w1T = np.ascontiguousarray(np.asarray(L0_ffn_w1_w).T)
    w2T = np.ascontiguousarray(np.asarray(L0_ffn_w2_w).T)
    h1T = np.ascontiguousarray(np.asarray(L0_ffn_h1_w).T)
    h2T = np.ascontiguousarray(np.asarray(L0_ffn_h2_w).T)
    w3T = np.ascontiguousarray(np.asarray(L0_ffn_w3_w).T)
    s1 = np.ascontiguousarray(np.asarray(L0_ffn_w1_s).reshape(HC, P).T)
    s2 = np.ascontiguousarray(np.asarray(L0_ffn_w2_s).reshape(HC, P).T)
    sh1 = np.ascontiguousarray(np.asarray(L0_ffn_h1_s).reshape(HC, P).T)
    sh2 = np.ascontiguousarray(np.asarray(L0_ffn_h2_s).reshape(HC, P).T)
    s3 = np.ascontiguousarray(np.asarray(L0_ffn_w3_s).reshape(D // P, P).T)
    dsel, bdsel, gsel, densel, rsel, brow, col0, eps = _consts()
    cb16 = np.ascontiguousarray(np.concatenate(
        [dsel, bdsel, gsel, densel, rsel, brow, col0], axis=1))
    cb32 = np.ascontiguousarray(np.concatenate(
        [eps, bqkv, bout, s1, s2, sh1, sh2, s3], axis=1))

    shared = dict(wqkv=wqkv, wout=np.ascontiguousarray(np.asarray(L0_attn_W_out)),
                  w1T=w1T, w2T=w2T, h1T=h1T, h2T=h2T, w3T=w3T,
                  cb16=cb16, cb32=cb32)

    in_maps = []
    for c in range(B):
        m = dict(shared)
        m["xT"] = np.ascontiguousarray(x[c].T)
        in_maps.append(m)

    nc = _get_prog()
    res = run_bass_kernel_spmd(nc, in_maps, core_ids=list(range(N_CORES)))
    out = np.stack([np.asarray(r["outT"], dtype=np.float32).T for r in res.results], axis=0)
    return out


# version 24 (final)
# speedup vs baseline: 1.1480x; 1.0288x over previous
import numpy as np
import ml_dtypes
from contextlib import ExitStack

import concourse.bass as bass
import concourse.mybir as mybir
from concourse.bass_utils import run_bass_kernel_spmd

F32 = mybir.dt.float32
BF16 = mybir.dt.bfloat16
AF = mybir.ActivationFunctionType
ALU = mybir.AluOpType
ts = bass.ts

P = 128
D = 1024
DH = 4096
NH, DK = 16, 64
S = 4096
T = 512
NT = S // T
KC = D // P
HC = DH // P
N_CORES = 8
EPS = 1e-5
DEBUG_BARRIER = False
SLOT = 8192


def build_program():
    nc = bass.Bass("TRN2", target_bir_lowering=False)

    xT = nc.dram_tensor("xT", [D, S], F32, kind="ExternalInput").ap()
    wqkv = nc.dram_tensor("wqkv", [D, 3 * D], F32, kind="ExternalInput").ap()
    wout = nc.dram_tensor("wout", [D, D], F32, kind="ExternalInput").ap()
    w1T = nc.dram_tensor("w1T", [D, DH], F32, kind="ExternalInput").ap()
    w2T = nc.dram_tensor("w2T", [D, DH], F32, kind="ExternalInput").ap()
    h1T = nc.dram_tensor("h1T", [DH, DH], F32, kind="ExternalInput").ap()
    h2T = nc.dram_tensor("h2T", [DH, DH], F32, kind="ExternalInput").ap()
    w3T = nc.dram_tensor("w3T", [DH, D], F32, kind="ExternalInput").ap()
    cb16 = nc.dram_tensor("cb16", [P, 2688], BF16, kind="ExternalInput").ap()
    cb32 = nc.dram_tensor("cb32", [P, 169], F32, kind="ExternalInput").ap()

    outT = nc.dram_tensor("outT", [D, S], F32, kind="ExternalOutput").ap()

    wbqkv_d = nc.dram_tensor("wbqkv_d", [D, 3 * D], BF16, kind="Internal").ap()
    wbout_d = nc.dram_tensor("wbout_d", [D, D], BF16, kind="Internal").ap()
    wb1_d = nc.dram_tensor("wb1_d", [D, DH], BF16, kind="Internal").ap()
    wb2_d = nc.dram_tensor("wb2_d", [D, DH], BF16, kind="Internal").ap()
    hb1_d = nc.dram_tensor("hb1_d", [DH, DH], BF16, kind="Internal").ap()
    hb2_d = nc.dram_tensor("hb2_d", [DH, DH], BF16, kind="Internal").ap()
    wb3_d = nc.dram_tensor("wb3_d", [DH, D], BF16, kind="Internal").ap()

    with ExitStack() as g:
        E = g.enter_context
        cb16_s = E(nc.sbuf_tensor("cb16_s", [P, 2688], BF16))
        cb32_s = E(nc.sbuf_tensor("cb32_s", [P, 169], F32))
        dsel_s = cb16_s[:, 0:P]
        bdsel_s = cb16_s[:, P:P + KC * P]
        gsel_s = cb16_s[:, P + KC * P:P + 2 * KC * P]
        densel_s = cb16_s[:, 2176:2304]
        rsel_s = cb16_s[:, 2304:2432]
        brow_s = cb16_s[:, 2432:2560]
        col0_s = cb16_s[:, 2560:2688]
        eps_s = cb32_s[:, 0:1]
        bqkv_s = cb32_s[:, 1:25]
        bout_s = cb32_s[:, 25:33]
        s1_s = cb32_s[:, 33:65]
        s2_s = cb32_s[:, 65:97]
        sh1_s = cb32_s[:, 97:129]
        sh2_s = cb32_s[:, 129:161]
        s3_s = cb32_s[:, 161:169]

        wsl = [E(nc.sbuf_tensor(f"wsl{j}", [P, SLOT], BF16)) for j in range(2)]
        xbuf = E(nc.sbuf_tensor("xbuf", [P, KC * T], F32))
        xln_s = E(nc.sbuf_tensor("xln_s", [P, KC * T], BF16))
        x1ln_s = E(nc.sbuf_tensor("x1ln_s", [P, KC * T], BF16))
        q_s = E(nc.sbuf_tensor("q_s", [P, KC * T], BF16))
        k_s = E(nc.sbuf_tensor("k_s", [P, KC * T], BF16))
        v_s = E(nc.sbuf_tensor("v_s", [P, KC * T], BF16))
        qrep = E(nc.sbuf_tensor("qrep", [P, T], BF16))
        tmpa = E(nc.sbuf_tensor("tmpa", [P, KC * T], BF16))
        esb = [E(nc.sbuf_tensor(f"esb{j}", [P, T], BF16)) for j in range(2)]
        tmp2 = [E(nc.sbuf_tensor(f"tmp2_{j}", [P, T], BF16)) for j in range(2)]
        rband = E(nc.sbuf_tensor("rband", [33, T], F32))
        rbandb = E(nc.sbuf_tensor("rbandb", [P, 2 * T], BF16))
        scr = E(nc.sbuf_tensor("scr", [1, 2 * T], F32))
        rowb = E(nc.sbuf_tensor("rowb", [P, T], BF16))

        sA = [E(nc.sbuf_tensor(f"sA{j}", [P, T], BF16)) for j in range(2)]
        xbb = [E(nc.sbuf_tensor("xbb0", [P, T], BF16)), qrep]
        ctxb = E(nc.sbuf_tensor("ctxb", [P, KC * T], BF16))
        x1_s = E(nc.sbuf_tensor("x1_s", [P, KC * T], F32))
        hB = E(nc.sbuf_tensor("hB", [P, HC * T], BF16))
        g1B = E(nc.sbuf_tensor("g1B", [P, HC * T], BF16))

        pb = [E(nc.psum_tensor(f"pb{i}", [P, T], F32)) for i in range(8)]

        SW = E(nc.semaphore("SW"))
        SI = E(nc.semaphore("SI"))
        SO = E(nc.semaphore("SO"))
        STn = E(nc.semaphore("STn"))
        SV = E(nc.semaphore("SV"))
        SS = E(nc.semaphore("SS"))
        WSEM = {name: E(nc.semaphore(f"W_{name}")) for name in
                ("qkv", "wout", "w1", "w2", "h1", "h2", "w3")}
        WCNT = {name: 0 for name in WSEM}
        SL = [E(nc.semaphore(f"SL{j}")) for j in range(2)]
        SLD = E(nc.semaphore("SLD"))
        SX = E(nc.semaphore("SX"))
        blk = E(nc.Block())
        C = dict(w=0, i=0, o=0, t=0, v=0, s=0)
        SEMS = {"t": STn, "v": SV, "s": SS, "i": SI, "o": SO, "w": SW, "x": None}

        def t_inc(instr):
            C["t"] += 1
            return instr.then_inc(STn, 1)

        def v_inc(instr):
            C["v"] += 1
            return instr.then_inc(SV, 1)

        def s_inc(instr):
            C["s"] += 1
            return instr.then_inc(SS, 1)

        def wt(e, tok):
            if tok is None:
                return
            if tok[1] > 0:
                sem = SX if tok[0] == "x" else SEMS[tok[0]]
                e.wait_ge(sem, tok[1])

        XCNT = [0]
        @blk.sync
        def _(e):
            e.dma_start(out=xbuf[:].rearrange("p (k t) -> p k t", k=KC),
                        in_=xT[:, 0:T].rearrange("(k p) t -> p k t", p=P)).then_inc(SX, 16)
            for dst, src in [(cb16_s[:], cb16), (cb32_s[:], cb32)]:
                e.dma_start(out=dst, in_=src).then_inc(SI, 16)
                C["i"] += 16
        XCNT[0] += 16
        consts_loaded = C["i"]

        @blk.tensor
        def _(e):
            e.wait_ge(SI, consts_loaded)
        @blk.scalar
        def _(e):
            e.wait_ge(SI, consts_loaded)

        @blk.vector
        def _(e):
            e.wait_ge(SI, consts_loaded)
            e.memset(esb[0][:], 0.0)
            e.memset(esb[1][:], 0.0)
            e.memset(rbandb[:], 0.0)
            v_inc(e.memset(rowb[:], 0.0))
        memset_done = C["v"]

        def emit_casts():
            @blk.gpsimd
            def _(e):
                for name, src_, dst, nrows in [("wout", wout, wbout_d, D),
                                               ("w1", w1T, wb1_d, D), ("w2", w2T, wb2_d, D),
                                               ("qkv", wqkv, wbqkv_d, D)]:
                    for r0 in range(0, nrows, 256):
                        e.dma_start(out=dst[r0:r0 + 256, :], in_=src_[r0:r0 + 256, :]).then_inc(WSEM[name], 16)
                        WCNT[name] += 16
                    e.wait_ge(WSEM[name], WCNT[name])
                for name, src_, dst, cstep in [("h1", h1T, hb1_d, 512), ("h2", h2T, hb2_d, 512),
                                               ("w3", w3T, wb3_d, 256)]:
                    ncols = src_.shape[1]
                    for c0 in range(0, ncols, cstep):
                        if c0 > 0:
                            e.wait_ge(WSEM[name], WCNT[name])
                        e.dma_start(out=dst[:, c0:c0 + cstep], in_=src_[:, c0:c0 + cstep]).then_inc(WSEM[name], 16)
                        WCNT[name] += 16
                    if name != "w3":
                        e.wait_ge(WSEM[name], WCNT[name])

        x_loaded = 0
        x_free = None
        x1_free = None
        x1_store = None
        xln_free = None
        x1ln_free = None
        qkv_free = None
        ctxb_free = None
        rowb_free = None
        pbank_free = [None] * 8
        slot_free = [None, None]
        slot_loaded = [0, 0]
        slot_sem = [None, None]
        DCNT = [0]
        slot_idx = [0]
        dummy_tok = [0]
        tmpa_free = [0, 0]
        esb_free = [0, 0]
        psS_free = [None, None]
        rbandb_free = [0, 0]
        tmp2_free = [0, 0]
        sA_free = [None, None]
        hB_free = None
        g1_freeB = None
        h_ready = None
        g1_ready = None
        g2_ready = None
        hB_mm_last = 0
        out_stored = None
        x1ln_ready_box = [None]
        x1ln_chunks_box = [None]

        SLCNT = [0, 0]

        def load_slice(kind, pos, direct=False):
            j = slot_idx[0] % 2
            slot_idx[0] += 1
            if direct:
                @blk.gpsimd
                def _(e, pos=pos, j=j):
                    wt(e, slot_free[j])
                    if DCNT[0] > 0:
                        e.wait_ge(SLD, DCNT[0])
                    e.dma_start(out=wsl[j][:].rearrange("p (k m) -> p k m", k=KC),
                                in_=wqkv[:, ts(pos, D)].rearrange("(k p) m -> p k m", p=P)).then_inc(SLD, 16)
                DCNT[0] += 16
                slot_loaded[j] = DCNT[0]
                slot_sem[j] = SLD
                return j
            slot_sem[j] = SL[j]
            @blk.sync
            def _(e, kind=kind, pos=pos, j=j):
                wt(e, slot_free[j])
                if kind == "w12":
                    e.wait_ge(WSEM["w1"], WCNT["w1"])
                    e.wait_ge(WSEM["w2"], WCNT["w2"])
                elif kind in ("h1", "h2"):
                    e.wait_ge(WSEM[kind], 16 * (pos // 2 + 1))
                elif kind == "w3":
                    e.wait_ge(WSEM["w3"], 16 * (pos + 1))
                else:
                    wn = {"qkv": "qkv", "wout": "wout"}[kind]
                    e.wait_ge(WSEM[wn], WCNT[wn])
                if kind == "qkv":
                    e.dma_start(out=wsl[j][:].rearrange("p (k m) -> p k m", k=KC),
                                in_=wbqkv_d[:, ts(pos, D)].rearrange("(k p) m -> p k m", p=P)).then_inc(SL[j], 16)
                    SLCNT[j] += 16
                elif kind == "wout":
                    e.dma_start(out=wsl[j][:].rearrange("p (k m) -> p k m", k=KC),
                                in_=wbout_d.rearrange("(k p) m -> p k m", p=P)).then_inc(SL[j], 16)
                    SLCNT[j] += 16
                elif kind == "w12":
                    e.dma_start(out=wsl[j][:, 0:KC * T].rearrange("p (k m) -> p k m", k=KC),
                                in_=wb1_d[:, ts(pos, T)].rearrange("(k p) m -> p k m", p=P)).then_inc(SL[j], 16)
                    e.wait_ge(SL[j], SLCNT[j] + 16)
                    e.dma_start(out=wsl[j][:, KC * T:2 * KC * T].rearrange("p (k m) -> p k m", k=KC),
                                in_=wb2_d[:, ts(pos, T)].rearrange("(k p) m -> p k m", p=P)).then_inc(SL[j], 16)
                    SLCNT[j] += 32
                else:
                    src = {"h1": hb1_d, "h2": hb2_d, "w3": wb3_d}[kind]
                    half = HC // 2
                    e.dma_start(out=wsl[j][:, 0:half * 256].rearrange("p (k m) -> p k m", k=half),
                                in_=src[0:half * P, ts(pos, 256)].rearrange("(k p) m -> p k m", p=P)
                                ).then_inc(SL[j], 16)
                    e.wait_ge(SL[j], SLCNT[j] + 16)
                    e.dma_start(out=wsl[j][:, half * 256:HC * 256].rearrange("p (k m) -> p k m", k=half),
                                in_=src[half * P:HC * P, ts(pos, 256)].rearrange("(k p) m -> p k m", p=P)
                                ).then_inc(SL[j], 16)
                    SLCNT[j] += 32
            slot_loaded[j] = SLCNT[j]
            return j

        def emit_ln(src_f32, dst_bf, pre_toks, pre_chunks=None, mid_cb=None):
            nonlocal rowb_free
            mm_tok = [0, 0]
            first = [True]
            for c in range(KC):
                @blk.vector
                def _(e, c=c):
                    if first[0]:
                        for tok in pre_toks:
                            wt(e, tok)
                        first[0] = False
                    else:
                        wt(e, pre_chunks[c] if pre_chunks else None)
                    if mm_tok[c % 2] > 0:
                        e.wait_ge(STn, mm_tok[c % 2])
                    if tmp2_free[c % 2] > 0:
                        e.wait_ge(STn, tmp2_free[c % 2])
                    v_inc(e.tensor_copy(out=xbb[c % 2][:], in_=src_f32[:, ts(c, T)]))
                    e.wait_ge(SV, C["v"])
                    v_inc(e.tensor_tensor(out=tmp2[c % 2][:], in0=xbb[c % 2][:], in1=xbb[c % 2][:], op=ALU.mult))
                sq_done = C["v"]

                @blk.tensor
                def _(e, c=c, sq_done=sq_done):
                    e.wait_ge(SV, sq_done)
                    if c == 0:
                        wt(e, pbank_free[4])
                        wt(e, pbank_free[5])
                    e.matmul(out=pb[4][:], lhsT=col0_s[:], rhs=xbb[c % 2][:],
                             start=(c == 0), stop=(c == KC - 1), skip_group_check=True)
                    t_inc(e.matmul(out=pb[5][:], lhsT=col0_s[:], rhs=tmp2[c % 2][:],
                                   start=(c == 0), stop=(c == KC - 1), skip_group_check=True))
                mm_tok[c % 2] = C["t"]
                tmp2_free[c % 2] = C["t"]
            stats_done = C["t"]
            if mid_cb is not None:
                mid_cb()

            m_row = scr[0:1, 0:T]
            v_row = scr[0:1, T:2 * T]
            w_row = rband[0:1, :]

            @blk.vector
            def _(e, vcov=None):
                e.wait_ge(STn, stats_done)
                e.wait_ge(SV, C["v"])
                v_inc(e.tensor_scalar(out=m_row, in0=pb[4][0:1, :], scalar1=1.0 / D, scalar2=None, op0=ALU.mult))
                v_inc(e.tensor_scalar(out=v_row, in0=pb[5][0:1, :], scalar1=1.0 / D, scalar2=None, op0=ALU.mult))
                e.wait_ge(SV, C["v"])
                v_inc(e.tensor_tensor(out=w_row, in0=m_row, in1=m_row, op=ALU.mult))
                e.wait_ge(SV, C["v"])
                v_inc(e.tensor_tensor(out=v_row, in0=v_row, in1=w_row, op=ALU.subtract))
            var_done = C["v"]
            pbank_free[4] = ("v", var_done)
            pbank_free[5] = ("v", var_done)

            @blk.scalar
            def _(e):
                e.wait_ge(SV, var_done)
                if dummy_tok[0] > 0:
                    e.wait_ge(SS, dummy_tok[0])
                s_inc(e.activation(out=w_row, in_=v_row, func=AF.Ln, bias=eps_s[0:1, :]))
                e.wait_ge(SS, C["s"])
                s_inc(e.activation(out=w_row, in_=w_row, func=AF.Exp, scale=-0.5))
            r_done = C["s"]
            dummy_tok[0] = r_done

            @blk.vector
            def _(e):
                e.wait_ge(SS, r_done)
                wt(e, rowb_free)
                v_inc(e.tensor_tensor(out=v_row, in0=m_row, in1=w_row, op=ALU.mult))
                e.tensor_copy(out=rowb[0:1, :], in_=w_row)
                e.wait_ge(SV, C["v"])
                v_inc(e.tensor_copy(out=rowb[32:33, :], in_=v_row))
            rowb_done = C["v"]

            @blk.tensor
            def _(e):
                e.wait_ge(SV, rowb_done)
                wt(e, pbank_free[2])
                wt(e, pbank_free[3])
                e.matmul(out=pb[2][:], lhsT=brow_s[:], rhs=rowb[:], start=True, stop=True)
                t_inc(e.matmul(out=pb[3][:], lhsT=rsel_s[:], rhs=rowb[:], start=True, stop=True))
            bcast_done = C["t"]
            rowb_free = ("t", bcast_done)

            apply_chunks = []
            for c in range(KC):
                @blk.vector
                def _(e, c=c):
                    if c == 0:
                        e.wait_ge(STn, bcast_done)
                    else:
                        e.wait_ge(SV, apply_chunks[-1])
                    v_inc(e.tensor_tensor(out=xbb[0][:], in0=src_f32[:, ts(c, T)], in1=pb[2][:], op=ALU.mult))
                    e.wait_ge(SV, C["v"])
                    v_inc(e.tensor_tensor(out=dst_bf[:, ts(c, T)], in0=xbb[0][:], in1=pb[3][:], op=ALU.subtract))
                apply_chunks.append(C["v"])
            apply_done = C["v"]
            pbank_free[2] = ("v", apply_done)
            pbank_free[3] = ("v", apply_done)
            return apply_done, apply_chunks

        def emit_qkv_slice(jslot, sl, ln1_chunks):
            nonlocal qkv_free
            for mc in range(KC):
                m = sl * KC + mc
                bank = 4 + m % 2
                @blk.tensor
                def _(e, jslot=jslot, sl=sl, mc=mc, m=m, bank=bank):
                    if mc == 0:
                        e.wait_ge(slot_sem[jslot], slot_loaded[jslot])
                    wt(e, pbank_free[bank])
                    ins = None
                    for k in range(KC):
                        if sl == 0 and mc == 0:
                            e.wait_ge(SV, ln1_chunks[k])
                        ins = e.matmul(out=pb[bank][:],
                                       lhsT=wsl[jslot][:, k * D + mc * P: k * D + (mc + 1) * P],
                                       rhs=xln_s[:, ts(k, T)], start=(k == 0), stop=(k == KC - 1))
                    t_inc(ins)
                mmd = C["t"]

                dst = (q_s, k_s, v_s)[m // KC]
                col = (m % KC) * T
                @blk.vector
                def _(e, m=m, bank=bank, dst=dst, col=col, mmd=mmd):
                    e.wait_ge(STn, mmd)
                    if m == 0:
                        wt(e, qkv_free)
                    v_inc(e.tensor_scalar(out=dst[:, col:col + T], in0=pb[bank][:],
                                          scalar1=bqkv_s[:, m:m + 1], scalar2=None, op0=ALU.add))
                pbank_free[bank] = ("v", C["v"])
            return C["v"]

        A = {}

        def attn_V1(h):
            c_q, r_q = h // 2, (h % 2) * DK
            prev_sc = A[h - 1]["sc"] if h >= 1 else (A[NH - 1]["sc"] if (NH - 1) in A and "sc" in A[NH - 1] else 0)
            @blk.vector
            def _(e, h=h, c_q=c_q, r_q=r_q, prev_sc=prev_sc):
                if prev_sc > 0:
                    e.wait_ge(STn, prev_sc)
                e.tensor_copy(out=qrep[0:DK, :], in_=q_s[r_q:r_q + DK, ts(c_q, T)])
                v_inc(e.tensor_copy(out=qrep[DK:P, :], in_=q_s[r_q:r_q + DK, ts(c_q, T)]))
                e.wait_ge(SV, C["v"])
                ins = None
                for c in range(KC):
                    ins = e.tensor_tensor(out=tmpa[:, ts(c, T)], in0=k_s[:, ts(c, T)],
                                          in1=qrep[:], op=ALU.mult)
                v_inc(ins)
            A[h] = {"erep": {}, "tmp2": {}}
            A[h]["tmp"] = C["v"]

        def attn_sc(h, qkv_done):
            sb = 6 + (h % 2)
            @blk.tensor
            def _(e, h=h, sb=sb, qkv_done=qkv_done):
                e.wait_ge(SV, A[h]["tmp"])
                if h >= 2:
                    e.wait_ge(SS, A[h - 2]["r"])
                ins = None
                for c in range(KC):
                    ins = e.matmul(out=pb[sb][:], lhsT=bdsel_s[:, ts(c, P)],
                                   rhs=tmpa[:, ts(c, T)],
                                   start=(c == 0), stop=(c == KC - 1), skip_group_check=True)
                t_inc(ins)
            A[h]["sc"] = C["t"]

        def attn_S1(h):
            sb = 6 + (h % 2)
            @blk.scalar
            def _(e, h=h, sb=sb):
                e.wait_ge(STn, A[h]["sc"])
                if h >= 2:
                    e.wait_ge(STn, A[h - 2]["erep"][KC - 1])
                if dummy_tok[0] > 0:
                    e.wait_ge(SS, dummy_tok[0])
                s_inc(e.activation(out=rband[0:1, 0:1], in_=eps_s[0:1, :], func=AF.Ln))
                dummy_tok[0] = C["s"]
                s_inc(e.activation(out=esb[h % 2][0:NH, :], in_=pb[sb][0:NH, :],
                                   func=AF.Exp, scale=0.125))
            A[h]["exp"] = C["s"]

        def attn_erep_pair(h, j):
            sb = 6 + (h % 2)
            cpair = (2 * j, 2 * j + 1)
            if j == 0:
                tok2 = ("v", A[h - 2]["ctx"]) if h >= 2 else pbank_free[2]
                tok3 = ("v", A[h - 1]["tmp2"][7]) if h >= 1 else pbank_free[3]
            else:
                tok2 = ("v", A[h]["tmp2"][2 * j - 2])
                tok3 = ("v", A[h]["tmp2"][2 * j - 1])
            @blk.tensor
            def _(e, h=h, j=j, cpair=cpair, sb=sb, tok2=tok2, tok3=tok3):
                if j == 0:
                    e.wait_ge(SS, A[h]["exp"])
                    if h == 0:
                        e.wait_ge(SV, memset_done)
                wt(e, tok2)
                wt(e, tok3)
                for c in cpair:
                    gp = c % 2
                    t_inc(e.matmul(out=pb[2 + gp][:], lhsT=gsel_s[:, ts(c, P)],
                                   rhs=esb[h % 2][:], start=True, stop=True))
                    A[h]["erep"][c] = C["t"]
                if j == 0:
                    t_inc(e.matmul(out=pb[sb][:], lhsT=densel_s[:],
                                   rhs=esb[h % 2][:], start=True, stop=True))
                    A[h]["den"] = C["t"]

        def attn_tmp2_pair(h, j):
            cpair = (2 * j, 2 * j + 1)
            if j == 0:
                utok = A[h - 1]["u"] if h >= 1 else 0
            else:
                utok = A[h]["upair"][j - 1]
            @blk.vector
            def _(e, h=h, cpair=cpair, utok=utok):
                for c in cpair:
                    gp = c % 2
                    e.wait_ge(STn, A[h]["erep"][c])
                    if utok > 0:
                        e.wait_ge(STn, utok)
                    v_inc(e.tensor_tensor(out=tmp2[gp][:], in0=pb[2 + gp][:],
                                          in1=v_s[:, ts(c, T)], op=ALU.mult))
                    A[h]["tmp2"][c] = C["v"]
            if j == 3:
                A[h]["tmp2_done"] = C["v"]

        def attn_S2a(h):
            sb = 6 + (h % 2)
            @blk.scalar
            def _(e, h=h, sb=sb):
                e.wait_ge(STn, A[h]["den"])
                if h >= 2:
                    e.wait_ge(STn, A[h - 2]["rrep"])
                s_inc(e.activation(out=rband[32:33, :], in_=pb[sb][32:33, :], func=AF.Ln))
                e.wait_ge(SS, C["s"])
                s_inc(e.activation(out=rbandb[32:33, ts(h % 2, T)], in_=rband[32:33, :],
                                   func=AF.Exp, scale=-1.0))
            A[h]["r"] = C["s"]

        def attn_u_pair(h, j):
            ub = 4 + (h % 2)
            cpair = (2 * j, 2 * j + 1)
            @blk.tensor
            def _(e, h=h, ub=ub, cpair=cpair):
                for c in cpair:
                    gp = c % 2
                    e.wait_ge(SV, A[h]["tmp2"][c])
                    if c == 0:
                        wt(e, pbank_free[4 + h % 2])
                    t_inc(e.matmul(out=pb[ub][:], lhsT=dsel_s[:], rhs=tmp2[gp][:],
                                   start=(c == 0), stop=(c == KC - 1), skip_group_check=True))
            A[h].setdefault("upair", {})[j] = C["t"]
            if j == 3:
                A[h]["u"] = C["t"]

        def attn_rrep(h):
            if (h + 1) in A and 6 in A[h + 1]["tmp2"]:
                tok = ("v", A[h + 1]["tmp2"][6])
            elif h >= 1 and "ctx" in A[h - 1]:
                tok = ("v", A[h - 1]["ctx"])
            else:
                tok = None
            @blk.tensor
            def _(e, h=h, tok=tok):
                e.wait_ge(SS, A[h]["r"])
                wt(e, tok)
                t_inc(e.matmul(out=pb[2][:], lhsT=rsel_s[:],
                               rhs=rbandb[:, ts(h % 2, T)], start=True, stop=True))
            A[h]["rrep"] = C["t"]

        def attn_S2b(h):
            c_q, r_q = h // 2, (h % 2) * DK
            ub = 4 + (h % 2)
            @blk.scalar
            def _(e, h=h, ub=ub, c_q=c_q, r_q=r_q):
                e.wait_ge(STn, A[h]["u"])
                if h < 2:
                    wt(e, ctxb_free)
                s_inc(e.activation(out=ctxb[r_q:r_q + DK, ts(c_q, T)], in_=pb[ub][0:DK, :],
                                   func=AF.Copy))
            A[h]["evac"] = C["s"]
            pbank_free[4 + h % 2] = ("s", A[h]["evac"])

        def attn_ctx(h):
            c_q, r_q = h // 2, (h % 2) * DK
            @blk.vector
            def _(e, h=h, c_q=c_q, r_q=r_q):
                e.wait_ge(STn, A[h]["rrep"])
                e.wait_ge(SS, A[h]["evac"])
                if h < 2:
                    wt(e, ctxb_free)
                v_inc(e.tensor_tensor(out=ctxb[r_q:r_q + DK, ts(c_q, T)],
                                      in0=ctxb[r_q:r_q + DK, ts(c_q, T)],
                                      in1=pb[2][r_q:r_q + DK, :], op=ALU.mult))
            A[h]["ctx"] = C["v"]
            pbank_free[2] = ("v", A[h]["ctx"])

        def emit_w12_slice(jslot, mr):
            nonlocal hB_free, h_ready, x1ln_free
            mm_toks = {}
            silu_base = C["s"]
            sttv0 = C["v"]
            for mc in range(4):
                mcg = mr * 4 + mc
                @blk.tensor
                def _(e, jslot=jslot, mc=mc, mcg=mcg, mr=mr):
                    if mc == 0:
                        e.wait_ge(slot_sem[jslot], slot_loaded[jslot] - 16)
                        if mr > 0:
                            wt(e, x1ln_ready_box[0])
                    wt(e, pbank_free[0] if mc % 2 == 0 else pbank_free[1])
                    bA = mc % 2
                    for k in range(KC):
                        if mr == 0 and mc == 0:
                            e.wait_ge(SV, x1ln_chunks_box[0][k])
                        ins = e.matmul(out=pb[bA][:],
                                       lhsT=wsl[jslot][:, k * T + mc * P: k * T + (mc + 1) * P],
                                       rhs=x1ln_s[:, ts(k, T)], start=(k == 0), stop=(k == KC - 1))
                    t_inc(ins)
                    if mc == 0:
                        e.wait_ge(slot_sem[jslot], slot_loaded[jslot])
                    if mc >= 2:
                        e.wait_ge(SV, sttv0 + mc - 1)
                    wt(e, pbank_free[2 + bA])
                    for k in range(KC):
                        ins = e.matmul(out=pb[2 + bA][:],
                                       lhsT=wsl[jslot][:, KC * T + k * T + mc * P: KC * T + k * T + (mc + 1) * P],
                                       rhs=x1ln_s[:, ts(k, T)], start=(k == 0), stop=(k == KC - 1))
                    t_inc(ins)
                mm_toks[mc] = (C["t"] - 1, C["t"])
                pbank_free[mc % 2] = ("s", silu_base + mc + 1)

            stt_base = C["v"]
            prev_stt = (sA_free[0], sA_free[1])
            @blk.scalar
            def _(e, mr=mr, mm_toks=mm_toks, stt_base=stt_base, prev_stt=prev_stt):
                for mc in range(4):
                    mcg = mr * 4 + mc
                    e.wait_ge(STn, mm_toks[mc][0])
                    if mc >= 2:
                        e.wait_ge(SV, stt_base + mc - 1)
                    else:
                        wt(e, prev_stt[mc])
                    s_inc(e.activation(out=sA[mc % 2][:], in_=pb[mc % 2][:], func=AF.Silu,
                                       scale=s1_s[:, mcg:mcg + 1]))
            assert C["s"] == silu_base + 4

            for mc in range(4):
                mcg = mr * 4 + mc
                @blk.vector
                def _(e, mc=mc, mcg=mcg, mm_toks=mm_toks, silu_base=silu_base):
                    e.wait_ge(STn, mm_toks[mc][1])
                    e.wait_ge(SS, silu_base + mc + 1)
                    if mcg == 0:
                        wt(e, hB_free)
                    v_inc(e.scalar_tensor_tensor(out=hB[:, ts(mcg, T)], in0=pb[2 + mc % 2][:],
                                                 scalar=s2_s[:, mcg:mcg + 1], in1=sA[mc % 2][:],
                                                 op0=ALU.mult, op1=ALU.mult))
                pbank_free[2 + mc % 2] = ("v", C["v"])
                sA_free[mc % 2] = ("v", C["v"])
            if mr == 7:
                h_ready = ("v", C["v"])
                x1ln_free = ("t", C["t"])

        def hmul_mc(jslot, kind, sidx, mc):
            rhs_buf = hB if kind == "h1" else g1B
            rhs_ready = h_ready if kind == "h1" else g1_ready
            mcg = sidx * 2 + mc
            gelu_inc = C["s"] + 1
            @blk.tensor
            def _(e, jslot=jslot, mc=mc, rhs_buf=rhs_buf, rhs_ready=rhs_ready, sidx=sidx):
                if mc == 0:
                    e.wait_ge(slot_sem[jslot], slot_loaded[jslot] - 16)
                    wt(e, rhs_ready)
                wt(e, pbank_free[mc % 2])
                ins = None
                for k in range(HC):
                    if mc == 0 and k == HC // 2:
                        e.wait_ge(slot_sem[jslot], slot_loaded[jslot])
                    ins = e.matmul(out=pb[mc % 2][:],
                                   lhsT=wsl[jslot][:, k * 256 + mc * P: k * 256 + (mc + 1) * P],
                                   rhs=rhs_buf[:, ts(k, T)], start=(k == 0), stop=(k == HC - 1))
                t_inc(ins)
            mm_done = C["t"]

            dst_buf = g1B if kind == "h1" else hB
            scol = sh1_s if kind == "h1" else sh2_s
            @blk.scalar
            def _(e, kind=kind, mcg=mcg, mc=mc, dst_buf=dst_buf, scol=scol, mm_done=mm_done):
                e.wait_ge(STn, mm_done)
                if mcg == 0:
                    if kind == "h1":
                        wt(e, g1_freeB)
                    else:
                        e.wait_ge(STn, hB_mm_last)
                s_inc(e.activation(out=dst_buf[:, ts(mcg, T)], in_=pb[mc % 2][:], func=AF.Gelu,
                                   scale=scol[:, mcg:mcg + 1]))
            assert C["s"] == gelu_inc
            pbank_free[mc % 2] = ("s", gelu_inc)
            return mm_done

        def finish_hmul(kind, sidx, mm_done):
            nonlocal g1_ready, g2_ready, hB_mm_last, g1_freeB
            if kind == "h1" and sidx == 15:
                g1_ready = ("s", C["s"])
                hB_mm_last = C["t"]
            if kind == "h2" and sidx == 15:
                g2_ready = ("s", C["s"])
                g1_freeB = ("t", C["t"])

        def emit_hmul_slice(jslot, kind, sidx):
            for mc in range(2):
                hmul_mc(jslot, kind, sidx, mc)
            finish_hmul(kind, sidx, None)

        def emit_w3_slice(jslot, sidx, tile):
            nonlocal hB_free, x1_free
            for mc in range(2):
                mcg = sidx * 2 + mc
                @blk.tensor
                def _(e, jslot=jslot, mc=mc, sidx=sidx):
                    if mc == 0:
                        e.wait_ge(slot_sem[jslot], slot_loaded[jslot] - 16)
                        wt(e, g2_ready)
                    wt(e, pbank_free[mc % 2])
                    for k in range(HC):
                        if mc == 0 and k == HC // 2:
                            e.wait_ge(slot_sem[jslot], slot_loaded[jslot])
                        ins = e.matmul(out=pb[mc % 2][:],
                                       lhsT=wsl[jslot][:, k * 256 + mc * P: k * 256 + (mc + 1) * P],
                                       rhs=hB[:, ts(k, T)], start=(k == 0), stop=(k == HC - 1))
                    t_inc(ins)
                mmd = C["t"]

                @blk.vector
                def _(e, mc=mc, mcg=mcg, mmd=mmd):
                    e.wait_ge(STn, mmd)
                    v_inc(e.scalar_tensor_tensor(out=x1_s[:, ts(mcg, T)], in0=pb[mc % 2][:],
                                                 scalar=s3_s[:, mcg:mcg + 1], in1=x1_s[:, ts(mcg, T)],
                                                 op0=ALU.mult, op1=ALU.add))
                pbank_free[mc % 2] = ("v", C["v"])
            if sidx == 3:
                hB_free = ("t", C["t"])

        def emit_out_store(tile):
            nonlocal out_stored, x1_free
            @blk.sync
            def _(e, tile=tile):
                e.wait_ge(SV, C["v"])
                e.dma_start(out=outT[:, ts(tile, T)].rearrange("(k p) t -> p k t", p=P),
                            in_=x1_s[:].rearrange("p (k t) -> p k t", k=KC)).then_inc(SO, 16)
            C["o"] += 16
            out_stored = ("o", C["o"])
            x1_free = ("o", C["o"])

        def emit_wout(ctx_pair_toks, tile):
            nonlocal ctxb_free, x_free
            x1_chunks = []
            w_loaded = slot_loaded[wout_slot[0]]
            for m in range(KC):
                bank = m % 2
                @blk.tensor
                def _(e, m=m, bank=bank, w_loaded=w_loaded):
                    if m == 0:
                        e.wait_ge(slot_sem[wout_slot[0]], w_loaded)
                    wt(e, pbank_free[bank])
                    ins = None
                    for k in range(KC):
                        if m == 0:
                            e.wait_ge(SV, ctx_pair_toks[k])
                        ins = e.matmul(out=pb[bank][:],
                                       lhsT=wsl[wout_slot[0]][:, k * D + m * P: k * D + (m + 1) * P],
                                       rhs=ctxb[:, ts(k, T)], start=(k == 0), stop=(k == KC - 1))
                    t_inc(ins)
                wmm = C["t"]
                if m == KC - 1:
                    ctxb_free = ("t", wmm)

                @blk.vector
                def _(e, m=m, bank=bank, wmm=wmm):
                    e.wait_ge(STn, wmm)
                    if m == 0:
                        wt(e, x1_free)
                    v_inc(e.scalar_tensor_tensor(out=x1_s[:, ts(m, T)], in0=pb[bank][:],
                                                 scalar=bout_s[:, m:m + 1], in1=xbuf[:, ts(m, T)],
                                                 op0=ALU.add, op1=ALU.add))
                pbank_free[bank] = ("v", C["v"])
                x1_chunks.append(C["v"])
            x_free = ("v", C["v"])
            return C["v"], x1_chunks

        x_loaded = XCNT[0]

        wout_slot = [0]
        w12_pre = [None, None]

        def dbg_barrier():
            @blk.tensor
            def _(e):
                e.wait_ge(SV, C["v"]); e.wait_ge(SS, C["s"]); e.wait_ge(SI, C["i"]); e.wait_ge(SO, C["o"])
                t_inc(e.engine_nop() if hasattr(e, "engine_nop") else e.nop())
            tcnt = C["t"]
            @blk.vector
            def _(e, tcnt=tcnt):
                e.wait_ge(STn, tcnt)
                v_inc(e.engine_nop())
            vcnt = C["v"]
            @blk.scalar
            def _(e, tcnt=tcnt, vcnt=vcnt):
                e.wait_ge(STn, tcnt); e.wait_ge(SV, vcnt)
                s_inc(e.nop())
            scnt = C["s"]
            @blk.sync
            def _(e, tcnt=tcnt, vcnt=vcnt, scnt=scnt):
                e.wait_ge(STn, tcnt); e.wait_ge(SV, vcnt); e.wait_ge(SS, scnt)

        def do_slice(kind, pos, tile):
            if kind == "w12" and pos in (0, 1) and w12_pre[pos] is not None:
                jj = w12_pre[pos]
                w12_pre[pos] = None
            else:
                jj = load_slice(kind, pos)
            if kind == "w12":
                emit_w12_slice(jj, pos)
            elif kind in ("h1", "h2"):
                emit_hmul_slice(jj, kind, pos)
            else:
                emit_w3_slice(jj, pos, tile)
            slot_free[jj] = ("t", C["t"])

        for i in range(NT):
            ffn_slices = []
            if i > 0:
                ffn_slices = [("w12", mr) for mr in range(8)] + \
                             [("h1", s_) for s_ in range(16)] + \
                             [("h2", s_) for s_ in range(16)]

            pre = [("x", x_loaded), xln_free, x1ln_ready_box[0]]
            mid = None
            if i > 0:
                def mid(i=i):
                    kind, pos = ffn_slices[0]
                    do_slice(kind, pos, i - 1)
            ln1_done, ln1_chunks = emit_ln(xbuf, xln_s, pre, mid_cb=mid)

            qkv_done = 0
            fidx = 1 if i > 0 else 0
            for sl in range(3):
                if fidx < 8 and i > 0:
                    kind, pos = ffn_slices[fidx]; fidx += 1
                    do_slice(kind, pos, i - 1)
                jq = load_slice("qkv", sl, direct=(i == 0))
                qkv_done = emit_qkv_slice(jq, sl, ln1_chunks)
                slot_free[jq] = ("t", C["t"])
            xln_free = ("t", C["t"])
            if i == 0:
                emit_casts()

            while fidx < 8 and i > 0:
                kind, pos = ffn_slices[fidx]; fidx += 1
                do_slice(kind, pos, i - 1)

            pend = []

            def queue_ffn():
                nonlocal fidx
                if i > 0 and fidx < len(ffn_slices):
                    kind, pos = ffn_slices[fidx]; fidx += 1
                    jj = load_slice(kind, pos)
                    pend.append([jj, kind, pos, 0])
                    pend.append([jj, kind, pos, 1])

            def run_ffn_mc():
                if pend:
                    jj, kind, pos, mc = pend.pop(0)
                    mm = hmul_mc(jj, kind, pos, mc)
                    if mc == 1:
                        finish_hmul(kind, pos, mm)
                        slot_free[jj] = ("t", C["t"])

            for h in range(NH + 2):
                if h < NH:
                    attn_V1(h)
                    queue_ffn()
                    run_ffn_mc()
                    attn_sc(h, qkv_done)
                    attn_S1(h)
                if 1 <= h <= NH:
                    hh = h - 1
                    attn_erep_pair(hh, 0)
                    attn_tmp2_pair(hh, 0)
                    run_ffn_mc()
                    attn_erep_pair(hh, 1)
                    attn_u_pair(hh, 0)
                    attn_tmp2_pair(hh, 1)
                    queue_ffn()
                    run_ffn_mc()
                    attn_erep_pair(hh, 2)
                    attn_u_pair(hh, 1)
                    attn_tmp2_pair(hh, 2)
                    attn_S2a(hh)
                    run_ffn_mc()
                    attn_erep_pair(hh, 3)
                    attn_u_pair(hh, 2)
                    attn_tmp2_pair(hh, 3)
                    run_ffn_mc()
                    attn_u_pair(hh, 3)
                if 2 <= h <= NH + 1:
                    attn_rrep(h - 2)
                    attn_S2b(h - 2)
                    attn_ctx(h - 2)
            ctx_done = A[NH - 1]["ctx"]
            qkv_free = ("v", ctx_done)
            pbank_free[2] = ("v", A[NH - 1]["ctx"])
            pbank_free[3] = ("v", A[NH - 1]["tmp2_done"])
            while pend:
                run_ffn_mc()
            while i > 0 and fidx < len(ffn_slices):
                kind, pos = ffn_slices[fidx]; fidx += 1
                do_slice(kind, pos, i - 1)

            if i > 0:
                for s_ in range(4):
                    do_slice("w3", s_, i - 1)
            jw = load_slice("wout", 0)
            wout_slot[0] = jw
            wout_loaded = slot_loaded[jw]
            w12_pre[0] = load_slice("w12", 0)
            if i > 0:
                emit_out_store(i - 1)

            x1_done, x1_chunks_w = emit_wout([A[2 * k_ + 1]["ctx"] for k_ in range(KC)], i)
            slot_free[wout_slot[0]] = ("t", C["t"])

            if i + 1 < NT:
                @blk.sync
                def _(e, i=i):
                    wt(e, x_free)
                    e.dma_start(out=xbuf[:].rearrange("p (k t) -> p k t", k=KC),
                                in_=xT[:, ts(i + 1, T)].rearrange("(k p) t -> p k t", p=P)).then_inc(SX, 16)
                XCNT[0] += 16
                x_loaded = XCNT[0]
            w12_pre[1] = load_slice("w12", 1)

            ln2_done, ln2_chunks = emit_ln(x1_s, x1ln_s, [x1ln_free, ("v", x1_chunks_w[0])],
                                           pre_chunks=[("v", t_) for t_ in x1_chunks_w])
            x1ln_ready_box[0] = ("v", ln2_done)
            x1ln_chunks_box[0] = ln2_chunks
            if DEBUG_BARRIER:
                dbg_barrier()

        @blk.tensor
        def _(e, ln2_done=ln2_done):
            e.wait_ge(SV, ln2_done)
        ffn_slices = [("w12", mr) for mr in range(8)] + \
                     [("h1", s_) for s_ in range(16)] + \
                     [("h2", s_) for s_ in range(16)]
        for kind, pos in ffn_slices:
            do_slice(kind, pos, NT - 1)
        for s_ in range(4):
            do_slice("w3", s_, NT - 1)
        emit_out_store(NT - 1)

        @blk.sync
        def _(e):
            e.wait_ge(SO, C["o"])

    return nc


_PROG_CACHE = {}

def _get_prog():
    if "nc" not in _PROG_CACHE:
        _PROG_CACHE["nc"] = build_program()
    return _PROG_CACHE["nc"]


def _consts():
    bf = ml_dtypes.bfloat16
    dsel = np.zeros((P, P), bf)
    for p in range(P):
        dsel[p, p % DK] = 1
    bdsel = np.zeros((P, KC * P), bf)
    gsel = np.zeros((P, KC * P), bf)
    for c in range(KC):
        for p in range(P):
            gg = 2 * c + p // DK
            bdsel[p, c * P + gg] = 1
            gsel[gg, c * P + p] = 1
    densel = np.zeros((P, P), bf); densel[0:NH, 32] = 1
    rsel = np.zeros((P, P), bf); rsel[32, :] = 1
    brow = np.zeros((P, P), bf); brow[0, :] = 1
    col0 = np.zeros((P, P), bf); col0[:, 0] = 1
    eps = np.full((P, 1), EPS, np.float32)
    return dsel, bdsel, gsel, densel, rsel, brow, col0, eps


def kernel(L0_attn_x, L0_attn_W_qkv, L0_attn_b_qkv, L0_attn_W_out, L0_attn_b_out,
           L0_ffn_w1_w, L0_ffn_w1_s, L0_ffn_w2_w, L0_ffn_w2_s,
           L0_ffn_h1_w, L0_ffn_h1_s, L0_ffn_h2_w, L0_ffn_h2_s,
           L0_ffn_w3_w, L0_ffn_w3_s):
    x = np.asarray(L0_attn_x)
    B = x.shape[0]
    wq = np.asarray(L0_attn_W_qkv)
    wq3 = wq.reshape(D, NH, 3, DK)
    wqkv = np.ascontiguousarray(wq3.transpose(0, 2, 1, 3).reshape(D, 3 * D))
    bq3 = np.asarray(L0_attn_b_qkv).reshape(NH, 3, DK)
    bqkv = np.ascontiguousarray(bq3.transpose(1, 0, 2).reshape(3 * D // P, P).T)
    bout = np.ascontiguousarray(np.asarray(L0_attn_b_out).reshape(D // P, P).T)
    w1T = np.ascontiguousarray(np.asarray(L0_ffn_w1_w).T)
    w2T = np.ascontiguousarray(np.asarray(L0_ffn_w2_w).T)
    h1T = np.ascontiguousarray(np.asarray(L0_ffn_h1_w).T)
    h2T = np.ascontiguousarray(np.asarray(L0_ffn_h2_w).T)
    w3T = np.ascontiguousarray(np.asarray(L0_ffn_w3_w).T)
    s1 = np.ascontiguousarray(np.asarray(L0_ffn_w1_s).reshape(HC, P).T)
    s2 = np.ascontiguousarray(np.asarray(L0_ffn_w2_s).reshape(HC, P).T)
    sh1 = np.ascontiguousarray(np.asarray(L0_ffn_h1_s).reshape(HC, P).T)
    sh2 = np.ascontiguousarray(np.asarray(L0_ffn_h2_s).reshape(HC, P).T)
    s3 = np.ascontiguousarray(np.asarray(L0_ffn_w3_s).reshape(D // P, P).T)
    dsel, bdsel, gsel, densel, rsel, brow, col0, eps = _consts()
    cb16 = np.ascontiguousarray(np.concatenate(
        [dsel, bdsel, gsel, densel, rsel, brow, col0], axis=1))
    cb32 = np.ascontiguousarray(np.concatenate(
        [eps, bqkv, bout, s1, s2, sh1, sh2, s3], axis=1))

    shared = dict(wqkv=wqkv, wout=np.ascontiguousarray(np.asarray(L0_attn_W_out)),
                  w1T=w1T, w2T=w2T, h1T=h1T, h2T=h2T, w3T=w3T,
                  cb16=cb16, cb32=cb32)

    in_maps = []
    for c in range(B):
        m = dict(shared)
        m["xT"] = np.ascontiguousarray(x[c].T)
        in_maps.append(m)

    nc = _get_prog()
    res = run_bass_kernel_spmd(nc, in_maps, core_ids=list(range(N_CORES)))
    out = np.stack([np.asarray(r["outT"], dtype=np.float32).T for r in res.results], axis=0)
    return out
